# version 05
# baseline (speedup 1.0000x reference)
import numpy as np
import ml_dtypes
from contextlib import ExitStack

import concourse.bass as bass
import concourse.mybir as mybir
from concourse.bass_utils import run_bass_kernel_spmd

F32 = mybir.dt.float32
BF16 = mybir.dt.bfloat16
AF = mybir.ActivationFunctionType
ALU = mybir.AluOpType
AX = mybir.AxisListType
BF16NP = ml_dtypes.bfloat16

NCORE = 8
L = 16384
TC = L // NCORE
D = 1024

MAGIC = 12582912.0


def cdiv(a, b):
    return (a + b - 1) // b


class Prog:

    ENG = ("v", "s", "t", "g", "y")

    DMA_POOL = 6

    def __init__(self, nc):
        self.nc = nc
        self.stack = ExitStack()
        self.q = {e: [] for e in self.ENG}
        self.sem = {}
        self.cnt = {}
        names = ["pv", "ps", "pt", "pg", "py", "cc"]
        for base in ("dy", "dg", "dsc"):
            names += [f"{base}{i}" for i in range(self.DMA_POOL)]
        for name in names:
            self.sem[name] = self.stack.enter_context(nc.semaphore(name))
            self.cnt[name] = 0
        self._dmarr = {"y": 0, "g": 0, "s": 0}
        self._bar = dict(self.cnt)
        self._uid = 0
        self._ctx = []

    def uid(self):
        self._uid += 1
        return self._uid

    def _curctx(self, ctx):
        if ctx is not None:
            return ctx
        return self._ctx[-1] if self._ctx else self.stack

    def sb(self, name, shape, dt=F32, ctx=None):
        return self._curctx(ctx).enter_context(
            self.nc.sbuf_tensor(f"{name}_{self.uid()}", shape, dt))

    def sbl(self, name, ntile, free, dt=F32, p=128, ctx=None):
        return [self._curctx(ctx).enter_context(
            self.nc.sbuf_tensor(f"{name}{i}_{self.uid()}", [p, free], dt))
            for i in range(ntile)]

    def ps_(self, name, shape, dt=F32, ctx=None):
        return self._curctx(ctx).enter_context(
            self.nc.psum_tensor(f"{name}_{self.uid()}", shape, dt))

    def const(self, name, arr):
        return self.nc.inline_tensor(np.ascontiguousarray(arr), name=f"{name}_{self.uid()}")

    def do(self, e, fn, tick=False):
        if tick:
            sname = "p" + e
            self.cnt[sname] += 1
            val = self.cnt[sname]
            sem = self.sem[sname]

            def thunk(E, fn=fn, sem=sem):
                fn(E).then_inc(sem, 1)
            self.q[e].append(thunk)
            return (sname, val)
        self.q[e].append(fn)
        return None

    def wait(self, e, tokens):
        if tokens is None:
            return
        if isinstance(tokens, tuple):
            tokens = [tokens]
        merged = {}
        for t in tokens:
            if t is None:
                continue
            s, v = t
            merged[s] = max(merged.get(s, 0), v)
        for s, v in merged.items():
            def thunk(E, s=s, v=v):
                E.wait_ge(self.sem[s], v)
            self.q[e].append(thunk)

    def dma(self, e, out, in_, **kw):
        base = {"y": "dy", "g": "dg", "s": "dsc"}[e]
        i = self._dmarr[e]
        self._dmarr[e] = (i + 1) % self.DMA_POOL
        sname = f"{base}{i}"
        prev = self.cnt[sname]
        if prev > 0:
            self.wait(e, (sname, prev))
        self.cnt[sname] = prev + 16
        val = self.cnt[sname]
        sem = self.sem[sname]

        def thunk(E, out=out, in_=in_, kw=kw):
            E.dma_start(out=out, in_=in_, **kw).then_inc(sem, 16)
        self.q[e].append(thunk)
        return (sname, val)

    def collective(self, kind, op, ins, outs, ngroups=NCORE):
        self.cnt["cc"] += 1
        val = self.cnt["cc"]
        sem = self.sem["cc"]

        def thunk(E, kind=kind, op=op, ins=ins, outs=outs):
            E.collective_compute(
                kind, op, ins=ins, outs=outs,
                replica_groups=[list(range(ngroups))]).then_inc(sem, 1)
        self.q["g"].append(thunk)
        return ("cc", val)

    def barrier(self):
        toks = [(s, v) for s, v in self.cnt.items()
                if v > self._bar.get(s, 0)]
        for e in self.ENG:
            self.wait(e, [t for t in toks if t[0] != "p" + e])
        self._bar = dict(self.cnt)
        return toks

    def phase(self, name=""):
        from contextlib import contextmanager

        @contextmanager
        def _cm():
            ctx = ExitStack()
            self._ctx.append(ctx)
            try:
                yield ctx
            finally:
                self.barrier()
                self._ctx.pop()
                ctx.close()
        return _cm()

    def build(self):
        nc = self.nc
        with nc.Block() as block:
            for e, reg in (("y", block.sync), ("t", block.tensor),
                           ("v", block.vector), ("s", block.scalar),
                           ("g", block.gpsimd)):
                def runner(E, e=e):
                    for thunk in self.q[e]:
                        thunk(E)
                reg(runner)
        self.stack.close()



def emit_matmul(p, out_psum, lhsT, rhs, M, N,
                lhs_dep=None, rhs_dep=None, start=True, stop=True,
                n_base=0, out_n_base=0, tick_last=True, m_base=0,
                rank1=None, nchunk=512):
    nK = len(lhsT)
    nM = len(out_psum)
    tok = None
    p.wait("t", lhs_dep)
    p.wait("t", rhs_dep)
    nr = len(rank1) if rank1 else 0
    for m in range(nM):
        msz = min(128, M - m * 128)
        for n0 in range(0, N, nchunk):
            nsz = min(nchunk, N - n0)
            for k in range(nK):
                def thunk(E, m=m, n0=n0, k=k, msz=msz, nsz=nsz):
                    return E.matmul(
                        out_psum[m][:msz, out_n_base + n0:out_n_base + n0 + nsz],
                        lhsT[k][:, m_base + m * 128:m_base + m * 128 + msz],
                        rhs[k][:, n_base + n0:n_base + n0 + nsz],
                        start=(k == 0 and start),
                        stop=(k == nK - 1 and nr == 0 and stop))
                last = (m == nM - 1) and (n0 + nchunk >= N) and (k == nK - 1) and nr == 0
                tok = p.do("t", thunk, tick=(tick_last and last))
            for ri in range(nr):
                row_l, row_r = rank1[ri]

                def thunk(E, m=m, n0=n0, msz=msz, nsz=nsz, row_l=row_l,
                          row_r=row_r, ri=ri):
                    return E.matmul(
                        out_psum[m][:msz, out_n_base + n0:out_n_base + n0 + nsz],
                        row_l[0:1, m_base + m * 128:m_base + m * 128 + msz],
                        row_r[0:1, n_base + n0:n_base + n0 + nsz],
                        start=False, stop=(ri == nr - 1 and stop))
                last = (m == nM - 1) and (n0 + nchunk >= N) and (ri == nr - 1)
                tok = p.do("t", thunk, tick=(tick_last and last))
    return tok


def emit_bcast_row(p, psum, row_ap, ones_row, n, dep=None, m=128, n_base=0,
                   tick=True):
    p.wait("t", dep)
    tok = None
    for n0 in range(0, n, 512):
        nsz = min(512, n - n0)

        def thunk(E, n0=n0, nsz=nsz):
            return E.matmul(psum[0:m, n_base + n0:n_base + n0 + nsz],
                            ones_row[0:1, 0:m],
                            row_ap[0:1, n0:n0 + nsz], start=True, stop=True)
        tok = p.do("t", thunk, tick=(tick and n0 + 512 >= n))
    return tok


def emit_ew(p, e, fn, dep=None, tick=True, sedge=None):
    if e in ("v", "s", "g"):
        tick = True
        if sedge is None:
            sedge = True
    if sedge and p.cnt["p" + e] > 0:
        p.wait(e, ("p" + e, p.cnt["p" + e]))
    p.wait(e, dep)
    return p.do(e, fn, tick=tick)


def emit_masked_pick(p, dst, src_all, rowbase, rstride, width, pselcols, scr,
                     ncores, e="y", dep=None, war=None):
    tok = None
    lastc = [war, war]
    for cc in range(ncores):
        p.wait(e, lastc[cc % 2])
        r0 = cc * rstride + rowbase
        ld = p.dma(e, scr[cc % 2][:, 0:width], src_all[r0:r0 + 128, 0:width])
        pc = pselcols[cc]
        if cc == 0:
            tok = emit_ew(p, "v", lambda E, dst=dst, s=scr[cc % 2], pc=pc,
                          width=width: E.tensor_scalar(
                              dst[:, 0:width], s[:, 0:width], pc, None,
                              ALU.mult), dep=[ld] + (dep or []))
        else:
            tok = emit_ew(p, "v", lambda E, dst=dst, s=scr[cc % 2], pc=pc,
                          width=width: E.scalar_tensor_tensor(
                              dst[:, 0:width], s[:, 0:width], pc,
                              dst[:, 0:width], ALU.mult, ALU.add), dep=[ld])
        lastc[cc % 2] = tok
    return tok


def emit_ln_feat(p, xT, out_bf, gamma=None, beta=None, eps=1e-5,
                 out_extra=None, ones=None, ones_row=None, TCH=1024):
    nk = len(xT)
    sizes = [xT[k].shape[0] for k in range(nk)]
    nch = sum(sizes)
    T = xT[0].shape[1]
    inv = 1.0 / nch
    with p.phase("ln") as ctx:
        psS = p.ps_("lnS", [1, TCH], ctx=ctx)
        psQ = p.ps_("lnQ", [1, TCH], ctx=ctx)
        psA = p.ps_("lnA", [128, TCH], ctx=ctx)
        psB = p.ps_("lnB", [128, TCH], ctx=ctx)
        sq = [p.sb("lnsq0", [128, TCH]), p.sb("lnsq1", [128, TCH])]
        R = p.sb("lnrows", [1, 4 * TCH])
        for c0 in range(0, T, TCH):
            p.barrier()
            for k in range(nk):
                for n0 in range(0, TCH, 512):
                    nsz = min(512, TCH - n0)

                    def thunk(E, k=k, n0=n0, c0=c0, nsz=nsz):
                        return E.matmul(psS[0:1, n0:n0 + nsz], ones[:sizes[k], 0:1],
                                        xT[k][:, c0 + n0:c0 + n0 + nsz],
                                        start=(k == 0), stop=(k == nk - 1))
                    p.do("t", thunk)
            mm_tok = [None, None]
            sq_tok = [None, None]
            for k in range(nk):
                pp = k % 2
                sq_tok[pp] = emit_ew(p, "s", lambda E, k=k, pp=pp, c0=c0: E.activation(
                    sq[pp][:sizes[k], :], xT[k][:, c0:c0 + TCH], AF.Square),
                    dep=mm_tok[pp])
                p.wait("t", sq_tok[pp])
                for n0 in range(0, TCH, 512):
                    nsz = min(512, TCH - n0)

                    def thunk(E, k=k, n0=n0, pp=pp, nsz=nsz):
                        return E.matmul(psQ[0:1, n0:n0 + nsz], ones[:sizes[k], 0:1],
                                        sq[pp][:sizes[k], n0:n0 + nsz],
                                        start=(k == 0), stop=(k == nk - 1))
                    mm_tok[pp] = p.do("t", thunk, tick=(n0 + 512 >= TCH))
            stat_tok = mm_tok[(nk - 1) % 2]
            mu, rstd, nmr, tmp = (R[0:1, 0:TCH], R[0:1, TCH:2 * TCH],
                                  R[0:1, 2 * TCH:3 * TCH], R[0:1, 3 * TCH:4 * TCH])
            emit_ew(p, "v", lambda E: E.tensor_scalar_mul(mu, psS[0:1, 0:TCH], inv),
                    dep=stat_tok, sedge=True)
            emit_ew(p, "v", lambda E: E.tensor_scalar_mul(tmp, psQ[0:1, 0:TCH], inv),
                    sedge=True)
            emit_ew(p, "v", lambda E: E.tensor_tensor(rstd, mu, mu, ALU.mult), sedge=True)
            t4 = emit_ew(p, "v", lambda E: E.tensor_tensor(tmp, tmp, rstd, ALU.subtract),
                         sedge=True)
            t5 = emit_ew(p, "s", lambda E: E.activation(rstd, tmp, AF.Ln, bias=float(eps)),
                         dep=t4)
            t6 = emit_ew(p, "s", lambda E: E.activation(rstd, rstd, AF.Exp, scale=-0.5),
                         sedge=True)
            t7 = emit_ew(p, "v", lambda E: E.tensor_tensor(nmr, mu, rstd, ALU.mult),
                         dep=t6, sedge=True)
            t8 = emit_ew(p, "v", lambda E: E.tensor_scalar_mul(nmr, nmr, -1.0), sedge=True)
            bA = emit_bcast_row(p, psA, rstd, ones_row, TCH, dep=t6)
            bB = emit_bcast_row(p, psB, nmr, ones_row, TCH, dep=t8)
            p.wait("v", [bA, bB])
            for k in range(nk):
                emit_ew(p, "v", lambda E, k=k, c0=c0: E.tensor_tensor(
                    sq[0][:sizes[k], :], xT[k][:, c0:c0 + TCH], psA[:sizes[k], :],
                    ALU.mult), tick=False)
                if gamma is None:
                    emit_ew(p, "v", lambda E, k=k, c0=c0: E.tensor_tensor(
                        out_bf[k][:, c0:c0 + TCH], sq[0][:sizes[k], :],
                        psB[:sizes[k], :], ALU.add), tick=False)
                else:
                    emit_ew(p, "v", lambda E, k=k: E.tensor_tensor(
                        sq[0][:sizes[k], :], sq[0][:sizes[k], :], psB[:sizes[k], :],
                        ALU.add), tick=False)
                    emit_ew(p, "v", lambda E, k=k, c0=c0: E.tensor_scalar(
                        out_bf[k][:, c0:c0 + TCH], sq[0][:sizes[k], :], gamma[k],
                        beta[k], ALU.mult, ALU.add), tick=False)
                if out_extra is not None:
                    emit_ew(p, "v", lambda E, k=k, c0=c0: E.tensor_copy(
                        out_extra[k][:, c0:c0 + TCH], out_bf[k][:, c0:c0 + TCH]),
                        tick=False)
            emit_ew(p, "v", lambda E: E.tensor_copy(R[0:1, 0:1], R[0:1, 0:1]))
    return None


def load_tiles(p, dram, nrow, width, dt=F32, e="y", name="wt", ctx=None,
               dram_row0=0):
    nt = cdiv(nrow, 128)
    tiles = []
    toks = []
    for i in range(nt):
        rsz = min(128, nrow - i * 128)
        t = (ctx or p.stack).enter_context(
            p.nc.sbuf_tensor(f"{name}{i}_{p.uid()}", [rsz, width], dt))
        tiles.append(t)
        toks.append(p.dma(e, t[:, :], dram[dram_row0 + i * 128:dram_row0 + i * 128 + rsz, :]))
    return tiles, toks[-1]


def emit_quantize_absmean(p, ctx, dram, K, M, ones, ones_row, name="wq"):
    nc = p.nc
    nk = cdiv(K, 128)
    wq = []
    for i in range(nk):
        rsz = min(128, K - i * 128)
        wq.append(ctx.enter_context(
            nc.sbuf_tensor(f"{name}{i}_{p.uid()}", [rsz, M], BF16)))
    with p.phase("quant") as qctx:
        stream = [p.sb("qs0", [128, M]), p.sb("qs1", [128, M])]
        acc = p.sb("qacc", [128, nk])
        acc2 = p.sb("qacc2", [128, 1])
        rows = p.sb("qrow", [128, 2])
        ps1 = p.ps_("qps", [1, 1], ctx=qctx)
        psb = p.ps_("qpsb", [128, 2], ctx=qctx)
        emit_ew(p, "v", lambda E: E.memset(acc[:], 0.0), tick=False)
        ld_tok = [None, None]
        red_tok = [None, None]
        for i in range(nk):
            pp = i % 2
            rsz = min(128, K - i * 128)
            p.wait("y", ("pv", p.cnt["pv"]))
            ld_tok[pp] = p.dma("y", stream[pp][:rsz, :], dram[i * 128:i * 128 + rsz, :])
            red_tok[pp] = emit_ew(p, "v", lambda E, i=i, pp=pp, rsz=rsz: E.tensor_reduce(
                acc[:rsz, i:i + 1], stream[pp][:rsz, :], AX.X, ALU.add,
                apply_absolute_value=True), dep=ld_tok[pp])
        rt = emit_ew(p, "v", lambda E: E.tensor_reduce(
            acc2[:, 0:1], acc[:, 0:nk], AX.X, ALU.add),
            dep=[red_tok[0], red_tok[1]])
        p.wait("t", rt)
        p.do("t", lambda E: E.matmul(ps1[0:1, 0:1], acc2[:, 0:1], ones[:, 0:1],
                                     start=True, stop=True), tick=True)
        tmm = ("pt", p.cnt["pt"])
        t1 = emit_ew(p, "v", lambda E: E.tensor_scalar(
            rows[0:1, 0:1], ps1[0:1, 0:1], 1.0 / (K * M), 1e-6, ALU.mult,
            ALU.add), dep=tmm, sedge=True)
        t1 = emit_ew(p, "v", lambda E: E.tensor_scalar_mul(rows[0:1, 0:1],
                                                           rows[0:1, 0:1], 0.5),
                     sedge=True)
        t1 = emit_ew(p, "v", lambda E: E.tensor_scalar_mul(rows[0:1, 1:2],
                                                           rows[0:1, 0:1], -1.0),
                     sedge=True)
        p.wait("t", t1)
        p.do("t", lambda E: E.matmul(psb[:, 0:2], ones_row[0:1, 0:128], rows[0:1, 0:2],
                                     start=True, stop=True), tick=True)
        tbb = ("pt", p.cnt["pt"])
        tcp = emit_ew(p, "v", lambda E: E.tensor_copy(rows[:, 0:2], psb[:, 0:2]),
                      dep=tbb)
        gsc = [p.sb("qg0", [128, M], BF16), p.sb("qg1", [128, M], BF16)]
        ld_tok = [None, None]
        q_tok = [None, None]
        for i in range(nk):
            pp = i % 2
            rsz = min(128, K - i * 128)
            p.wait("y", ("pv", p.cnt["pv"]))
            ld_tok[pp] = p.dma("y", stream[pp][:rsz, :], dram[i * 128:i * 128 + rsz, :])
            emit_ew(p, "v", lambda E, pp=pp, rsz=rsz: E.tensor_scalar(
                gsc[0][:rsz, :], stream[pp][:rsz, :], rows[:rsz, 0:1], None,
                ALU.is_gt), dep=[ld_tok[pp], tcp], tick=False)
            emit_ew(p, "v", lambda E, pp=pp, rsz=rsz: E.tensor_scalar(
                gsc[1][:rsz, :], stream[pp][:rsz, :], rows[:rsz, 1:2], None,
                ALU.is_lt), tick=False)
            q_tok[pp] = emit_ew(p, "v", lambda E, i=i, rsz=rsz: E.tensor_tensor(
                wq[i][:, :], gsc[0][:rsz, :], gsc[1][:rsz, :], ALU.subtract))
    return wq


def block_l0ffn(p, io, consts, dbg=None):
    nc = p.nc
    ones, ones_row = consts["ones"], consts["ones_row"]
    H = 4096
    NH = H // 128
    TQ = 128
    with p.phase("l0ffn") as ctx:
        lnwb = p.sb("lnwb", [128, 2 * NH])
        lnwb_tok = [p.dma("y", lnwb[:, 0:NH], io["l0f_lnw"][:, :]),
                    p.dma("y", lnwb[:, NH:2 * NH], io["l0f_lnb"][:, :])]
        lnw_bf = p.sb("lnwbf", [128, NH], BF16)
        p.wait("v", lnwb_tok)
        emit_ew(p, "v", lambda E: E.tensor_copy(lnw_bf[:, :], lnwb[:, 0:NH]))
        xn = p.sbl("l0f_xn", 8, TC, BF16, ctx=ctx)
        with p.phase("preln") as c0:
            x32, xtok = load_tiles(p, io["res_in"], D, TC, F32, name="l0f_x", ctx=c0)
            p.wait("t", xtok)
            p.wait("s", xtok)
            p.wait("v", xtok)
            emit_ln_feat(p, x32, xn, ones=ones, ones_row=ones_row)
        wfcq = emit_quantize_absmean(p, ctx, io["l0f_wfcT"], D, H, ones, ones_row, "wfcq")
        wprq = emit_quantize_absmean(p, ctx, io["l0f_wprT"], H, D, ones, ones_row, "wprq")
        wlrow = p.sb("wlrow", [1, 1024], ctx=ctx)
        wlrow2 = p.sb("wlrow2", [1, 1024], ctx=ctx)
        with p.phase("wl") as c2:
            psr = p.ps_("wlps", [1, 2048], ctx=c2)
            lnb_bf = p.sb("lnbbf", [128, NH], BF16)
            tb = emit_ew(p, "v", lambda E: E.tensor_copy(lnb_bf[:, :], lnwb[:, NH:2 * NH]))
            tw = emit_ew(p, "v", lambda E: E.tensor_copy(lnw_bf[:, :], lnwb[:, 0:NH]))
            p.wait("t", [tb, tw])
            for k in range(NH):
                for (cofs, wcol) in ((0, lnw_bf), (1024, lnb_bf)):
                    for n0 in range(0, 1024, 512):
                        def thunk(E, k=k, n0=n0, cofs=cofs, wcol=wcol):
                            return E.matmul(psr[0:1, cofs + n0:cofs + n0 + 512],
                                            wcol[:, k:k + 1],
                                            wprq[k][:, n0:n0 + 512],
                                            start=(k == 0), stop=(k == NH - 1))
                        p.do("t", thunk,
                             tick=(k == NH - 1 and cofs == 1024 and n0 == 512))
            tmm = ("pt", p.cnt["pt"])
            emit_ew(p, "v", lambda E: E.tensor_copy(wlrow[0:1, :], psr[0:1, 0:1024]),
                    dep=tmm, sedge=True)
            emit_ew(p, "v", lambda E: E.tensor_copy(wlrow2[0:1, :], psr[0:1, 1024:2048]),
                    sedge=True)
        with p.phase("ffnstream") as c3:
            ps_out = p.ps_("l0f_out", [128, 8 * TQ], ctx=c3)
            ps_h = [p.ps_("l0f_hA", [128, TQ], ctx=c3),
                    p.ps_("l0f_hB", [128, TQ], ctx=c3)]
            ps_sS = p.ps_("l0f_sS", [1, TQ], ctx=c3)
            ps_sQ = p.ps_("l0f_sQ", [1, TQ], ctx=c3)
            ps_bc = p.ps_("l0f_bc", [128, TQ], ctx=c3)
            hg = p.sb("l0f_hg", [128, 2 * TQ])
            sqv = p.sb("l0f_sq", [128, 2 * TQ])
            hgall = p.sbl("l0f_hga", NH, TQ, F32, ctx=c3)
            htil = p.sbl("l0f_ht", NH, TQ, BF16, ctx=c3)
            rows = p.sb("l0f_rows", [1, 5 * TQ])
            rstdb = p.sb("l0f_rstdb", [128, TQ])
            rtile = p.sb("l0f_rt", [128, TQ])
            scr = p.sb("l0f_scr", [128, TQ])
            for tq0 in range(0, TC, TQ):
                p.barrier()
                mm_tok = [None, None]
                g_tok = [None, None]
                for m in range(NH):
                    pp = m % 2
                    p.wait("t", g_tok[pp])
                    for k in range(8):
                        def thunk(E, m=m, k=k, pp=pp, tq0=tq0):
                            return E.matmul(ps_h[pp][:, :],
                                            wfcq[k][:, m * 128:(m + 1) * 128],
                                            xn[k][:, tq0:tq0 + TQ],
                                            start=(k == 0), stop=(k == 7))
                        mm_tok[pp] = p.do("t", thunk, tick=(k == 7))
                    g_tok[pp] = emit_ew(p, "s", lambda E, m=m, pp=pp: E.activation(
                        hgall[m][:, :], ps_h[pp][:, :],
                        AF.Gelu), dep=mm_tok[pp])
                    emit_ew(p, "v", lambda E, m=m: E.tensor_scalar(
                        htil[m][:, :], hgall[m][:, :],
                        lnwb[:, m:m + 1], None, ALU.mult), dep=g_tok[pp],
                        tick=(m == NH - 1))
                ht_last = ("pv", p.cnt["pv"])
                p.wait("t", [g_tok[0], g_tok[1]])
                for m in range(NH):
                    def thunk(E, m=m):
                        return E.matmul(ps_sS[0:1, 0:TQ], ones[:, 0:1],
                                        hgall[m][:, :],
                                        start=(m == 0), stop=(m == NH - 1))
                    p.do("t", thunk, tick=(m == NH - 1))
                s1_last = ("pt", p.cnt["pt"])
                sq_tok = [None, None]
                st_tok = None
                for m in range(NH):
                    pp = m % 2
                    sq_tok[pp] = emit_ew(p, "s", lambda E, m=m, pp=pp: E.activation(
                        sqv[:, pp * TQ:(pp + 1) * TQ], hgall[m][:, :], AF.Square),
                        dep=st_tok if m >= 2 else None)
                    p.wait("t", sq_tok[pp])
                    def thunk(E, m=m, pp=pp):
                        return E.matmul(ps_sQ[0:1, 0:TQ], ones[:, 0:1],
                                        sqv[:, pp * TQ:(pp + 1) * TQ],
                                        start=(m == 0), stop=(m == NH - 1))
                    st_tok = p.do("t", thunk, tick=True)
                mu, rstd, nmu, tmp, irow = (
                    rows[0:1, 0:TQ], rows[0:1, TQ:2 * TQ],
                    rows[0:1, 2 * TQ:3 * TQ], rows[0:1, 3 * TQ:4 * TQ],
                    rows[0:1, 4 * TQ:5 * TQ])
                emit_ew(p, "v", lambda E: E.tensor_scalar_mul(mu, ps_sS[0:1, 0:TQ], 1.0 / H),
                        dep=[st_tok, s1_last], sedge=True)
                emit_ew(p, "v", lambda E: E.tensor_scalar_mul(tmp, ps_sQ[0:1, 0:TQ], 1.0 / H),
                        sedge=True)
                emit_ew(p, "v", lambda E: E.tensor_tensor(rstd, mu, mu, ALU.mult), sedge=True)
                t4 = emit_ew(p, "v", lambda E: E.tensor_tensor(tmp, tmp, rstd, ALU.subtract),
                             sedge=True)
                emit_ew(p, "s", lambda E: E.activation(rstd, tmp, AF.Ln, bias=1e-5),
                        dep=t4)
                t6a = emit_ew(p, "s", lambda E: E.activation(irow, rstd, AF.Exp, scale=0.5),
                              sedge=True)
                t6 = emit_ew(p, "s", lambda E: E.activation(rstd, rstd, AF.Exp, scale=-0.5),
                             sedge=True)
                t8 = emit_ew(p, "v", lambda E: E.tensor_scalar_mul(nmu, mu, -1.0),
                             dep=t6, sedge=True)
                bA = emit_bcast_row(p, ps_bc, rstd, ones_row, TQ, dep=t6)
                cA = emit_ew(p, "v", lambda E: E.tensor_copy(rstdb[:, :], ps_bc[:, :]),
                             dep=bA)
                if dbg is not None and tq0 == 0:
                    print("DBG BLOCK EMITTED")
                    p.barrier()
                    dstg = p.sb("dstg", [128, 5 * TQ])
                    dtok = emit_ew(p, "v", lambda E: E.tensor_copy(
                        dstg[0:1, 0:5 * TQ], rows[0:1, :]), dep=t8, sedge=True)
                    dtok = emit_ew(p, "v", lambda E: E.tensor_copy(
                        dstg[:, TQ:2 * TQ], htil[0][:, :]), dep=ht_last)
                    p.wait("y", dtok)
                    p.dma("y", dbg["d_rows"][:, :], dstg[0:1, 0:4 * TQ])
                    p.dma("y", dbg["d_htil"][:, :], dstg[:, TQ:2 * TQ])
                p.wait("t", [ht_last, t8, t6a])
                for mo in range(8):
                    for k in range(NH):
                        def thunk(E, mo=mo, k=k):
                            return E.matmul(ps_out[:, mo * TQ:(mo + 1) * TQ],
                                            wprq[k][:, mo * 128:(mo + 1) * 128],
                                            htil[k][:, :],
                                            start=(k == 0), stop=False)
                        p.do("t", thunk)
                    def thunk(E, mo=mo):
                        return E.matmul(ps_out[:, mo * TQ:(mo + 1) * TQ],
                                        wlrow[0:1, mo * 128:(mo + 1) * 128],
                                        rows[0:1, 2 * TQ:3 * TQ],
                                        start=False, stop=False)
                    p.do("t", thunk)

                    def thunk(E, mo=mo):
                        return E.matmul(ps_out[:, mo * TQ:(mo + 1) * TQ],
                                        wlrow2[0:1, mo * 128:(mo + 1) * 128],
                                        rows[0:1, 4 * TQ:5 * TQ],
                                        start=False, stop=True)
                    mm2 = p.do("t", thunk, tick=(mo == 7))
                if dbg is not None and tq0 == 0:
                    dstg2 = p.sb("dstg2", [128, TQ])
                    dtok = emit_ew(p, "v", lambda E: E.tensor_copy(
                        dstg2[:, :], ps_out[:, 0:TQ]), dep=mm2)
                    p.wait("y", dtok)
                    p.dma("y", dbg["d_psout"][:, :], dstg2[:, :])
                    dtok = emit_ew(p, "v", lambda E: E.tensor_copy(
                        dstg2[0:1, 0:8], wlrow[0:1, 0:8]))
                    dtok = emit_ew(p, "v", lambda E: E.tensor_copy(
                        dstg2[0:1, 8:16], wlrow2[0:1, 0:8]), dep=dtok, sedge=True)
                    p.wait("y", dtok)
                    p.dma("y", dbg["d_wl"][:, :], dstg2[0:1, 0:16])
                p.wait("v", [mm2, cA])
                sv = None
                for mo in range(8):
                    p.wait("y", sv)
                    ld = p.dma("y", rtile[:, :], io["res_in"][mo * 128:(mo + 1) * 128,
                                                              tq0:tq0 + TQ])
                    emit_ew(p, "v", lambda E, mo=mo: E.tensor_tensor(
                        scr[:, :], ps_out[:, mo * TQ:(mo + 1) * TQ],
                        rstdb[:, :], ALU.mult), tick=False)
                    sv = emit_ew(p, "v", lambda E, mo=mo: E.tensor_tensor(
                        hg[:, 0:TQ], rtile[:, :], scr[:, :], ALU.add), dep=ld)
                    p.wait("y", sv)
                    st = p.dma("y", io["res_out"][mo * 128:(mo + 1) * 128, tq0:tq0 + TQ],
                               hg[:, 0:TQ])
                    p.wait("v", st)
    return None



SCALAR_CONSTS = [0.0, 1.0, -1.0, -0.5, 0.5, 1e-5, 1e-6, 1e-12, 2.0, -2.0, 0.03125,
                 float(np.pi), float(np.pi / 2), float(-np.pi)]


def setup_consts(p):
    nc = p.nc
    consts = {}
    ones = p.sb("ones_col", [128, 1])
    od = p.const("onescol", np.ones((128, 1), np.float32))
    t1 = p.dma("y", ones[:, :], od[:])
    ones_row = p.sb("ones_row", [1, 512])
    ord_ = p.const("onesrow", np.ones((1, 512), np.float32))
    t2 = p.dma("y", ones_row[:, :], ord_[:])
    ones_row_bf = p.sb("ones_row_bf", [1, 512], BF16)
    orb = p.const("onesrowbf", np.ones((1, 512), BF16NP))
    t3 = p.dma("y", ones_row_bf[:, :], orb[:])
    ident_bf = p.sb("ident_bf", [128, 128], BF16)
    idb = p.const("identbf", np.eye(128, dtype=BF16NP))
    t5 = p.dma("y", ident_bf[:, :], idb[:])
    ident_f32 = p.sb("ident_f32", [128, 128])
    idf = p.const("identf", np.eye(128, dtype=np.float32))
    t6 = p.dma("y", ident_f32[:, :], idf[:])
    zero_row_bf = p.sb("zero_row_bf", [1, 512], BF16)
    zrb = p.const("zerorowbf", np.zeros((1, 512), BF16NP))
    t7 = p.dma("y", zero_row_bf[:, :], zrb[:])
    consts_extra = dict(ident_bf=ident_bf, ident_f32=ident_f32,
                        zero_row_bf=zero_row_bf)
    cvals = sorted(set(SCALAR_CONSTS))
    cp = p.sb("cpool", [128, len(cvals)])
    cpd = p.const("cpool", np.tile(np.array(cvals, np.float32), (128, 1)))
    t4 = p.dma("y", cp[:, :], cpd[:])
    for i, cv in enumerate(cvals):
        nc.const_aps.register(F32, cv, cp[:, i:i + 1])
    consts.update(ones=ones, ones_row=ones_row, ones_row_bf=ones_row_bf,
                  tok=[t1, t2, t3, t4, t5, t6, t7], **consts_extra)
    p.barrier()
    return consts


def host_prep_l0ffn(inp):
    f32 = np.float32
    return {
        "l0f_wfcT": np.ascontiguousarray(inp["L0_ffn_W_fc"].T.astype(f32)),
        "l0f_wprT": np.ascontiguousarray(inp["L0_ffn_W_proj"].T.astype(f32)),
        "l0f_lnw": np.ascontiguousarray(inp["L0_ffn_ln_w"].reshape(32, 128).T.astype(f32)),
        "l0f_lnb": np.ascontiguousarray(inp["L0_ffn_ln_b"].reshape(32, 128).T.astype(f32)),
    }


L0FFN_IO = {
    "l0f_wfcT": [D, 4096], "l0f_wprT": [4096, D],
    "l0f_lnw": [128, 32], "l0f_lnb": [128, 32],
}


def declare_io(nc, spec, kind="ExternalInput"):
    return {k: nc.dram_tensor(k, shp, F32, kind=kind) for k, shp in spec.items()}


def build_l0ffn_standalone():
    nc = bass.Bass("TRN2", target_bir_lowering=False, debug=False)
    io = declare_io(nc, L0FFN_IO)
    io["res_in"] = nc.dram_tensor("res_in", [D, TC], F32, kind="ExternalInput")
    io["res_out"] = nc.dram_tensor("res_out", [D, TC], F32, kind="ExternalOutput")
    p = Prog(nc)
    consts = setup_consts(p)
    block_l0ffn(p, io, consts)
    p.build()
    return nc



L1ATTN_IO = {
    "l1a_wq": [D, D], "l1a_wk": [D, D], "l1a_wv": [D, D],
    "l1a_bq": [128, 8], "l1a_bk": [128, 8], "l1a_bv": [1, D],
}


def host_prep_l1attn(inp):
    f32 = np.float32
    return {
        "l1a_wq": np.ascontiguousarray(inp["L1_attn_W_q"].astype(f32)),
        "l1a_wk": np.ascontiguousarray(inp["L1_attn_W_k"].astype(f32)),
        "l1a_wv": np.ascontiguousarray(inp["L1_attn_W_v"].astype(f32)),
        "l1a_bq": np.ascontiguousarray(inp["L1_attn_b_q"].reshape(8, 128).T.astype(f32)),
        "l1a_bk": np.ascontiguousarray(inp["L1_attn_b_k"].reshape(8, 128).T.astype(f32)),
        "l1a_bv": np.ascontiguousarray(inp["L1_attn_b_v"].reshape(1, D).astype(f32)),
    }


def block_l1attn(p, io, consts, ncores, core_id=0):
    nc = p.nc
    ones, ones_row, ones_row_bf = consts["ones"], consts["ones_row"], consts["ones_row_bf"]
    NT = TC // 128
    SCALE = 1.0 / 32.0
    Lg = ncores * TC
    with p.phase("l1attn") as ctx:
        kt_loc = nc.dram_tensor(f"kt_loc_{p.uid()}", [D, TC], BF16)
        v_loc = nc.dram_tensor(f"v_loc_{p.uid()}", [TC, D], BF16)
        aspace = "Shared" if ncores > 4 else "Local"
        kt_all = nc.dram_tensor(f"kt_all_{p.uid()}", [ncores * D, TC], BF16,
                                addr_space=aspace)
        v_all = nc.dram_tensor(f"v_all_{p.uid()}", [Lg, D], BF16,
                               addr_space=aspace)
        oscr = nc.dram_tensor(f"l1a_oscr_{p.uid()}", [TC, D], F32)
        qt_d = nc.dram_tensor(f"l1a_qtd_{p.uid()}", [D, TC], BF16)
        xn_ctx = ExitStack()
        xn = p.sbl("l1a_xn", 8, TC, BF16, ctx=xn_ctx)
        with p.phase("preln") as c0:
            x32, xtok = load_tiles(p, io["res_in"], D, TC, F32, name="l1a_x", ctx=c0)
            for e in ("t", "s", "v"):
                p.wait(e, xtok)
            emit_ln_feat(p, x32, xn, ones=ones, ones_row=ones_row)
        with p.phase("proj") as c1:
            qt = p.sbl("l1a_qt", 8, TC, BF16, ctx=c1)
            wtile = p.sbl("l1a_w", 8, D, BF16, ctx=c1)
            bias = p.sb("l1a_b", [128, 16])
            btok = [p.dma("y", bias[:, 0:8], io["l1a_bq"][:, :]),
                    p.dma("y", bias[:, 8:16], io["l1a_bk"][:, :])]
            bvrow = p.sb("l1a_bvr", [1, D])
            btok.append(p.dma("y", bvrow[:, :], io["l1a_bv"][:, :]))
            bvrow_bf = p.sb("l1a_bvb", [1, D], BF16)
            bvb_tok = emit_ew(p, "v", lambda E: E.tensor_copy(bvrow_bf[:, :], bvrow[:, :]),
                              dep=btok)
            kt_tiles = p.sbl("l1a_kt", 8, TC, BF16, ctx=c1)
            ps = p.ps_("l1a_ps", [128, TC], F32, ctx=c1)
            ev = None
            for (wname, outt, bofs) in (("l1a_wq", qt, 0), ("l1a_wk", kt_tiles, 8)):
                p.wait("g", ev)
                wtok = [p.dma("g", wtile[k][:, :], io[wname][k * 128:(k + 1) * 128, :])
                        for k in range(8)]
                for m in range(8):
                    p.wait("t", ev)
                    tok = emit_matmul(p, [ps], wtile, xn, 128, TC,
                                      lhs_dep=wtok if m == 0 else None,
                                      m_base=m * 128)
                    ev = emit_ew(p, "s", lambda E, m=m, outt=outt, bofs=bofs:
                                 E.activation(outt[m][:, :], ps[:, :], AF.Identity,
                                              bias=bias[:, bofs + m:bofs + m + 1]),
                                 dep=[tok] + btok)
            p.wait("g", ev)
            wtok = [p.dma("g", wtile[k][:, :], io["l1a_wv"][k * 128:(k + 1) * 128, :])
                    for k in range(8)]
            vt = p.sbl("l1a_vt", NT, D, BF16, ctx=c1)
            for m in range(NT):
                p.wait("t", ev)
                if m == 0:
                    p.wait("t", wtok + [bvb_tok])
                for n0 in (0, 512):
                    for k in range(8):
                        def thunk(E, m=m, n0=n0, k=k):
                            return E.matmul(ps[:, n0:n0 + 512],
                                            xn[k][:, m * 128:(m + 1) * 128],
                                            wtile[k][:, n0:n0 + 512],
                                            start=(k == 0), stop=False)
                        p.do("t", thunk)

                    def thunk(E, m=m, n0=n0):
                        return E.matmul(ps[:, n0:n0 + 512], ones_row_bf[0:1, 0:128],
                                        bvrow_bf[0:1, n0:n0 + 512],
                                        start=False, stop=True)
                    tok = p.do("t", thunk, tick=True)
                ev = emit_ew(p, "s", lambda E, m=m: E.activation(
                    vt[m][:, 0:D], ps[:, 0:D], AF.Copy), dep=tok)
            kst_tok = []
            for k in range(8):
                p.wait("y", ev if k == 0 else None)
                kst_tok.append(p.dma("y", kt_loc[k * 128:(k + 1) * 128, :],
                                     kt_tiles[k][:, :]))
                kst_tok.append(p.dma("y", qt_d[k * 128:(k + 1) * 128, :],
                                     qt[k][:, :]))
            for m in range(NT):
                kst_tok.append(p.dma("y", v_loc[m * 128:(m + 1) * 128, :],
                                     vt[m][:, :]))
        cc1 = p.collective("AllGather", ALU.bypass, [kt_loc[:, :]], [kt_all[:, :]],
                           ngroups=ncores)
        cc2 = p.collective("AllGather", ALU.bypass, [v_loc[:, :]], [v_all[:, :]],
                           ngroups=ncores)
        p.barrier()
        xn_ctx.close()
        with p.phase("flash") as c2:
            p.wait("y", [cc1, cc2])
            sfull = p.sb("l1a_s", [128, Lg])
            pbf = p.sb("l1a_p", [128, Lg], BF16)
            kst = [p.sbl("l1a_k0", 8, TC, BF16, ctx=c2),
                   p.sbl("l1a_k1", 8, TC, BF16, ctx=c2)]
            vst = [p.sb("l1a_v0", [128, D], BF16), p.sb("l1a_v1", [128, D], BF16)]
            rowst = p.sb("l1a_rows", [128, 4])
            obuf = sfull[:, 0:D]
            ps_s = [p.ps_("l1a_ss0", [128, 512], F32, ctx=c2),
                    p.ps_("l1a_ss1", [128, 512], F32, ctx=c2)]
            ps_pt = [p.ps_("l1a_pt0", [128, 2048], BF16, ctx=c2),
                     p.ps_("l1a_pt1", [128, 2048], BF16, ctx=c2)]
            ps_o = p.ps_("l1a_pso", [128, D], F32, ctx=c2)
            ptsb = [p.sb("l1a_pb0", [128, 2048], BF16),
                    p.sb("l1a_pb1", [128, 2048], BF16)]
            kuse = [None, None]
            qcur = p.sbl("l1a_qc", 8, 128, BF16, ctx=c2)
            for q in range(NT):
                p.barrier()
                qld = [p.dma("y", qcur[k][:, :],
                             qt_d[k * 128:(k + 1) * 128, q * 128:(q + 1) * 128])
                       for k in range(8)]
                p.wait("t", qld)
                sev = [None, None]
                for r in range(ncores):
                    ppk = r % 2
                    p.wait("y", kuse[ppk])
                    p.wait("s", kuse[ppk])
                    kld = [p.dma("y" if k < 4 else "s", kst[ppk][k][:, :],
                                 kt_all[r * D + k * 128:r * D + (k + 1) * 128, :])
                           for k in range(8)]
                    p.wait("t", kld)
                    for ci, n0 in enumerate(range(0, TC, 512)):
                        pps = ci % 2
                        p.wait("t", sev[pps])
                        for k in range(8):
                            def thunk(E, ppk=ppk, pps=pps, n0=n0, k=k):
                                return E.matmul(ps_s[pps][:, :],
                                                qcur[k][:, :],
                                                kst[ppk][k][:, n0:n0 + 512],
                                                start=(k == 0), stop=(k == 7))
                            p.do("t", thunk, tick=(k == 7))
                        tok = ("pt", p.cnt["pt"])
                        sev[pps] = emit_ew(
                            p, "s", lambda E, r=r, n0=n0, pps=pps: E.activation(
                                sfull[:, r * TC + n0:r * TC + n0 + 512],
                                ps_s[pps][:, :], AF.Identity, scale=SCALE), dep=tok)
                    kuse[ppk] = sev[(len(range(0, TC, 512)) - 1) % 2]
                mx = emit_ew(p, "v", lambda E: E.tensor_reduce(
                    rowst[:, 0:1], sfull[:, :], AX.X, ALU.max, negate=True),
                    dep=[sev[0], sev[1]])
                ex = emit_ew(p, "s", lambda E: E.activation(
                    pbf[:, :], sfull[:, :], AF.Exp, bias=rowst[:, 0:1],
                    accum_out=rowst[:, 1:2]), dep=mx)
                rz = emit_ew(p, "v", lambda E: E.reciprocal(rowst[:, 2:3],
                                                            rowst[:, 1:2]), dep=ex)
                vuse = [None, None]
                ptuse = [None, None]
                first_half = {0: True, 512: True}
                nblk = Lg // 2048
                for blk in range(nblk):
                    ppb = blk % 2
                    p.wait("t", [ex, ptuse[ppb]])
                    for j in range(16):
                        def thunk(E, blk=blk, j=j, ppb=ppb):
                            return E.transpose(
                                ps_pt[ppb][:, j * 128:(j + 1) * 128],
                                pbf[:, blk * 2048 + j * 128:blk * 2048 + (j + 1) * 128],
                                consts["ident_bf"][:, :])
                        tok = p.do("t", thunk, tick=(j == 15))
                    ptev = emit_ew(p, "v", lambda E, ppb=ppb: E.tensor_copy(
                        ptsb[ppb][:, :], ps_pt[ppb][:, :]), dep=tok)
                    for j in range(16):
                        pj = j % 2
                        vq = "y" if pj == 0 else "s"
                        p.wait(vq, vuse[pj])
                        vld = p.dma(vq, vst[pj][:, :],
                                    v_all[blk * 2048 + j * 128:blk * 2048 + (j + 1) * 128, :])
                        p.wait("t", [vld, ptev])
                        for n0 in (0, 512):
                            def thunk(E, j=j, pj=pj, ppb=ppb, n0=n0,
                                      first=first_half[n0]):
                                return E.matmul(ps_o[:, n0:n0 + 512],
                                                ptsb[ppb][:, j * 128:(j + 1) * 128],
                                                vst[pj][:, n0:n0 + 512],
                                                start=first, stop=False)
                            tok = p.do("t", thunk, tick=(n0 == 512))
                            first_half[n0] = False
                        vuse[pj] = tok
                    ptuse[ppb] = tok
                for n0 in (0, 512):
                    def thunk(E, n0=n0):
                        return E.matmul(ps_o[:, n0:n0 + 512], ones_row_bf[0:1, 0:128],
                                        consts["zero_row_bf"][0:1, 0:512],
                                        start=False, stop=True)
                    tok = p.do("t", thunk, tick=(n0 == 512))
                ev = emit_ew(p, "s", lambda E: E.activation(
                    obuf[:, :], ps_o[:, :], AF.Identity, scale=rowst[:, 2:3]),
                    dep=[tok, rz])
                p.wait("y", ev)
                st = p.dma("y", oscr[q * 128:(q + 1) * 128, :], obuf[:, :])
                p.wait("s", st)
        with p.phase("resup") as c3:
            osb = p.sbl("l1a_ot", NT, D, F32, ctx=c3)
            otok = [p.dma("y", osb[m][:, :], oscr[m * 128:(m + 1) * 128, :])
                    for m in range(NT)]
            ps_t = p.ps_("l1a_pst", [128, TC], F32, ctx=c3)
            rt2 = p.sb("l1a_rt2", [128, TC])
            p.wait("t", otok)
            st = None
            for dk in range(8):
                for m in range(NT):
                    def thunk(E, dk=dk, m=m):
                        return E.transpose(ps_t[:, m * 128:(m + 1) * 128],
                                           osb[m][:, dk * 128:(dk + 1) * 128],
                                           consts["ident_f32"][:, :])
                    tok = p.do("t", thunk, tick=(m == NT - 1))
                p.wait("y", st)
                ld = p.dma("y", rt2[:, :], io["res_in"][dk * 128:(dk + 1) * 128, :])
                sv = emit_ew(p, "v", lambda E, dk=dk: E.tensor_tensor(
                    rt2[:, :], rt2[:, :], ps_t[:, :], ALU.add), dep=[tok, ld])
                p.wait("y", sv)
                st = p.dma("y", io["res_out"][dk * 128:(dk + 1) * 128, :], rt2[:, :])
                p.wait("v", st)
                p.wait("t", sv)
    return None


def build_l1attn_standalone(ncores=1):
    nc = bass.Bass("TRN2", target_bir_lowering=False, debug=False)
    io = declare_io(nc, L1ATTN_IO)
    io["res_in"] = nc.dram_tensor("res_in", [D, TC], F32, kind="ExternalInput")
    io["res_out"] = nc.dram_tensor("res_out", [D, TC], F32, kind="ExternalOutput")
    p = Prog(nc)
    consts = setup_consts(p)
    block_l1attn(p, io, consts, ncores)
    p.build()
    return nc



NHEAD = 16
DHEAD = 64
WIN = 256
INNER = 1024
DI2 = 2730
NJ = cdiv(DI2, 128)

L1FFN_IO = {}
for _i in range(6):
    L1FFN_IO.update({
        f"l1f_wqkv{_i}": [D, 3 * INNER],
        f"l1f_wo{_i}": [INNER, D],
        f"l1f_w1{_i}": [D, 256 * (NJ - 1) + 84],
        f"l1f_w2{_i}": [DI2, D],
        f"l1f_lng{_i}": [128, 8],
        f"l1f_lnb{_i}": [128, 8],
        f"l1f_b1{_i}": [128, 2 * NJ],
        f"l1f_b2{_i}": [128, 8],
    })
for _i in range(6):
    L1FFN_IO.update({
        f"l1f_cosq{_i}": [TC, DHEAD], f"l1f_sinq{_i}": [TC, DHEAD],
        f"l1f_cosk{_i}": [TC, DHEAD], f"l1f_sink{_i}": [TC, DHEAD],
    })
L1FFN_IO.update({
    "l1f_mask": [128, 2048],
    "l1f_c0": [128, 1],
    "l1f_psel": [128, 8],
})


def host_prep_l1ffn(inp, core_id):
    f32 = np.float32
    out = {}
    for i in range(6):
        W1 = inp["L1_ffn_W1"][i].astype(f32)
        chunks = []
        b1 = inp["L1_ffn_b1"][i].astype(f32)
        b1c = []
        for j in range(NJ):
            a0, a1 = j * 128, min((j + 1) * 128, DI2)
            chunks.append(W1[:, a0:a1])
            b1c.append(np.pad(b1[a0:a1], (0, 128 - (a1 - a0))))
            chunks.append(W1[:, DI2 + a0:DI2 + a1])
            b1c.append(np.pad(b1[DI2 + a0:DI2 + a1], (0, 128 - (a1 - a0))))
        out[f"l1f_w1{i}"] = np.ascontiguousarray(np.concatenate(chunks, 1))
        out[f"l1f_b1{i}"] = np.ascontiguousarray(np.stack(b1c, 1))
        out[f"l1f_wqkv{i}"] = np.ascontiguousarray(inp["L1_ffn_Wqkv"][i].astype(f32))
        out[f"l1f_wo{i}"] = np.ascontiguousarray(inp["L1_ffn_Wo"][i].astype(f32))
        out[f"l1f_w2{i}"] = np.ascontiguousarray(inp["L1_ffn_W2"][i].astype(f32))
        out[f"l1f_lng{i}"] = np.ascontiguousarray(
            inp["L1_ffn_ln_g"][i].reshape(8, 128).T.astype(f32))
        out[f"l1f_lnb{i}"] = np.ascontiguousarray(
            inp["L1_ffn_ln_b"][i].reshape(8, 128).T.astype(f32))
        out[f"l1f_b2{i}"] = np.ascontiguousarray(
            inp["L1_ffn_b2"][i].reshape(8, 128).T.astype(f32))
    half = np.arange(0, DHEAD, 2, dtype=np.float64)
    inv_freq = 1.0 / 10000.0 ** (half / DHEAD)
    t = np.arange(core_id * TC, (core_id + 1) * TC, dtype=np.float64)
    freqs = t[:, None] * inv_freq[None, :]
    freqs = np.concatenate([freqs, freqs], -1)
    base_scale = (half + 0.4 * DHEAD) / (1.4 * DHEAD)
    power = (t - L // 2) / (WIN // 2)
    scale = base_scale[None, :] ** power[:, None]
    scale = np.concatenate([scale, scale], -1)
    for i in range(6):
        qs = inp["L1_ffn_q_scale"][i].astype(np.float64)
        ks = inp["L1_ffn_k_scale"][i].astype(np.float64)
        qsr = np.concatenate([qs[DHEAD // 2:], qs[:DHEAD // 2]])
        ksr = np.concatenate([ks[DHEAD // 2:], ks[:DHEAD // 2]])
        out[f"l1f_cosq{i}"] = (np.cos(freqs) * scale * qs).astype(f32)
        out[f"l1f_sinq{i}"] = (np.sin(freqs) * scale * qsr).astype(f32)
        out[f"l1f_cosk{i}"] = (np.cos(freqs) / scale * ks).astype(f32)
        out[f"l1f_sink{i}"] = (np.sin(freqs) / scale * ksr).astype(f32)
    m = np.zeros((2, 128, 512), np.float32)
    for hlf in range(2):
        for i_ in range(128):
            iw = hlf * 128 + i_
            for j_ in range(512):
                jw = j_ - 256
                d_ = iw - jw
                m[hlf, i_, j_] = 1.0 if (0 <= d_ <= 256) else 0.0
    mfirst = m.copy()
    if core_id == 0:
        mfirst[:, :, :256] = 0.0
    out["l1f_mask"] = np.ascontiguousarray(
        np.concatenate([m.transpose(1, 0, 2).reshape(128, 1024),
                        mfirst.transpose(1, 0, 2).reshape(128, 1024)],
                       axis=1))
    out["l1f_c0"] = np.full((128, 1), 0.0 if core_id == 0 else 1.0, np.float32)
    ps = np.zeros((128, 8), np.float32)
    if core_id > 0:
        ps[:, core_id - 1] = 1.0
    out["l1f_psel"] = ps
    return out


def block_l1ffn(p, io, consts, ncores, core_id=0, nlayers=6):
    nc = p.nc
    ones, ones_row, ones_row_bf = consts["ones"], consts["ones_row"], consts["ones_row_bf"]
    NTT = TC // 128
    with p.phase("l1ffn") as ctx:
        aspace = "Shared" if ncores > 4 else "Local"
        xping = [nc.dram_tensor(f"l1f_x{j}_{p.uid()}", [D, TC], F32) for j in range(2)]
        hl_loc = nc.dram_tensor(f"l1f_hl_{p.uid()}", [128, 4], F32)
        hl_all = nc.dram_tensor(f"l1f_hla_{p.uid()}", [ncores * 128, 4], F32,
                                addr_space=aspace)
        kw_loc = nc.dram_tensor(f"l1f_kwl_{p.uid()}", [INNER, WIN], BF16)
        kw_all = nc.dram_tensor(f"l1f_kwa_{p.uid()}", [ncores * INNER, WIN], BF16,
                                addr_space=aspace)
        vw_loc = nc.dram_tensor(f"l1f_vwl_{p.uid()}", [WIN, INNER], BF16)
        vw_all = nc.dram_tensor(f"l1f_vwa_{p.uid()}", [ncores * WIN, INNER], BF16,
                                addr_space=aspace)
        xat_scr = nc.dram_tensor(f"l1f_xat_{p.uid()}", [D, TC], F32)
        prev = (core_id - 1) % ncores
        c0m = p.sb("l1f_c0m", [128, 1])
        psel = p.sb("l1f_ps", [128, 8])
        hscr = [p.sb("l1f_hs0", [128, 4]), p.sb("l1f_hs1", [128, 4])]
        mskb = p.sb("l1f_mskb", [128, 2048], BF16)
        ctok = [p.dma("y", c0m[:, :], io["l1f_c0"][:, :]),
                p.dma("y", psel[:, :], io["l1f_psel"][:, :])]
        for e_ in ("v", "s"):
            p.wait(e_, ctok)
        pcf = [psel[:, cc:cc + 1] for cc in range(8)]
        with p.phase("mskload") as cm0:
            msk = p.sb("l1f_msk", [128, 2048], ctx=cm0)
            mt = p.dma("y", msk[:, :], io["l1f_mask"][:, :])
            emit_ew(p, "v", lambda E: E.tensor_copy(mskb[:, :], msk[:, :]),
                    dep=[mt] + ctok)
        xln_d = nc.dram_tensor(f"l1f_xln_{p.uid()}", [D, TC], F32)
        with p.phase("l1f_pre") as cpre:
            x32p, xtokp = load_tiles(p, io["res_in"], D, TC, F32,
                                     name="l1f_px", ctx=cpre)
            t1p = p.sbl("l1f_pt1", 8, TC, F32, ctx=cpre)
            for e_ in ("t", "s", "v"):
                p.wait(e_, xtokp)
            emit_ln_feat(p, x32p, t1p, ones=consts["ones"],
                         ones_row=consts["ones_row"])
            pst_ = None
            for k in range(8):
                p.wait("y", [("pv", p.cnt["pv"]), pst_])
                pst_ = p.dma("y", xln_d[k * 128:(k + 1) * 128, :], t1p[k][:, :])
            p.wait("v", pst_)
        p.barrier()
        for li in range(nlayers):
            xin = xln_d if li == 0 else xping[li % 2]
            xout = io["res_out"] if li == nlayers - 1 else xping[(li + 1) % 2]
            with p.phase(f"l1f_at{li}") as c1:
                qT = p.sbl("qT", 8, TC, BF16, ctx=c1)
                kT = p.sbl("kT", 8, TC, BF16, ctx=c1)
                vt = p.sbl("vtt", NTT, INNER, BF16, ctx=c1)
                cqkv = ExitStack()
                p._ctx.append(cqkv)
                xs = p.sbl("xsh", 8, TC, BF16)
                x2 = p.sb("x2a", [128, TC])
                contrib = p.sb("ctb", [128, 4])
                halo = p.sb("halo", [128, 4])
                xuse = None
                for k in range(8):
                    p.wait("y", xuse)
                    xt_ = p.dma("y", x2[:, :], xin[k * 128:(k + 1) * 128, :])
                    if k < 4:
                        xuse = emit_ew(p, "v", lambda E, k=k: E.tensor_copy(
                            xs[k][:, :], x2[:, :]), dep=xt_)
                    else:
                        emit_ew(p, "v", lambda E, k=k: E.tensor_copy(
                            xs[k][:, 1:TC], x2[:, 0:TC - 1]), dep=xt_)
                        xuse = emit_ew(p, "v", lambda E, k=k, contrib=contrib:
                                       E.tensor_copy(
                                           contrib[:, k - 4:k - 3],
                                           x2[:, TC - 1:TC]))
                p.wait("y", xuse)
                cst = p.dma("y", hl_loc[:, :], contrib[:, :])
                p.wait("g", cst)
                cc = p.collective("AllGather", ALU.bypass, [hl_loc[:, :]],
                                  [hl_all[:, :]], ngroups=ncores)
                p.wait("y", cc)
                hz = emit_masked_pick(p, halo, hl_all, 0, 128, 4, pcf, hscr,
                                      ncores)
                for k in range(4):
                    emit_ew(p, "v", lambda E, k=k, halo=halo: E.tensor_copy(
                        xs[4 + k][:, 0:1], halo[:, k:k + 1]), dep=hz)
                xs_tok = ("pv", p.cnt["pv"])
                rot = p.sb("l1f_rot", [128, 4 * DHEAD])
                wq = p.sbl("wqk", 8, 2 * INNER, BF16)
                wtok = [p.dma("g", wq[k][:, :],
                              io[f"l1f_wqkv{li}"][k * 128:(k + 1) * 128, 0:2 * INNER])
                        for k in range(8)]
                psq = [p.ps_("psq0", [128, 512], F32),
                       p.ps_("psq1", [128, 512], F32)]
                pst = p.ps_("pstr", [128, 2048], BF16)
                qraw = p.sb("qraw", [128, 2 * INNER])
                sqs = p.sb("sqs", [128, 64])
                rhb = p.sb("rhb", [128, INNER])
                qrot = p.sb("qrot", [128, 2 * INNER], BF16)
                p.wait("t", wtok + [xs_tok])
                ev = [None, None]
                trev = None
                rot_lds = []
                rot_cons = None
                for tt in range(NTT):
                    for ci in range(4):
                        pp = ci % 2
                        p.wait("t", ev[pp])
                        for k in range(8):
                            def thunk(E, tt=tt, ci=ci, k=k, pp=pp):
                                return E.matmul(psq[pp][:, :],
                                                xs[k][:, tt * 128:(tt + 1) * 128],
                                                wq[k][:, ci * 512:(ci + 1) * 512],
                                                start=(k == 0), stop=(k == 7))
                            tok = p.do("t", thunk, tick=(k == 7))
                        ev[pp] = emit_ew(p, "s", lambda E, ci=ci, pp=pp: E.activation(
                            qraw[:, ci * 512:(ci + 1) * 512], psq[pp][:, :],
                            AF.Identity), dep=tok)
                    p.wait("y", rot_lds + [rot_cons])
                    rot_lds = []
                    for ri, rname in enumerate((f"l1f_cosq{li}", f"l1f_sinq{li}",
                                                f"l1f_cosk{li}", f"l1f_sink{li}")):
                        rot_lds.append(
                            p.dma("y", rot[:, ri * DHEAD:(ri + 1) * DHEAD],
                                  io[rname][tt * 128:(tt + 1) * 128, :]))
                    p.wait("v", rot_lds)

                    def rotslc(ri, tt):
                        return rot[:, ri * DHEAD:(ri + 1) * DHEAD]

                    rhb_last = None
                    for (ofs, sofs, cri, sri) in ((0, 0, 0, 1), (INNER, 96, 2, 3)):
                        sq_t = emit_ew(p, "s", lambda E, ofs=ofs: E.activation(
                            rhb[:, :], qraw[:, ofs:ofs + INNER], AF.Square),
                            dep=[ev[0], ev[1], rhb_last])
                        emit_ew(p, "v", lambda E: E.tensor_reduce(
                            sqs[:, 0:16], rhb[:, :].rearrange("p (h d) -> p h d", h=16),
                            AX.X, ALU.add), dep=sq_t)
                        mx_t = emit_ew(p, "v", lambda E: E.tensor_scalar_max(
                            sqs[:, 16:32], sqs[:, 0:16], 1e-24))
                        emit_ew(p, "s", lambda E: E.activation(
                            sqs[:, 32:48], sqs[:, 16:32], AF.Ln), dep=mx_t)
                        rst_t = emit_ew(p, "s", lambda E: E.activation(
                            sqs[:, 32:48], sqs[:, 32:48], AF.Exp, scale=-0.5),
                            sedge=True)
                        for h in range(16):
                            emit_ew(p, "v", lambda E, ofs=ofs, h=h: E.tensor_scalar(
                                qraw[:, ofs + h * 64:ofs + (h + 1) * 64],
                                qraw[:, ofs + h * 64:ofs + (h + 1) * 64],
                                sqs[:, 32 + h:33 + h], None, ALU.mult),
                                sedge=(h == 0), dep=rst_t if h == 0 else None)
                        emit_ew(p, "v", lambda E, ofs=ofs: E.tensor_scalar(
                            rhb[:, :].rearrange("p (h c d) -> p h c d", c=2, d=32)[:, :, 0:1, :],
                            qraw[:, ofs:ofs + INNER].rearrange(
                                "p (h c d) -> p h c d", c=2, d=32)[:, :, 1:2, :],
                            -1.0, None, ALU.mult))
                        emit_ew(p, "v", lambda E, ofs=ofs: E.tensor_copy(
                            rhb[:, :].rearrange("p (h c d) -> p h c d", c=2, d=32)[:, :, 1:2, :],
                            qraw[:, ofs:ofs + INNER].rearrange(
                                "p (h c d) -> p h c d", c=2, d=32)[:, :, 0:1, :]))
                        emit_ew(p, "v", lambda E, ofs=ofs, cri=cri, tt=tt: E.tensor_tensor(
                            qraw[:, ofs:ofs + INNER].rearrange("p (h d) -> p h d", h=16),
                            qraw[:, ofs:ofs + INNER].rearrange("p (h d) -> p h d", h=16),
                            rotslc(cri, tt).rearrange("p (a d) -> p a d", a=1).broadcast_to(
                                (128, 16, DHEAD)), ALU.mult))
                        emit_ew(p, "v", lambda E, sri=sri, tt=tt: E.tensor_tensor(
                            rhb[:, :].rearrange("p (h d) -> p h d", h=16),
                            rhb[:, :].rearrange("p (h d) -> p h d", h=16),
                            rotslc(sri, tt).rearrange("p (a d) -> p a d", a=1).broadcast_to(
                                (128, 16, DHEAD)), ALU.mult))
                        rhb_last = emit_ew(p, "v", lambda E, ofs=ofs: E.tensor_tensor(
                            qrot[:, ofs:ofs + INNER], qraw[:, ofs:ofs + INNER],
                            rhb[:, :], ALU.add))
                    rot_tok = ("pv", p.cnt["pv"])
                    rot_cons = rot_tok
                    p.wait("t", [rot_tok, trev])
                    for j in range(16):
                        def thunk(E, tt=tt, j=j, pst=pst, qrot=qrot):
                            return E.transpose(
                                pst[:, j * 128:(j + 1) * 128],
                                qrot[:, j * 128:(j + 1) * 128],
                                consts["ident_bf"][:, :])
                        tok = p.do("t", thunk, tick=(j == 15))
                    for j in range(8):
                        trev = emit_ew(p, "v", lambda E, tt=tt, j=j, pst=pst, qT=qT:
                                       E.tensor_copy(
                                           qT[j][:, tt * 128:(tt + 1) * 128],
                                           pst[:, j * 128:(j + 1) * 128]),
                                       dep=tok if j == 0 else None)
                    for j in range(8):
                        trev = emit_ew(p, "v", lambda E, tt=tt, j=j, pst=pst, kT=kT:
                                       E.tensor_copy(
                                           kT[j][:, tt * 128:(tt + 1) * 128],
                                           pst[:, (8 + j) * 128:(8 + j + 1) * 128]))
                p.barrier()
                wvtok = [p.dma("g", wq[k][:, 0:INNER],
                               io[f"l1f_wqkv{li}"][k * 128:(k + 1) * 128,
                                                   2 * INNER:3 * INNER])
                         for k in range(8)]
                p.wait("t", wvtok)
                ev = [None, None]
                for tt in range(NTT):
                    for ci in range(2):
                        pp = ci % 2
                        p.wait("t", ev[pp])
                        for k in range(8):
                            def thunk(E, tt=tt, ci=ci, k=k, pp=pp):
                                return E.matmul(psq[pp][:, :],
                                                xs[k][:, tt * 128:(tt + 1) * 128],
                                                wq[k][:, ci * 512:(ci + 1) * 512],
                                                start=(k == 0), stop=(k == 7))
                            tok = p.do("t", thunk, tick=(k == 7))
                        ev[pp] = emit_ew(p, "s", lambda E, tt=tt, ci=ci, pp=pp:
                                         E.activation(
                                             vt[tt][:, ci * 512:(ci + 1) * 512],
                                             psq[pp][:, :], AF.Identity), dep=tok)
                p.barrier()
                p._ctx.pop()
                cqkv.close()
                oT = p.sbl("oT", 8, TC, BF16, ctx=c1)
                catt = ExitStack()
                p._ctx.append(catt)
                pts = p.sb("ptscr", [128, 2048], BF16)
                pst = p.ps_("pstr2", [128, 2048], BF16)
                kvt = []
                for j in range(8):
                    kvt.append(p.dma("y", kw_loc[j * 128:(j + 1) * 128, :],
                                     kT[j][:, TC - WIN:TC]))
                for j in range(2):
                    kvt.append(p.dma("y", vw_loc[j * 128:(j + 1) * 128, :],
                                     vt[NTT - 2 + j][:, :]))
                p.wait("g", kvt)
                cck = p.collective("AllGather", ALU.bypass, [kw_loc[:, :]],
                                   [kw_all[:, :]], ngroups=ncores)
                ccv = p.collective("AllGather", ALU.bypass, [vw_loc[:, :]],
                                   [vw_all[:, :]], ngroups=ncores)
                kprev = p.sbl("kprev", 8, WIN, BF16)
                vprev = p.sbl("vprev", 2, INNER, BF16)
                kvscr = [p.sb("kvs0", [128, INNER], BF16),
                         p.sb("kvs1", [128, INNER], BF16)]
                p.wait("y", [cck, ccv])
                kvtok = []
                pktok = None
                for j in range(8):
                    pktok = emit_masked_pick(p, kprev[j], kw_all, j * 128,
                                             INNER, WIN, pcf, kvscr, ncores,
                                             war=pktok)
                    kvtok.append(pktok)
                for j in range(2):
                    pktok = emit_masked_pick(p, vprev[j], vw_all, j * 128,
                                             WIN, INNER, pcf, kvscr, ncores,
                                             war=pktok)
                    kvtok.append(pktok)
                sfb = p.sb("sfb", [128, 4096])
                pfb = p.sb("pfb", [128, 4096], BF16)
                pfb2 = p.sb("pfb2", [128, 4096], BF16)
                srows = p.sb("srows", [128, 48])
                psS_one = p.ps_("psS0", [128, 2048], F32)
                psS = [psS_one, psS_one]
                psO = p.ps_("psO", [128, 1024], F32)
                p.wait("t", kvtok)
                oev = None
                sev = [None, None]
                tok = None
                for wh in range(16):
                    w, half = wh // 2, wh % 2
                    for hg in range(2):
                        for grp in range(2):
                            pp = grp % 2
                            p.wait("t", [sev[0], sev[1]])
                            for hh in range(4):
                                h = hg * 8 + grp * 4 + hh
                                jt, pb = h // 2, (h % 2) * 64
                                if w == 0:
                                    def thunk(E, jt=jt, pb=pb, hh=hh, wh=wh, pp=pp):
                                        return E.matmul(
                                            psS[pp][:, hh * 512:hh * 512 + 256],
                                            qT[jt][pb:pb + 64, wh * 128:(wh + 1) * 128],
                                            kprev[jt][pb:pb + 64, :],
                                            start=True, stop=True)
                                    p.do("t", thunk)

                                    def thunk(E, jt=jt, pb=pb, hh=hh, wh=wh, pp=pp):
                                        return E.matmul(
                                            psS[pp][:, hh * 512 + 256:(hh + 1) * 512],
                                            qT[jt][pb:pb + 64, wh * 128:(wh + 1) * 128],
                                            kT[jt][pb:pb + 64, 0:256],
                                            start=True, stop=True)
                                    tok = p.do("t", thunk, tick=(hh == 3))
                                else:
                                    def thunk(E, jt=jt, pb=pb, hh=hh, w=w, wh=wh, pp=pp):
                                        return E.matmul(
                                            psS[pp][:, hh * 512:(hh + 1) * 512],
                                            qT[jt][pb:pb + 64, wh * 128:(wh + 1) * 128],
                                            kT[jt][pb:pb + 64, w * 256 - 256:w * 256 + 256],
                                            start=True, stop=True)
                                    tok = p.do("t", thunk, tick=(hh == 3))
                            sev[pp] = emit_ew(p, "s", lambda E, grp=grp, pp=pp:
                                              E.activation(
                                                  sfb[:, grp * 2048:(grp + 1) * 2048],
                                                  psS[pp][:, :], AF.Identity,
                                                  scale=8.0), dep=tok)
                        mx = emit_ew(p, "v", lambda E: E.tensor_reduce(
                            srows[:, 0:8], sfb[:, :].rearrange("p (h j) -> p h j", h=8),
                            AX.X, ALU.max, negate=True), dep=[sev[0], sev[1]])
                        for h8 in range(8):
                            emit_ew(p, "v", lambda E, h8=h8: E.tensor_scalar(
                                sfb[:, h8 * 512:(h8 + 1) * 512],
                                sfb[:, h8 * 512:(h8 + 1) * 512],
                                srows[:, h8:h8 + 1], None, ALU.add), sedge=(h8 == 0))
                        add_tok = ("pv", p.cnt["pv"])
                        exp_t = emit_ew(p, "s", lambda E: E.activation(
                            pfb[:, :], sfb[:, :], AF.Exp), dep=add_tok)
                        mvar = 1024 if w == 0 else 0
                        emit_ew(p, "v", lambda E, half=half, mvar=mvar: E.tensor_tensor(
                            pfb2[:, :].rearrange("p (h j) -> p h j", h=8),
                            pfb[:, :].rearrange("p (h j) -> p h j", h=8),
                            mskb[:, mvar + half * 512:mvar + (half + 1) * 512].rearrange(
                                "p (a j) -> p a j", a=1).broadcast_to((128, 8, 512)), ALU.mult),
                            dep=exp_t)
                        emit_ew(p, "v", lambda E: E.tensor_reduce(
                            srows[:, 16:24], pfb2[:, :].rearrange("p (h j) -> p h j", h=8),
                            AX.X, ALU.add), sedge=True)
                        emit_ew(p, "v", lambda E: E.reciprocal(srows[:, 32:40],
                                                               srows[:, 16:24]),
                                sedge=True)
                        for h8 in range(8):
                            emit_ew(p, "v", lambda E, h8=h8: E.tensor_scalar(
                                pfb2[:, h8 * 512:(h8 + 1) * 512],
                                pfb2[:, h8 * 512:(h8 + 1) * 512],
                                srows[:, 32 + h8:33 + h8], None, ALU.mult),
                                sedge=(h8 == 0))
                        pn_tok = ("pv", p.cnt["pv"])
                        p.wait("t", [pn_tok, oev])
                        for h8 in range(8):
                            h = hg * 8 + h8
                            for kc in range(4):
                                def thunk(E, h8=h8, kc=kc, pst=pst):
                                    return E.transpose(
                                        pst[:, ((h8 % 4) * 4 + kc) * 128:((h8 % 4) * 4 + kc + 1) * 128],
                                        pfb2[:, h8 * 512 + kc * 128:h8 * 512 + (kc + 1) * 128],
                                        consts["ident_bf"][:, :])
                                tok = p.do("t", thunk, tick=(kc == 3))
                            ptev = emit_ew(p, "v", lambda E, h8=h8, pst=pst:
                                           E.tensor_copy(
                                               pts[:, (h8 % 4) * 512:((h8 % 4) + 1) * 512],
                                               pst[:, (h8 % 4) * 512:((h8 % 4) + 1) * 512]),
                                           dep=tok)
                            p.wait("t", ptev)
                            jt, pb = h // 2, (h % 2) * 64
                            for kc in range(4):
                                if w == 0:
                                    vsrc = (vprev[kc] if kc < 2 else vt[kc - 2])
                                else:
                                    vsrc = vt[2 * w - 2 + kc]

                                def thunk(E, h=h, h8=h8, kc=kc, jt=jt, pb=pb, vsrc=vsrc):
                                    return E.matmul(
                                        psO[pb:pb + 64, jt * 128:(jt + 1) * 128],
                                        vsrc[:, h * 64:(h + 1) * 64],
                                        pts[:, ((h8 % 4) * 4 + kc) * 128:((h8 % 4) * 4 + kc + 1) * 128],
                                        start=(kc == 0), stop=(kc == 3))
                                tok = p.do("t", thunk, tick=(kc == 3))
                    oev = None
                    for jt in range(8):
                        oev = emit_ew(p, "v", lambda E, jt=jt, wh=wh: E.tensor_copy(
                            oT[jt][:, wh * 128:(wh + 1) * 128],
                            psO[:, jt * 128:(jt + 1) * 128]),
                            dep=tok if jt == 0 else None)
                p.barrier()
                p._ctx.pop()
                catt.close()
                wo = p.sbl("wo", 8, D, BF16, ctx=c1)
                wotok = [p.dma("g", wo[k][:, :],
                               io[f"l1f_wo{li}"][k * 128:(k + 1) * 128, :])
                         for k in range(8)]
                psA = [p.ps_("psA0", [128, 512], F32, ctx=c1),
                       p.ps_("psA1", [128, 512], F32, ctx=c1)]
                aot = p.sb("aot", [128, 512])
                xres = p.sb("xres", [128, 512])
                p.wait("t", wotok)
                aev = [None, None]
                sst = None
                for m in range(8):
                    for ci in range(4):
                        pp = (m * 4 + ci) % 2
                        p.wait("t", aev[pp])
                        for k in range(8):
                            def thunk(E, m=m, ci=ci, k=k, pp=pp):
                                return E.matmul(psA[pp][:, :],
                                                wo[k][:, m * 128:(m + 1) * 128],
                                                oT[k][:, ci * 512:(ci + 1) * 512],
                                                start=(k == 0), stop=(k == 7))
                            tok = p.do("t", thunk, tick=(k == 7))
                        p.wait("y", aev[pp])
                        xld = p.dma("y", xres[:, :],
                                    xin[m * 128:(m + 1) * 128, ci * 512:(ci + 1) * 512])
                        aev[pp] = emit_ew(p, "v", lambda E, pp=pp: E.tensor_tensor(
                            aot[:, :], psA[pp][:, :], xres[:, :], ALU.add),
                            dep=[tok, xld])
                        p.wait("y", [aev[pp], sst])
                        sst = p.dma("y", xat_scr[m * 128:(m + 1) * 128,
                                                 ci * 512:(ci + 1) * 512], aot[:, :])
                        p.wait("v", sst)
            with p.phase(f"l1f_ff{li}") as c2:
                xf = p.sbl("xfb", 8, TC, BF16, ctx=c2)
                cln = ExitStack()
                p._ctx.append(cln)
                xat = p.sbl("xat", 8, TC, F32)
                xtok = [p.dma("y", xat[k][:, :], xat_scr[k * 128:(k + 1) * 128, :])
                        for k in range(8)]
                contrib = p.sb("ctb2", [128, 4])
                halo = p.sb("halo2", [128, 4])
                p.wait("v", xtok)
                ct = None
                for k in range(4):
                    ct = emit_ew(p, "v", lambda E, k=k, contrib=contrib:
                                 E.tensor_copy(
                                     contrib[:, k:k + 1], xat[4 + k][:, TC - 1:TC]))
                p.wait("y", ct)
                cst = p.dma("y", hl_loc[:, :], contrib[:, :])
                p.wait("g", cst)
                cc = p.collective("AllGather", ALU.bypass, [hl_loc[:, :]],
                                  [hl_all[:, :]], ngroups=ncores)
                p.wait("y", cc)
                hz = emit_masked_pick(p, halo, hl_all, 0, 128, 4, pcf, hscr,
                                      ncores)
                xsh = p.sbl("xshf", 4, TC, F32)
                for k in range(4):
                    emit_ew(p, "v", lambda E, k=k: E.tensor_copy(
                        xsh[k][:, 1:TC], xat[4 + k][:, 0:TC - 1]))
                    emit_ew(p, "v", lambda E, k=k, halo=halo: E.tensor_copy(
                        xsh[k][:, 0:1], halo[:, k:k + 1]), dep=hz)
                lnp = p.sb("lnp", [128, 16])
                lnt = [p.dma("y", lnp[:, 0:8], io[f"l1f_lng{li}"][:, :]),
                       p.dma("y", lnp[:, 8:16], io[f"l1f_lnb{li}"][:, :])]
                p.wait("v", lnt)
                p.wait("t", ("pv", p.cnt["pv"]))
                emit_ln_feat(p, [xat[0], xat[1], xat[2], xat[3],
                                 xsh[0], xsh[1], xsh[2], xsh[3]], xf,
                             gamma=[lnp[:, i:i + 1] for i in range(8)],
                             beta=[lnp[:, 8 + i:9 + i] for i in range(8)],
                             eps=1e-5, ones=ones, ones_row=ones_row)
                p.barrier()
                p._ctx.pop()
                cln.close()
                w1 = p.sbl("w1t", 8, 5460, BF16, ctx=c2)
                w1tok = [p.dma("g", w1[k][:, :],
                               io[f"l1f_w1{li}"][k * 128:(k + 1) * 128, :])
                         for k in range(8)]
                w2 = []
                w2tok = []
                for j in range(NJ):
                    rsz = min(128, DI2 - j * 128)
                    w2.append(c2.enter_context(nc.sbuf_tensor(
                        f"w2t{j}_{p.uid()}", [rsz, D], BF16)))
                    w2tok.append(p.dma("g", w2[j][:, :],
                                       io[f"l1f_w2{li}"][j * 128:j * 128 + rsz, :]))
                b1c = p.sb("b1c", [128, 2 * NJ])
                b2c = p.sb("b2c", [128, 8])
                btok = [p.dma("y", b1c[:, :], io[f"l1f_b1{li}"][:, :]),
                        p.dma("y", b2c[:, :], io[f"l1f_b2{li}"][:, :])]
                psh = [p.ps_("psh0", [128, 512], F32, ctx=c2),
                       p.ps_("psh1", [128, 512], F32, ctx=c2)]
                psf = [p.ps_("psf0", [128, 512], F32, ctx=c2),
                       p.ps_("psf1", [128, 512], F32, ctx=c2)]
                ga = p.sb("ga", [128, 512])
                gg = p.sb("gg", [128, 512])
                agb = p.sbl("agb", NJ, 512, BF16, ctx=c2)
                fout = p.sb("fout", [128, 512])
                xres2 = p.sb("xres2", [128, 512])
                xres3 = p.sb("xres3", [128, 512])
                p.wait("t", w1tok)
                p.wait("s", btok)
                hev = [None, None]
                agt = None
                fst = None
                for tq in range(4):
                    for j in range(NJ):
                        rsz = min(128, DI2 - j * 128)
                        for which in (0, 1):
                            c0_ = 256 * j + which * rsz
                            pp = which
                            p.wait("t", hev[pp])
                            for k in range(8):
                                def thunk(E, tq=tq, k=k, pp=pp, c0_=c0_, rsz=rsz):
                                    return E.matmul(
                                        psh[pp][:rsz, :],
                                        w1[k][:, c0_:c0_ + rsz],
                                        xf[k][:, tq * 512:(tq + 1) * 512],
                                        start=(k == 0), stop=(k == 7))
                                tok = p.do("t", thunk, tick=(k == 7))
                            if which == 0:
                                hev[pp] = emit_ew(p, "s", lambda E, j=j, rsz=rsz, pp=pp:
                                                  E.activation(ga[:rsz, :],
                                                               psh[pp][:rsz, :],
                                                               AF.Identity,
                                                               bias=b1c[:rsz, 2 * j:2 * j + 1]),
                                                  dep=[tok, agt])
                            else:
                                hev[pp] = emit_ew(p, "s", lambda E, j=j, rsz=rsz, pp=pp:
                                                  E.activation(gg[:rsz, :],
                                                               psh[pp][:rsz, :],
                                                               AF.Gelu,
                                                               bias=b1c[:rsz, 2 * j + 1:2 * j + 2]),
                                                  dep=[tok, agt])
                        agt = emit_ew(p, "v", lambda E, j=j, rsz=rsz: E.tensor_tensor(
                            agb[j][:rsz, :], ga[:rsz, :], gg[:rsz, :], ALU.mult),
                            dep=[hev[0], hev[1]])
                    p.wait("t", [agt] + (w2tok if tq == 0 else []))
                    fev = [None, None]
                    for m in range(8):
                        pp = m % 2
                        p.wait("t", fev[pp])
                        for j in range(NJ):
                            rsz = min(128, DI2 - j * 128)

                            def thunk(E, tq=tq, m=m, j=j, pp=pp, rsz=rsz):
                                return E.matmul(psf[pp][:, :],
                                                w2[j][:, m * 128:(m + 1) * 128],
                                                agb[j][:rsz, :],
                                                start=(j == 0), stop=(j == NJ - 1))
                            tok = p.do("t", thunk, tick=(j == NJ - 1))
                        sev2 = emit_ew(p, "s", lambda E, m=m, pp=pp: E.activation(
                            ga[:, :] if False else gg[:, :], psf[pp][:, :], AF.Identity,
                            bias=b2c[:, m:m + 1]), dep=tok)
                        p.wait("y", fev[pp])
                        xld = p.dma("y", xres2[:, :],
                                    xat_scr[m * 128:(m + 1) * 128, tq * 512:(tq + 1) * 512])
                        fev[pp] = emit_ew(p, "v", lambda E: E.tensor_tensor(
                            fout[:, :], gg[:, :], xres2[:, :], ALU.add),
                            dep=[sev2, xld])
                        if li == nlayers - 1:
                            p.wait("y", fev[1 - pp])
                            xld2 = p.dma("y", xres3[:, :],
                                         io["res_in"][m * 128:(m + 1) * 128,
                                                      tq * 512:(tq + 1) * 512])
                            fev[pp] = emit_ew(p, "v", lambda E: E.tensor_tensor(
                                fout[:, :], fout[:, :], xres3[:, :], ALU.add),
                                dep=[xld2])
                        p.wait("y", [fev[pp], fst])
                        fst = p.dma("y", xout[m * 128:(m + 1) * 128,
                                             tq * 512:(tq + 1) * 512], fout[:, :])
                        p.wait("v", fst)
                        p.wait("s", fst)
    return None


def build_l1ffn_standalone(ncores=1, core_id=0, nlayers=6):
    nc = bass.Bass("TRN2", target_bir_lowering=False, debug=False)
    io = declare_io(nc, L1FFN_IO)
    io["res_in"] = nc.dram_tensor("res_in", [D, TC], F32, kind="ExternalInput")
    io["res_out"] = nc.dram_tensor("res_out", [D, TC], F32, kind="ExternalOutput")
    p = Prog(nc)
    consts = setup_consts(p)
    block_l1ffn(p, io, consts, ncores, core_id, nlayers)
    p.build()
    return nc



DI = 2048
DS = 16
NM = DI // 128
XT_EXT = 2176
GRID = 8

L0ATTN_IO = {
    "l0a_xext": [D, XT_EXT],
    "l0a_inwT": [D, 2 * DI],
    "l0a_outwT": [DI, D],
    "l0a_xpjTf": [DI, 96], "l0a_xpjTb": [DI, 96],
    "l0a_dtwTf": [64, DI], "l0a_dtwTb": [64, DI],
    "l0a_convf": [128, NM * 4], "l0a_convb": [128, NM * 4],
    "l0a_cbf": [128, NM], "l0a_cbb": [128, NM],
    "l0a_dtbf": [128, NM], "l0a_dtbb": [128, NM],
    "l0a_Alogf": [128, NM * DS], "l0a_Alogb": [128, NM * DS],
    "l0a_Df": [128, NM], "l0a_Db": [128, NM],
    "l0a_ln1g": [128, 8], "l0a_ln1b": [128, 8],
    "l0a_lng": [128, 8], "l0a_lnb": [128, 8],
    "l0a_kanc": [128, 8 * GRID * D], "l0a_kans": [128, 8 * GRID * D],
    "l0a_mseq": [128, 16],
    "l0a_hm": [128, 2],
}


def host_prep_l0attn(inp, core_id, x_full, ncores=8):
    f32 = np.float32
    out = {}
    hm = np.ones((128, 2), f32)
    if core_id == 0:
        hm[:, 0] = 0.0
    if core_id == ncores - 1:
        hm[:, 1] = 0.0
    out["l0a_hm"] = hm
    xe = np.zeros((XT_EXT, D), f32)
    t0 = core_id * TC
    lo, hi = max(0, t0 - 3), min(L, t0 + TC + 3)
    xe[3 - (t0 - lo):3 + (hi - t0)] = x_full[lo:hi]
    out["l0a_xext"] = np.ascontiguousarray(xe.T)
    out["l0a_inwT"] = np.ascontiguousarray(inp["L0_attn_in_proj_w"].T.astype(f32))
    out["l0a_outwT"] = np.ascontiguousarray(inp["L0_attn_out_proj_w"].T.astype(f32))
    for (suf, S) in (("f", "_f"), ("b", "_b")):
        out[f"l0a_xpjT{suf}"] = np.ascontiguousarray(
            inp[f"L0_attn_x_proj_w{S}"].T.astype(f32))
        out[f"l0a_dtwT{suf}"] = np.ascontiguousarray(
            inp[f"L0_attn_dt_w{S}"].T.astype(f32))
        out[f"l0a_conv{suf}"] = np.ascontiguousarray(
            inp[f"L0_attn_conv_w{S}"].astype(f32).reshape(NM, 128, 4)
            .transpose(1, 0, 2).reshape(128, NM * 4))
        out[f"l0a_cb{suf}"] = np.ascontiguousarray(
            inp[f"L0_attn_conv_b{S}"].astype(f32).reshape(NM, 128).T)
        out[f"l0a_dtb{suf}"] = np.ascontiguousarray(
            inp[f"L0_attn_dt_b{S}"].astype(f32).reshape(NM, 128).T)
        out[f"l0a_Alog{suf}"] = np.ascontiguousarray(
            inp[f"L0_attn_A_log{S}"].astype(f32).reshape(NM, 128, DS)
            .transpose(1, 0, 2).reshape(128, NM * DS))
        out[f"l0a_D{suf}"] = np.ascontiguousarray(
            inp[f"L0_attn_D{S}"].astype(f32).reshape(NM, 128).T)
    out["l0a_ln1g"] = np.ascontiguousarray(
        inp["L0_attn_ln1_g"].astype(f32).reshape(8, 128).T)
    out["l0a_ln1b"] = np.ascontiguousarray(
        inp["L0_attn_ln1_b"].astype(f32).reshape(8, 128).T)
    out["l0a_lng"] = np.ascontiguousarray(
        inp["L0_attn_ln_g"].astype(f32).reshape(8, 128).T)
    out["l0a_lnb"] = np.ascontiguousarray(
        inp["L0_attn_ln_b"].astype(f32).reshape(8, 128).T)
    coef = inp["L0_attn_kan_coef"].astype(f32)
    for c_, nm_ in ((coef[0], "l0a_kanc"), (coef[1], "l0a_kans")):
        t = c_.reshape(8, 128, 8, 128, GRID).transpose(3, 0, 4, 2, 1)
        out[nm_] = np.ascontiguousarray(t.reshape(128, 8 * GRID * 8 * 128))
    ms = np.zeros((128, 16), f32)
    for cc in range(8):
        ms[:, cc] = 1.0 if cc < core_id else 0.0
        ms[:, 8 + cc] = 1.0 if cc > core_id else 0.0
    out["l0a_mseq"] = ms
    return out


def block_l0attn(p, io, consts, ncores, core_id=0):
    nc = p.nc
    ones, ones_row, ones_row_bf = consts["ones"], consts["ones_row"], consts["ones_row_bf"]
    MAG = MAGIC
    inv2pi = 1.0 / (2 * np.pi)
    c1w = float(np.float32(6.28125))
    c2w = float(np.float32(2 * np.pi - 6.28125))
    with p.phase("l0attn") as ctx:
        aspace = "Shared" if ncores > 4 else "Local"
        xcf_d = nc.dram_tensor(f"l0a_xcf_{p.uid()}", [DI, TC], BF16)
        xcb_d = nc.dram_tensor(f"l0a_xcb_{p.uid()}", [DI, TC], BF16)
        sz_d = nc.dram_tensor(f"l0a_sz_{p.uid()}", [DI, TC], BF16)
        dtf_d = nc.dram_tensor(f"l0a_dtf_{p.uid()}", [DI, TC], BF16)
        dtb_d = nc.dram_tensor(f"l0a_dtb_{p.uid()}", [DI, TC], BF16)
        y1f_d = nc.dram_tensor(f"l0a_y1f_{p.uid()}", [DI, TC], F32)
        y1b_d = nc.dram_tensor(f"l0a_y1b_{p.uid()}", [DI, TC], F32)
        yg_d = nc.dram_tensor(f"l0a_yg_{p.uid()}", [2 * DI, TC], BF16)
        st_loc = nc.dram_tensor(f"l0a_stl_{p.uid()}", [128, 1024], F32)
        st_all = nc.dram_tensor(f"l0a_sta_{p.uid()}", [ncores * 128, 1024], F32,
                                addr_space=aspace)
        base_d = nc.dram_tensor(f"l0a_base_{p.uid()}", [D, TC], F32)
        nc._dbg_drams = {"xcf": xcf_d, "xcb": xcb_d, "sz": sz_d, "dtf": dtf_d,
                         "dtb": dtb_d, "y1f": y1f_d, "y1b": y1b_d, "yg": yg_d,
                         "base": base_d, "st": st_loc}
        selarr = np.zeros((96, 32 * 128), BF16NP)
        for s_ in range(32):
            selarr[64 + s_, s_ * 128:(s_ + 1) * 128] = 1
        seld = p.const("bcsel", selarr)
        sel = p.sb("bcsel", [96, 32 * 128], BF16)
        p.dma("y", sel[:, :], seld[:])
        xn_ctx = ExitStack()
        xn = p.sbl("l0a_xn", 8, XT_EXT, BF16, ctx=xn_ctx)
        with p.phase("ph0") as c0:
            x32, xtok = load_tiles(p, io["l0a_xext"], D, XT_EXT, F32,
                                   name="l0a_x", ctx=c0)
            t1 = p.sbl("l0a_t1", 8, XT_EXT, F32, ctx=c0)
            for e in ("t", "s", "v"):
                p.wait(e, xtok)
            emit_ln_feat(p, x32, t1, ones=ones, ones_row=ones_row, eps=1e-5,
                         TCH=544)
            bstage = p.sb("l0a_bst", [128, TC])
            bst = None
            for k in range(8):
                bv = emit_ew(p, "v", lambda E, k=k: E.tensor_tensor(
                    bstage[:, :], x32[k][:, 3:3 + TC], t1[k][:, 3:3 + TC],
                    ALU.add), dep=bst)
                p.wait("y", bv)
                bst = p.dma("y", base_d[k * 128:(k + 1) * 128, :], bstage[:, :])
            p.wait("v", bst)
            lnp = p.sb("l0a_lnp", [128, 16])
            lt = [p.dma("y", lnp[:, 0:8], io["l0a_ln1g"][:, :]),
                  p.dma("y", lnp[:, 8:16], io["l0a_ln1b"][:, :])]
            p.wait("v", lt)
            emit_ln_feat(p, t1, xn, gamma=[lnp[:, i:i + 1] for i in range(8)],
                         beta=[lnp[:, 8 + i:9 + i] for i in range(8)],
                         eps=1e-12, ones=ones, ones_row=ones_row, TCH=544)
            hm = p.sb("l0a_hmv", [128, 2])
            hmb = p.sb("l0a_hmb", [128, 2], BF16)
            hmt = p.dma("y", hm[:, :], io["l0a_hm"][:, :])
            emit_ew(p, "v", lambda E: E.tensor_copy(hmb[:, :], hm[:, :]),
                    dep=hmt)
            for k in range(8):
                emit_ew(p, "v", lambda E, k=k: E.tensor_tensor(
                    xn[k][:, 0:3], xn[k][:, 0:3],
                    hmb[:, 0:1].broadcast_to((128, 3)), ALU.mult))
                emit_ew(p, "v", lambda E, k=k: E.tensor_tensor(
                    xn[k][:, 3 + TC:6 + TC], xn[k][:, 3 + TC:6 + TC],
                    hmb[:, 1:2].broadcast_to((128, 3)), ALU.mult))
        with p.phase("ph1") as c1:
            wt = p.sbl("l0a_wt", 8, 2 * DI, BF16, ctx=c1)
            wtok = [p.dma("g", wt[k][:, :], io["l0a_inwT"][k * 128:(k + 1) * 128, :])
                    for k in range(8)]
            cw = p.sb("l0a_cw", [128, 2 * NM * 4])
            cb = p.sb("l0a_cb", [128, 2 * NM])
            ctok = [p.dma("y", cw[:, 0:NM * 4], io["l0a_convf"][:, :]),
                    p.dma("y", cw[:, NM * 4:], io["l0a_convb"][:, :]),
                    p.dma("y", cb[:, 0:NM], io["l0a_cbf"][:, :]),
                    p.dma("y", cb[:, NM:], io["l0a_cbb"][:, :])]
            psx = [p.ps_("l0a_px0", [128, 512], F32, ctx=c1),
                   p.ps_("l0a_px1", [128, 512], F32, ctx=c1)]
            xcp = p.sb("l0a_xcp", [128, XT_EXT])
            acc = p.sb("l0a_acc", [128, TC])
            stg = p.sb("l0a_stg", [128, TC], BF16)
            p.wait("t", wtok + ctok)
            p.wait("s", ctok)
            p.wait("v", ctok)
            ev = [None, None]
            dst = None
            for m in range(2 * NM):
                width = XT_EXT if m < NM else TC
                cofs = 3 if m >= NM else 0
                nch = cdiv(width, 512)
                for ci in range(nch):
                    n0 = ci * 512
                    nsz = min(512, width - n0)
                    pp = ci % 2
                    p.wait("t", ev[pp])
                    for k in range(8):
                        def thunk(E, m=m, k=k, n0=n0, nsz=nsz, pp=pp, cofs=cofs):
                            return E.matmul(psx[pp][:, 0:nsz],
                                            wt[k][:, m * 128:(m + 1) * 128],
                                            xn[k][:, cofs + n0:cofs + n0 + nsz],
                                            start=(k == 0), stop=(k == 7))
                        tok = p.do("t", thunk, tick=(k == 7))
                    if m < NM:
                        ev[pp] = emit_ew(p, "s", lambda E, n0=n0, nsz=nsz, pp=pp:
                                         E.activation(xcp[:, n0:n0 + nsz],
                                                      psx[pp][:, 0:nsz],
                                                      AF.Identity), dep=tok)
                    else:
                        ev[pp] = emit_ew(p, "s", lambda E, n0=n0, nsz=nsz, pp=pp:
                                         E.activation(stg[:, n0:n0 + nsz],
                                                      psx[pp][:, 0:nsz],
                                                      AF.Silu), dep=tok)
                if m >= NM:
                    p.wait("y", [ev[0], ev[1], dst])
                    dst = p.dma("y", sz_d[(m - NM) * 128:(m - NM + 1) * 128, :],
                                stg[:, :])
                    p.wait("s", dst)
                    continue
                cv = emit_ew(p, "v", lambda E, m=m: E.tensor_scalar(
                    acc[:, :], xcp[:, 0:TC], cw[:, m * 4:m * 4 + 1], None,
                    ALU.mult), dep=[ev[0], ev[1]])
                for k in range(1, 4):
                    cv = emit_ew(p, "v", lambda E, m=m, k=k: E.scalar_tensor_tensor(
                        acc[:, :], xcp[:, k:k + TC], cw[:, m * 4 + k:m * 4 + k + 1],
                        acc[:, :], ALU.mult, ALU.add))
                sv = emit_ew(p, "s", lambda E, m=m: E.activation(
                    stg[:, :], acc[:, :], AF.Silu, bias=cb[:, m:m + 1]), dep=cv)
                p.wait("y", [sv, dst])
                dst = p.dma("y", xcf_d[m * 128:(m + 1) * 128, :], stg[:, :])
                p.wait("v", dst)
                cv = emit_ew(p, "v", lambda E, m=m: E.tensor_scalar(
                    acc[:, :], xcp[:, 2053:5:-1], cw[:, NM * 4 + m * 4:NM * 4 + m * 4 + 1],
                    None, ALU.mult))
                for k in range(1, 4):
                    cv = emit_ew(p, "v", lambda E, m=m, k=k: E.scalar_tensor_tensor(
                        acc[:, :], xcp[:, 2053 - k:5 - k:-1],
                        cw[:, NM * 4 + m * 4 + k:NM * 4 + m * 4 + k + 1],
                        acc[:, :], ALU.mult, ALU.add))
                sv = emit_ew(p, "s", lambda E, m=m: E.activation(
                    stg[:, :], acc[:, :], AF.Silu, bias=cb[:, NM + m:NM + m + 1]),
                    dep=cv)
                p.wait("y", [sv, dst])
                dst = p.dma("y", xcb_d[m * 128:(m + 1) * 128, :], stg[:, :])
                p.wait("s", dst)
                p.wait("v", dst)
        p.barrier()
        xn_ctx.close()
        bcmat = p.sb("l0a_bc", [96, 2 * TC], BF16, ctx=ctx)
        with p.phase("ph2") as c2:
            psd = p.ps_("l0a_psd", [96, TC], F32, ctx=c2)
            psm = [p.ps_("l0a_pm0", [128, 512], F32, ctx=c2),
                   p.ps_("l0a_pm1", [128, 512], F32, ctx=c2)]
            xpj = p.sbl("l0a_xpj", NM, 96, BF16, ctx=c2)
            dtw = p.sb("l0a_dtw", [64, DI], BF16)
            dtbc = p.sb("l0a_dtbc", [128, 2 * NM])
            xck = [p.sb("l0a_xck0", [128, TC], BF16), p.sb("l0a_xck1", [128, TC], BF16)]
            xdb = p.sb("l0a_xdb", [96, TC])
            spa = p.sb("l0a_spa", [128, TC])
            spb = p.sb("l0a_spb", [128, TC])
            stg2 = p.sb("l0a_stg2", [128, TC], BF16)
            dcons = [None, None]
            dprev = []
            for di_, (xc_d, dt_d, xpj_io, dtw_io, dtb_c) in enumerate(
                    ((xcf_d, dtf_d, "l0a_xpjTf", "l0a_dtwTf", 0),
                     (xcb_d, dtb_d, "l0a_xpjTb", "l0a_dtwTb", NM))):
                p.wait("g", [dcons[0], dcons[1]] + dprev)
                xptok = [p.dma("g", xpj[k][:, :],
                               io[xpj_io][k * 128:(k + 1) * 128, :])
                         for k in range(NM)]
                wtok2 = [p.dma("g", dtw[:, :], io[dtw_io][:, :])]
                btok2 = [p.dma("y", dtbc[:, di_ * NM:(di_ + 1) * NM],
                               io["l0a_dtbf" if di_ == 0 else "l0a_dtbb"][:, :])]
                p.wait("t", xptok)
                xld = [None, None]
                cons = [dcons[0], dcons[1]]
                for k in range(NM):
                    pp = k % 2
                    p.wait("y", cons[pp])
                    xld[pp] = p.dma("y", xck[pp][:, :],
                                    xc_d[k * 128:(k + 1) * 128, :])
                    p.wait("t", xld[pp])
                    for n0 in range(0, TC, 512):
                        def thunk(E, k=k, n0=n0, pp=pp):
                            return E.matmul(psd[:, n0:n0 + 512], xpj[k][:, :],
                                            xck[pp][:, n0:n0 + 512],
                                            start=(k == 0), stop=(k == NM - 1))
                        tok = p.do("t", thunk, tick=(n0 + 512 >= TC))
                    cons[pp] = tok
                xdt = emit_ew(p, "v", lambda E: E.tensor_copy(xdb[:, :], psd[:, :]),
                              dep=tok)
                bct = emit_ew(p, "v", lambda E, di_=di_: E.tensor_copy(
                    bcmat[:, di_ * TC:(di_ + 1) * TC], xdb[:, :]))
                p.wait("t", [xdt, bct] + wtok2)
                p.wait("s", btok2)
                dst2 = None
                rt = [None, None]
                for m in range(NM):
                    for ci, n0 in enumerate(range(0, TC, 512)):
                        pp = ci % 2
                        p.wait("t", rt[pp])
                        def thunk(E, m=m, n0=n0, pp=pp, di_=di_):
                            return E.matmul(psm[pp][:, :], dtw[:, m * 128:(m + 1) * 128],
                                            bcmat[0:64, di_ * TC + n0:di_ * TC + n0 + 512],
                                            start=True, stop=True)
                        tok = p.do("t", thunk, tick=True)
                        a1 = emit_ew(p, "s", lambda E, m=m, n0=n0, pp=pp, di_=di_:
                                     E.activation(spa[:, n0:n0 + 512], psm[pp][:, :],
                                                  AF.Abs,
                                                  bias=dtbc[:, di_ * NM + m:di_ * NM + m + 1]),
                                     dep=tok)
                        r1 = emit_ew(p, "s", lambda E, m=m, n0=n0, pp=pp, di_=di_:
                                     E.activation(spb[:, n0:n0 + 512], psm[pp][:, :],
                                                  AF.Relu,
                                                  bias=dtbc[:, di_ * NM + m:di_ * NM + m + 1]))
                        rt[pp] = r1
                    e1 = emit_ew(p, "s", lambda E: E.activation(
                        spa[:, :], spa[:, :], AF.Exp, scale=-1.0), sedge=True)
                    l1 = emit_ew(p, "s", lambda E: E.activation(
                        spa[:, :], spa[:, :], AF.Ln, bias=1.0), sedge=True)
                    dt_t = emit_ew(p, "v", lambda E: E.tensor_tensor(
                        stg2[:, :], spa[:, :], spb[:, :], ALU.add), dep=l1)
                    p.wait("y", dt_t)
                    dst2 = p.dma("y", dt_d[m * 128:(m + 1) * 128, :], stg2[:, :])
                    p.wait("s", dst2)
                    p.wait("v", dst2)
                dcons = [cons[0], cons[1]]
                dprev = [rt[0], rt[1]]
        _l0a_scan_and_gate(p, io, consts, ncores, core_id,
                           (xcf_d, xcb_d, sz_d, dtf_d, dtb_d, y1f_d, y1b_d,
                            yg_d, st_loc, st_all, bcmat, sel), ctx)
        _l0a_out_kan(p, io, consts, ncores, core_id, (yg_d, base_d))
    return None


def _l0a_scan_and_gate(p, io, consts, ncores, core_id, drams, ctx):
    nc = p.nc
    ones, ones_row = consts["ones"], consts["ones_row"]
    (xcf_d, xcb_d, sz_d, dtf_d, dtb_d, y1f_d, y1b_d, yg_d, st_loc, st_all,
     bcmat, sel) = drams
    onesbf_d = p.const("onesbf2k", np.ones((128, TC), BF16NP))
    Acol = p.sb("l0a_A", [128, 2 * NM * DS], ctx=ctx)
    at = [p.dma("y", Acol[:, 0:NM * DS], io["l0a_Alogf"][:, :]),
          p.dma("y", Acol[:, NM * DS:], io["l0a_Alogb"][:, :])]
    mseq = p.sb("l0a_mseq", [128, 32], ctx=ctx)
    mt = p.dma("y", mseq[:, 0:16], io["l0a_mseq"][:, :])
    iv = emit_ew(p, "v", lambda E: E.tensor_scalar(
        mseq[:, 16:32], mseq[:, 0:16], -1.0, 1.0, ALU.mult, ALU.add), dep=mt)
    p.wait("s", at)
    ax = emit_ew(p, "s", lambda E: E.activation(Acol[:, :], Acol[:, :], AF.Exp))
    av = emit_ew(p, "v", lambda E: E.tensor_scalar_mul(Acol[:, :], Acol[:, :],
                                                       -1.0), dep=ax)
    Fst = p.sb("l0a_F", [128, 512], ctx=ctx)
    Pst = p.sb("l0a_P", [128, 512], ctx=ctx)
    Hst = p.sb("l0a_H", [128, 512], ctx=ctx)
    p.barrier()
    dirs = ((0, xcf_d, dtf_d, y1f_d, 0), (1, xcb_d, dtb_d, y1b_d, NM * DS))

    def bc_bcast(psb2, be, ce, s_, di_, dep):
        p.wait("t", dep)
        for (dst, srow) in ((be, s_), (ce, DS + s_)):
            for n0 in range(0, TC, 512):
                def thunk(E, srow=srow, n0=n0, di_=di_):
                    return E.matmul(psb2[:, n0:n0 + 512],
                                    sel[:, srow * 128:(srow + 1) * 128],
                                    bcmat[:, di_ * TC + n0:di_ * TC + n0 + 512],
                                    start=True, stop=True)
                tok = p.do("t", thunk, tick=(n0 + 512 >= TC))
            ev = emit_ew(p, "v", lambda E, dst=dst: E.tensor_copy(dst[:, :],
                                                                  psb2[:, :]),
                         dep=tok)
            p.wait("t", ev)
        return ev

    for di_, xc_d, dt_d, y1_d, aofs in dirs:
        for half in range(2):
            with p.phase(f"swA{di_}{half}") as ch:
                m0 = half * (NM // 2)
                dt = p.sbl("sa_dt", 8, TC, BF16)
                u = p.sbl("sa_u", 8, TC, BF16)
                yt = p.sbl("sa_y", 8, TC, F32)
                be = p.sb("sa_be", [128, TC], BF16)
                ce = p.sb("sa_ce", [128, TC], BF16)
                dapair = [p.sb("sa_da0", [128, TC], BF16),
                          p.sb("sa_da1", [128, TC], BF16)]
                dbx = p.sb("sa_dbx", [128, TC], BF16)
                hh = p.sb("sa_h", [128, TC], BF16)
                prod = dbx
                psb2 = p.ps_("sa_ps", [128, TC], F32)
                ldt = []
                xcl = None
                for j in range(8):
                    ldt.append(p.dma("y", dt[j][:, :],
                                     dt_d[(m0 + j) * 128:(m0 + j + 1) * 128, :]))
                for j in range(8):
                    p.wait("y", xcl)
                    xt2 = p.dma("y", hh[:, :],
                                xc_d[(m0 + j) * 128:(m0 + j + 1) * 128, :])
                    xcl = emit_ew(p, "v", lambda E, j=j, u=u, dt=dt, hh=hh:
                                  E.tensor_tensor(
                                      u[j][:, :], dt[j][:, :], hh[:, :], ALU.mult),
                                  dep=[xt2, ldt[j]])
                    emit_ew(p, "v", lambda E, j=j, yt=yt: E.memset(yt[j][:, :], 0.0))
                uv = ("pv", p.cnt["pv"])
                bev = [uv]
                sc = None
                scprev = [None, None]
                for s_ in range(DS):
                    ev = bc_bcast(psb2, be, ce, s_, di_, bev)
                    bev = []
                    for j in range(8):
                        m = m0 + j
                        jp = j % 2
                        da = dapair[jp]
                        acl = Acol[:, aofs + m * DS + s_:aofs + m * DS + s_ + 1]
                        ad = emit_ew(p, "s", lambda E, j=j, acl=acl, da=da, dt=dt:
                                     E.activation(
                                         da[:, :], dt[j][:, :], AF.Exp, scale=acl),
                                     dep=[av, ev, scprev[jp]])
                        dbv = emit_ew(p, "v", lambda E, j=j, dbx=dbx, u=u, be=be:
                                      E.tensor_tensor(
                                          dbx[:, :], u[j][:, :], be[:, :],
                                          ALU.mult), dep=ad)
                        sc = emit_ew(p, "v", lambda E, hh=hh, da=da, dbx=dbx:
                                     E.tensor_tensor_scan(
                                         hh[:, :], da[:, :], dbx[:, :], 0.0,
                                         ALU.mult, ALU.add))
                        scprev[jp] = sc
                        fc = emit_ew(p, "v", lambda E, m=m, s_=s_, di_=di_, hh=hh:
                                     E.tensor_copy(
                                         Fst[:, di_ * 256 + m * DS + s_:
                                             di_ * 256 + m * DS + s_ + 1],
                                         hh[:, TC - 1:TC]))
                        pv_ = emit_ew(p, "v", lambda E, prod=prod, hh=hh, ce=ce:
                                      E.tensor_tensor(
                                          prod[:, :], hh[:, :], ce[:, :],
                                          ALU.mult))
                        ya = emit_ew(p, "v", lambda E, j=j, yt=yt, prod=prod:
                                     E.tensor_tensor(
                                         yt[j][:, :], yt[j][:, :], prod[:, :],
                                         ALU.add))
                        bev.append(ya)
                        bev.append(ad)
                sdc = p.sb("sa_sd", [128, 2])
                for j in range(8):
                    m = m0 + j
                    sdv = emit_ew(p, "v", lambda E, j=j, sdc=sdc, dt=dt:
                                  E.tensor_reduce(
                                      sdc[:, 0:1], dt[j][:, :], AX.X, ALU.add))
                    for s_ in range(DS):
                        acl = Acol[:, aofs + m * DS + s_:aofs + m * DS + s_ + 1]
                        tsv = emit_ew(p, "v", lambda E, acl=acl, sdc=sdc:
                                      E.tensor_scalar(
                                          sdc[:, 1:2], sdc[:, 0:1], acl, None,
                                          ALU.mult), sedge=True)
                        emit_ew(p, "s", lambda E, m=m, s_=s_, di_=di_, sdc=sdc:
                                E.activation(
                                    Pst[:, di_ * 256 + m * DS + s_:
                                        di_ * 256 + m * DS + s_ + 1],
                                    sdc[:, 1:2], AF.Exp), dep=tsv)
                        p.wait("v", ("ps", p.cnt["ps"]))
                st = None
                for j in range(8):
                    p.wait("y", ("pv", p.cnt["pv"]))
                    st = p.dma("y", y1_d[(m0 + j) * 128:(m0 + j + 1) * 128, :],
                               yt[j][:, :])
                p.wait("v", st)
                p.wait("s", st)
    with p.phase("l0a_comb") as cc0:
        stg = p.sb("cb_stg", [128, 1024])
        cpv = emit_ew(p, "v", lambda E, stg=stg: E.tensor_copy(stg[:, 0:512],
                                                               Fst[:, :]))
        cpv = emit_ew(p, "v", lambda E, stg=stg: E.tensor_copy(stg[:, 512:1024],
                                                               Pst[:, :]))
        p.wait("y", cpv)
        cst = p.dma("y", st_loc[:, :], stg[:, :])
        p.wait("g", cst)
        cc = p.collective("AllGather", ALU.bypass, [st_loc[:, :]], [st_all[:, :]],
                          ngroups=ncores)
        p.wait("y", cc)
        blk = p.sb("cb_blk", [128, 1024])
        tmp = p.sb("cb_tmp", [128, 256])
        tmp2 = p.sb("cb_tmp2", [128, 256])
        hz = emit_ew(p, "v", lambda E: E.memset(Hst[:, :], 0.0))
        for di_ in range(2):
            ccs = range(ncores) if di_ == 0 else range(ncores - 1, -1, -1)
            for cc_ in ccs:
                bt = p.dma("y", blk[:, :], st_all[cc_ * 128:(cc_ + 1) * 128, :])
                mcol = mseq[:, di_ * 8 + cc_:di_ * 8 + cc_ + 1]
                imcol = mseq[:, 16 + di_ * 8 + cc_:16 + di_ * 8 + cc_ + 1]
                pm = emit_ew(p, "v", lambda E, di_=di_, mcol=mcol, imcol=imcol:
                             E.scalar_tensor_tensor(
                                 tmp[:, :], blk[:, 512 + di_ * 256:768 + di_ * 256],
                                 mcol, imcol.broadcast_to((128, 256)),
                                 ALU.mult, ALU.add), dep=[bt, iv])
                emit_ew(p, "v", lambda E, di_=di_: E.tensor_tensor(
                    tmp2[:, :], Hst[:, 0:256] if di_ == 0 else Hst[:, 256:512],
                    tmp[:, :], ALU.mult))
                emit_ew(p, "v", lambda E, di_=di_, mcol=mcol: E.tensor_scalar(
                    tmp[:, :], blk[:, di_ * 256:(di_ + 1) * 256], mcol, None,
                    ALU.mult))
                hv = emit_ew(p, "v", lambda E, di_=di_: E.tensor_tensor(
                    Hst[:, di_ * 256:(di_ + 1) * 256], tmp2[:, :], tmp[:, :],
                    ALU.add))
                p.wait("y", hv)
    Dcol = p.sb("l0a_Dc", [128, 2 * NM], ctx=ctx)
    dtk = [p.dma("y", Dcol[:, 0:NM], io["l0a_Df"][:, :]),
           p.dma("y", Dcol[:, NM:], io["l0a_Db"][:, :])]
    for di_, xc_d, dt_d, y1_d, aofs in dirs:
        for half in range(2):
            with p.phase(f"swB{di_}{half}") as ch:
                m0 = half * (NM // 2)
                dt = p.sbl("sb_dt", 8, TC, BF16)
                yt = p.sbl("sb_y", 8, TC, F32)
                ct = p.sbl("sb_ct", 8, TC, F32)
                be = p.sb("sb_be", [128, TC], BF16)
                ce = p.sb("sb_ce", [128, TC], BF16)
                dec = p.sb("sb_dec", [128, TC], BF16)
                hc = p.sb("sb_hc", [128, TC], BF16)
                prod = dec
                stg = ce
                psb2 = p.ps_("sb_ps", [128, TC], F32)
                onesbf = be
                ot1 = p.dma("y", onesbf[:, :], onesbf_d[:])
                ldt = []
                for j in range(8):
                    ldt.append(p.dma("y", dt[j][:, :],
                                     dt_d[(m0 + j) * 128:(m0 + j + 1) * 128, :]))
                    ldt.append(p.dma("y", yt[j][:, :],
                                     y1_d[(m0 + j) * 128:(m0 + j + 1) * 128, :]))
                p.wait("v", ldt + [ot1])
                for j in range(8):
                    emit_ew(p, "v", lambda E, j=j, ct=ct, onesbf=onesbf, dt=dt:
                            E.tensor_tensor_scan(
                                ct[j][:, :], onesbf[:, :], dt[j][:, :], 0.0,
                                ALU.mult, ALU.add))
                ctv = ("pv", p.cnt["pv"])
                bev = [ctv]
                ya = None
                for s_ in range(DS):
                    ev = bc_bcast(psb2, be, ce, s_, di_, bev)
                    bev = []
                    for j in range(8):
                        m = m0 + j
                        acl = Acol[:, aofs + m * DS + s_:aofs + m * DS + s_ + 1]
                        hcol = Hst[:, di_ * 256 + m * DS + s_:
                                   di_ * 256 + m * DS + s_ + 1]
                        dv = emit_ew(p, "s", lambda E, j=j, acl=acl, dec=dec, ct=ct:
                                     E.activation(
                                         dec[:, :], ct[j][:, :], AF.Exp, scale=acl),
                                     dep=[av, ev, ctv, ya])
                        pv_ = emit_ew(p, "v", lambda E, hcol=hcol, hc=hc, dec=dec,
                                      ce=ce: E.scalar_tensor_tensor(
                                          hc[:, :], dec[:, :], hcol, ce[:, :],
                                          ALU.mult, ALU.mult), dep=dv)
                        ya = emit_ew(p, "v", lambda E, j=j, yt=yt, hc=hc:
                                     E.tensor_tensor(
                                         yt[j][:, :], yt[j][:, :], hc[:, :],
                                         ALU.add))
                        bev.append(ya)
                        bev.append(dv)
                p.wait("y", bev)
                st = None
                g2 = None
                for j in range(8):
                    m = m0 + j
                    dcl = Dcol[:, di_ * NM + m:di_ * NM + m + 1]
                    p.wait("y", g2)
                    xld = p.dma("y", dec[:, :],
                                xc_d[(m0 + j) * 128:(m0 + j + 1) * 128, :])
                    zld = p.dma("y", hc[:, :],
                                sz_d[(m0 + j) * 128:(m0 + j + 1) * 128, :])
                    g1 = emit_ew(p, "v", lambda E, j=j, dcl=dcl, be=be, dec=dec,
                                 yt=yt: E.scalar_tensor_tensor(
                                     be[:, :], dec[:, :], dcl, yt[j][:, :],
                                     ALU.mult, ALU.add),
                                 dep=list(dtk) + [xld, zld])
                    g2 = emit_ew(p, "v", lambda E, j=j, stg=stg, be=be, hc=hc,
                                 di_=di_: E.tensor_tensor(
                                     stg[:, :], be[:, :],
                                     hc[:, ::-1] if di_ == 1 else hc[:, :],
                                     ALU.mult))
                    p.wait("y", [g2, st])
                    st = p.dma("y", yg_d[(di_ * NM + m) * 128:(di_ * NM + m + 1) * 128, :],
                               stg[:, :])
                    p.wait("v", st)
    return None


def _l0a_out_kan(p, io, consts, ncores, core_id, drams):
    nc = p.nc
    ones, ones_row = consts["ones"], consts["ones_row"]
    (yg_d, base_d) = drams
    MAG = MAGIC
    inv2pi = float(1.0 / (2 * np.pi))
    c1w = float(np.float32(6.28125))
    c2w = float(np.float32(2 * np.pi - 6.28125))
    with p.phase("l0a_out") as c4:
        vT = p.sbl("l0a_vT", 8, TC, F32, ctx=c4)
        with p.phase("l0a_op") as c5:
            ot = p.sbl("l0a_ot", 8, TC, F32)
            cop = ExitStack()
            p._ctx.append(cop)
            ow = p.sbl("l0a_ow", NM, D, BF16)
            owt = [p.dma("g", ow[k][:, :], io["l0a_outwT"][k * 128:(k + 1) * 128, :])
                   for k in range(NM)]
            ygs = p.sbl("l0a_ygs", NM, 512, BF16)
            psu = p.ps_("l0a_pu", [128, 8 * 512], F32)
            p.wait("t", owt)
            ev = []
            for tq in range(4):
                mmtok = None
                for half in range(2):
                    ygld = []
                    p.wait("y", ev + ([mmtok] if mmtok else []))
                    for k in range(NM):
                        if half == 0:
                            ygld.append(p.dma("y", ygs[k][:, :],
                                              yg_d[k * 128:(k + 1) * 128,
                                                   tq * 512:(tq + 1) * 512]))
                        else:
                            r0 = TC - (tq + 1) * 512
                            ygld.append(p.dma("y", ygs[k][:, :],
                                              yg_d[(NM + k) * 128:(NM + k + 1) * 128,
                                                   r0:r0 + 512]))
                    p.wait("t", ygld)
                    ev = []
                    for mo in range(8):
                        for k in range(NM):
                            def thunk(E, mo=mo, k=k, half=half):
                                rhs = ygs[k][:, :] if half == 0 else ygs[k][:, ::-1]
                                return E.matmul(psu[:, mo * 512:(mo + 1) * 512],
                                                ow[k][:, mo * 128:(mo + 1) * 128],
                                                rhs, start=(half == 0 and k == 0),
                                                stop=(half == 1 and k == NM - 1))
                            tok = p.do("t", thunk, tick=(k == NM - 1))
                        if half == 1:
                            ev.append(emit_ew(
                                p, "s", lambda E, tq=tq, mo=mo:
                                E.activation(ot[mo][:, tq * 512:(tq + 1) * 512],
                                             psu[:, mo * 512:(mo + 1) * 512],
                                             AF.Identity), dep=tok))
                    mmtok = tok
            p.barrier()
            p._ctx.pop()
            cop.close()
            lnp = p.sb("l0a_lnp2", [128, 16])
            lt = [p.dma("y", lnp[:, 0:8], io["l0a_lng"][:, :]),
                  p.dma("y", lnp[:, 8:16], io["l0a_lnb"][:, :])]
            p.wait("v", lt + ev)
            p.wait("t", ev)
            p.wait("s", ev)
            emit_ln_feat(p, ot, vT, gamma=[lnp[:, i:i + 1] for i in range(8)],
                         beta=[lnp[:, 8 + i:9 + i] for i in range(8)],
                         eps=1e-12, ones=ones, ones_row=ones_row)
        cs = p.sbl("l0a_cs", 8 * GRID, 512, BF16, ctx=c4)
        kslc = [p.sb("l0a_ks0", [128, 64 * 128], BF16, ctx=c4),
                p.sb("l0a_ks1", [128, 64 * 128], BF16, ctx=c4)]
        tbuf = p.sb("l0a_tb", [128, 512], ctx=c4)
        kbuf = p.sb("l0a_kb", [128, 512], ctx=c4)
        abuf = p.sb("l0a_ab", [128, 512], ctx=c4)
        bstage = p.sb("l0a_bst2", [128, 512], ctx=c4)
        psk = p.ps_("l0a_psk", [128, 8 * 512], F32, ctx=c4)
        for tq in range(4):
            p.barrier()
            sin_tok = None
            mm_done = []
            kld = [None, None]
            for trig in range(2):
                p.wait("s", mm_done)
                mm_done = []
                for kk in range(8):
                    for g in range(GRID):
                        gv = float(g + 1)
                        bias = float(np.pi / 2) if trig == 0 else 0.0
                        if trig == 0:
                            emit_ew(p, "v", lambda E, kk=kk, gv=gv, tq=tq:
                                    E.tensor_scalar(
                                        tbuf[:, :],
                                        vT[kk][:, tq * 512:(tq + 1) * 512],
                                        gv * inv2pi, 0.25, ALU.mult, ALU.add),
                                    dep=sin_tok)
                            t1_ = emit_ew(p, "v", lambda E: E.tensor_scalar(
                                tbuf[:, :], tbuf[:, :], MAG, None, ALU.add),
                                sedge=True)
                        else:
                            t1_ = emit_ew(p, "v", lambda E, kk=kk, gv=gv, tq=tq:
                                          E.tensor_scalar(
                                              tbuf[:, :],
                                              vT[kk][:, tq * 512:(tq + 1) * 512],
                                              gv * inv2pi, MAG, ALU.mult,
                                              ALU.add), dep=sin_tok)
                        emit_ew(p, "v", lambda E: E.tensor_scalar(
                            kbuf[:, :], tbuf[:, :], MAG, None, ALU.subtract),
                            sedge=True)
                        emit_ew(p, "v", lambda E, kk=kk, gv=gv, tq=tq: E.tensor_scalar(
                            abuf[:, :], vT[kk][:, tq * 512:(tq + 1) * 512], gv,
                            None, ALU.mult), sedge=True)
                        emit_ew(p, "v", lambda E: E.scalar_tensor_tensor(
                            abuf[:, :], kbuf[:, :], -c1w, abuf[:, :], ALU.mult,
                            ALU.add), sedge=True)
                        emit_ew(p, "v", lambda E: E.scalar_tensor_tensor(
                            abuf[:, :], kbuf[:, :], -c2w, abuf[:, :], ALU.mult,
                            ALU.add), sedge=True)
                        lo, hi = ((-4.7123880, 1.5707957) if trig == 0
                                  else (-3.1415925, 3.1415925))
                        rr = emit_ew(p, "v", lambda E, lo=lo, hi=hi:
                                     E.tensor_scalar(
                                         abuf[:, :], abuf[:, :], lo, hi,
                                         ALU.max, ALU.min), sedge=True)
                        sin_tok = emit_ew(p, "s", lambda E, kk=kk, g=g, bias=bias:
                                          E.activation(
                                              cs[g * 8 + kk][:, :], abuf[:, :],
                                              AF.Sin, bias=bias), dep=rr)
                trig_tok = [("ps", p.cnt["ps"]), ("pv", p.cnt["pv"])]
                kio = io["l0a_kanc"] if trig == 0 else io["l0a_kans"]
                for mo in range(8):
                    pp = mo % 2
                    p.wait("g", kld[pp])
                    kld[pp] = p.dma("g", kslc[pp][:, :],
                                    kio[:, mo * 8192:(mo + 1) * 8192])
                    p.wait("t", [kld[pp]] + trig_tok)
                    for kc in range(64):
                        def thunk(E, tq=tq, trig=trig, mo=mo, pp=pp, kc=kc):
                            return E.matmul(
                                psk[:, mo * 512:(mo + 1) * 512],
                                kslc[pp][:, kc * 128:(kc + 1) * 128],
                                cs[kc][:, :],
                                start=(trig == 0 and kc == 0),
                                stop=(trig == 1 and kc == 63))
                        tok = p.do("t", thunk, tick=(kc == 63))
                    kld[pp] = tok
                    mm_done.append(tok)
            bld = None
            sv = None
            stt = None
            for mo in range(8):
                p.wait("y", sv)
                bld = p.dma("y", bstage[:, :],
                            base_d[mo * 128:(mo + 1) * 128, tq * 512:(tq + 1) * 512])
                sv = emit_ew(p, "v", lambda E, mo=mo: E.tensor_tensor(
                    tbuf[:, :], psk[:, mo * 512:(mo + 1) * 512], bstage[:, :],
                    ALU.add), dep=[bld, ("pt", p.cnt["pt"])])
                p.wait("y", sv)
                stt = p.dma("y", io["res_out"][mo * 128:(mo + 1) * 128,
                                               tq * 512:(tq + 1) * 512], tbuf[:, :])
                p.wait("v", stt)
    return None


def block_l0attn_full(p, io, consts, ncores, core_id=0):
    block_l0attn(p, io, consts, ncores, core_id)


def build_l0attn_standalone(ncores=1, core_id=0):
    nc = bass.Bass("TRN2", target_bir_lowering=False, debug=False)
    io = declare_io(nc, L0ATTN_IO)
    io["res_out"] = nc.dram_tensor("res_out", [D, TC], F32, kind="ExternalOutput")
    p = Prog(nc)
    consts = setup_consts(p)
    block_l0attn(p, io, consts, ncores, core_id)
    p.build()
    return nc


def build_fused(ncores=8, core_id=0):
    nc = bass.Bass("TRN2", target_bir_lowering=False, debug=False)
    io = {}
    for iod in (L0ATTN_IO, L0FFN_IO, L1ATTN_IO, L1FFN_IO):
        io.update(declare_io(nc, iod))
    r1 = nc.dram_tensor("r1_d", [D, TC], F32)
    r2 = nc.dram_tensor("r2_d", [D, TC], F32)
    r3 = nc.dram_tensor("r3_d", [D, TC], F32)
    out = nc.dram_tensor("res_out", [D, TC], F32, kind="ExternalOutput")
    p = Prog(nc)
    consts = setup_consts(p)
    block_l0attn(p, dict(io, res_out=r1), consts, ncores, core_id)
    block_l0ffn(p, dict(io, res_in=r1, res_out=r2), consts)
    block_l1attn(p, dict(io, res_in=r2, res_out=r3), consts, ncores)
    block_l1ffn(p, dict(io, res_in=r3, res_out=out), consts, ncores, core_id, 6)
    p.build()
    return nc


def host_prep_fused(inputs, c, x, ncores=8):
    h = {}
    h.update(host_prep_l0attn(inputs, c, x, ncores=ncores))
    h.update(host_prep_l0ffn(inputs))
    h.update(host_prep_l1attn(inputs))
    h.update(host_prep_l1ffn(inputs, c))
    return h


def kernel(**inputs):
    from concourse.bass_utils import run_bass_kernel_spmd
    NCOR = 8
    cores = list(range(NCOR))
    x = np.ascontiguousarray(
        np.asarray(inputs["L0_attn_x"], np.float32).reshape(L, D))
    nc = build_fused(NCOR)
    hosts = [host_prep_fused(inputs, c, x, ncores=NCOR) for c in cores]
    res = run_bass_kernel_spmd(nc, hosts, core_ids=cores)
    out = np.concatenate(
        [np.asarray(res.results[c]["res_out"], np.float32).T for c in cores], 0)
    return np.ascontiguousarray(out.reshape(1, L, D))


def kernel_unfused(**inputs):
    from concourse.bass_utils import run_bass_kernel_spmd
    NCOR = 8
    cores = list(range(NCOR))
    x = np.ascontiguousarray(
        np.asarray(inputs["L0_attn_x"], np.float32).reshape(L, D))

    nc = build_l0attn_standalone(NCOR)
    hosts = [host_prep_l0attn(inputs, c, x, ncores=NCOR) for c in cores]
    res = run_bass_kernel_spmd(nc, hosts, core_ids=cores)
    rr = [np.ascontiguousarray(np.asarray(res.results[c]["res_out"], np.float32))
          for c in cores]

    nc = build_l0ffn_standalone()
    w = host_prep_l0ffn(inputs)
    hosts = [dict(w, res_in=rr[c]) for c in cores]
    res = run_bass_kernel_spmd(nc, hosts, core_ids=cores)
    rr = [np.ascontiguousarray(np.asarray(res.results[c]["res_out"], np.float32))
          for c in cores]

    nc = build_l1attn_standalone(NCOR)
    w = host_prep_l1attn(inputs)
    hosts = [dict(w, res_in=rr[c]) for c in cores]
    res = run_bass_kernel_spmd(nc, hosts, core_ids=cores)
    rr = [np.ascontiguousarray(np.asarray(res.results[c]["res_out"], np.float32))
          for c in cores]

    nc = build_l1ffn_standalone(NCOR, 0, 6)
    hosts = [dict(host_prep_l1ffn(inputs, c), res_in=rr[c]) for c in cores]
    res = run_bass_kernel_spmd(nc, hosts, core_ids=cores)
    out = np.concatenate(
        [np.asarray(res.results[c]["res_out"], np.float32).T for c in cores], 0)
    return np.ascontiguousarray(out.reshape(1, L, D))


# version 06 (final)
# speedup vs baseline: 1.0226x; 1.0226x over previous
import numpy as np
import ml_dtypes
from contextlib import ExitStack

import concourse.bass as bass
import concourse.mybir as mybir
from concourse.bass_utils import run_bass_kernel_spmd

F32 = mybir.dt.float32
BF16 = mybir.dt.bfloat16
AF = mybir.ActivationFunctionType
ALU = mybir.AluOpType
AX = mybir.AxisListType
BF16NP = ml_dtypes.bfloat16

NCORE = 8
L = 16384
TC = L // NCORE
D = 1024

MAGIC = 12582912.0


def cdiv(a, b):
    return (a + b - 1) // b


class Prog:

    ENG = ("v", "s", "t", "g", "y")

    DMA_POOL = 6

    def __init__(self, nc):
        self.nc = nc
        self.stack = ExitStack()
        self.q = {e: [] for e in self.ENG}
        self.sem = {}
        self.cnt = {}
        names = ["pv", "ps", "pt", "pg", "py", "cc"]
        for base in ("dy", "dg", "dsc"):
            names += [f"{base}{i}" for i in range(self.DMA_POOL)]
        for name in names:
            self.sem[name] = self.stack.enter_context(nc.semaphore(name))
            self.cnt[name] = 0
        self._dmarr = {"y": 0, "g": 0, "s": 0}
        self._bar = dict(self.cnt)
        self._uid = 0
        self._ctx = []

    def uid(self):
        self._uid += 1
        return self._uid

    def _curctx(self, ctx):
        if ctx is not None:
            return ctx
        return self._ctx[-1] if self._ctx else self.stack

    def sb(self, name, shape, dt=F32, ctx=None):
        return self._curctx(ctx).enter_context(
            self.nc.sbuf_tensor(f"{name}_{self.uid()}", shape, dt))

    def sbl(self, name, ntile, free, dt=F32, p=128, ctx=None):
        return [self._curctx(ctx).enter_context(
            self.nc.sbuf_tensor(f"{name}{i}_{self.uid()}", [p, free], dt))
            for i in range(ntile)]

    def ps_(self, name, shape, dt=F32, ctx=None):
        return self._curctx(ctx).enter_context(
            self.nc.psum_tensor(f"{name}_{self.uid()}", shape, dt))

    def const(self, name, arr):
        return self.nc.inline_tensor(np.ascontiguousarray(arr), name=f"{name}_{self.uid()}")

    def do(self, e, fn, tick=False):
        if tick:
            sname = "p" + e
            self.cnt[sname] += 1
            val = self.cnt[sname]
            sem = self.sem[sname]

            def thunk(E, fn=fn, sem=sem):
                fn(E).then_inc(sem, 1)
            self.q[e].append(thunk)
            return (sname, val)
        self.q[e].append(fn)
        return None

    def wait(self, e, tokens):
        if tokens is None:
            return
        if isinstance(tokens, tuple):
            tokens = [tokens]
        merged = {}
        for t in tokens:
            if t is None:
                continue
            s, v = t
            merged[s] = max(merged.get(s, 0), v)
        for s, v in merged.items():
            def thunk(E, s=s, v=v):
                E.wait_ge(self.sem[s], v)
            self.q[e].append(thunk)

    def dma(self, e, out, in_, **kw):
        base = {"y": "dy", "g": "dg", "s": "dsc"}[e]
        i = self._dmarr[e]
        self._dmarr[e] = (i + 1) % self.DMA_POOL
        sname = f"{base}{i}"
        prev = self.cnt[sname]
        if prev > 0:
            self.wait(e, (sname, prev))
        self.cnt[sname] = prev + 16
        val = self.cnt[sname]
        sem = self.sem[sname]

        def thunk(E, out=out, in_=in_, kw=kw):
            E.dma_start(out=out, in_=in_, **kw).then_inc(sem, 16)
        self.q[e].append(thunk)
        return (sname, val)

    def collective(self, kind, op, ins, outs, ngroups=NCORE):
        self.cnt["cc"] += 1
        val = self.cnt["cc"]
        sem = self.sem["cc"]

        def thunk(E, kind=kind, op=op, ins=ins, outs=outs):
            E.collective_compute(
                kind, op, ins=ins, outs=outs,
                replica_groups=[list(range(ngroups))]).then_inc(sem, 1)
        self.q["g"].append(thunk)
        return ("cc", val)

    def barrier(self):
        toks = [(s, v) for s, v in self.cnt.items()
                if v > self._bar.get(s, 0)]
        for e in self.ENG:
            self.wait(e, [t for t in toks if t[0] != "p" + e])
        self._bar = dict(self.cnt)
        return toks

    def phase(self, name=""):
        from contextlib import contextmanager

        @contextmanager
        def _cm():
            ctx = ExitStack()
            self._ctx.append(ctx)
            try:
                yield ctx
            finally:
                self.barrier()
                self._ctx.pop()
                ctx.close()
        return _cm()

    def build(self):
        nc = self.nc
        with nc.Block() as block:
            for e, reg in (("y", block.sync), ("t", block.tensor),
                           ("v", block.vector), ("s", block.scalar),
                           ("g", block.gpsimd)):
                def runner(E, e=e):
                    for thunk in self.q[e]:
                        thunk(E)
                reg(runner)
        self.stack.close()



def emit_matmul(p, out_psum, lhsT, rhs, M, N,
                lhs_dep=None, rhs_dep=None, start=True, stop=True,
                n_base=0, out_n_base=0, tick_last=True, m_base=0,
                rank1=None, nchunk=512):
    nK = len(lhsT)
    nM = len(out_psum)
    tok = None
    p.wait("t", lhs_dep)
    p.wait("t", rhs_dep)
    nr = len(rank1) if rank1 else 0
    for m in range(nM):
        msz = min(128, M - m * 128)
        for n0 in range(0, N, nchunk):
            nsz = min(nchunk, N - n0)
            for k in range(nK):
                def thunk(E, m=m, n0=n0, k=k, msz=msz, nsz=nsz):
                    return E.matmul(
                        out_psum[m][:msz, out_n_base + n0:out_n_base + n0 + nsz],
                        lhsT[k][:, m_base + m * 128:m_base + m * 128 + msz],
                        rhs[k][:, n_base + n0:n_base + n0 + nsz],
                        start=(k == 0 and start),
                        stop=(k == nK - 1 and nr == 0 and stop))
                last = (m == nM - 1) and (n0 + nchunk >= N) and (k == nK - 1) and nr == 0
                tok = p.do("t", thunk, tick=(tick_last and last))
            for ri in range(nr):
                row_l, row_r = rank1[ri]

                def thunk(E, m=m, n0=n0, msz=msz, nsz=nsz, row_l=row_l,
                          row_r=row_r, ri=ri):
                    return E.matmul(
                        out_psum[m][:msz, out_n_base + n0:out_n_base + n0 + nsz],
                        row_l[0:1, m_base + m * 128:m_base + m * 128 + msz],
                        row_r[0:1, n_base + n0:n_base + n0 + nsz],
                        start=False, stop=(ri == nr - 1 and stop))
                last = (m == nM - 1) and (n0 + nchunk >= N) and (ri == nr - 1)
                tok = p.do("t", thunk, tick=(tick_last and last))
    return tok


def emit_bcast_row(p, psum, row_ap, ones_row, n, dep=None, m=128, n_base=0,
                   tick=True):
    p.wait("t", dep)
    tok = None
    for n0 in range(0, n, 512):
        nsz = min(512, n - n0)

        def thunk(E, n0=n0, nsz=nsz):
            return E.matmul(psum[0:m, n_base + n0:n_base + n0 + nsz],
                            ones_row[0:1, 0:m],
                            row_ap[0:1, n0:n0 + nsz], start=True, stop=True)
        tok = p.do("t", thunk, tick=(tick and n0 + 512 >= n))
    return tok


def emit_ew(p, e, fn, dep=None, tick=True, sedge=None):
    if e in ("v", "s", "g"):
        tick = True
        if sedge is None:
            sedge = True
    if sedge and p.cnt["p" + e] > 0:
        p.wait(e, ("p" + e, p.cnt["p" + e]))
    p.wait(e, dep)
    return p.do(e, fn, tick=tick)


def emit_masked_pick(p, dst, src_all, rowbase, rstride, width, pselcols, scr,
                     ncores, e="y", dep=None, war=None):
    tok = None
    lastc = [war, war]
    for cc in range(ncores):
        p.wait(e, lastc[cc % 2])
        r0 = cc * rstride + rowbase
        ld = p.dma(e, scr[cc % 2][:, 0:width], src_all[r0:r0 + 128, 0:width])
        pc = pselcols[cc]
        if cc == 0:
            tok = emit_ew(p, "v", lambda E, dst=dst, s=scr[cc % 2], pc=pc,
                          width=width: E.tensor_scalar(
                              dst[:, 0:width], s[:, 0:width], pc, None,
                              ALU.mult), dep=[ld] + (dep or []))
        else:
            tok = emit_ew(p, "v", lambda E, dst=dst, s=scr[cc % 2], pc=pc,
                          width=width: E.scalar_tensor_tensor(
                              dst[:, 0:width], s[:, 0:width], pc,
                              dst[:, 0:width], ALU.mult, ALU.add), dep=[ld])
        lastc[cc % 2] = tok
    return tok


def emit_ln_feat(p, xT, out_bf, gamma=None, beta=None, eps=1e-5,
                 out_extra=None, ones=None, ones_row=None, TCH=1024):
    nk = len(xT)
    sizes = [xT[k].shape[0] for k in range(nk)]
    nch = sum(sizes)
    T = xT[0].shape[1]
    inv = 1.0 / nch
    with p.phase("ln") as ctx:
        psS = p.ps_("lnS", [1, TCH], ctx=ctx)
        psQ = p.ps_("lnQ", [1, TCH], ctx=ctx)
        psA = p.ps_("lnA", [128, TCH], ctx=ctx)
        psB = p.ps_("lnB", [128, TCH], ctx=ctx)
        sq = [p.sb("lnsq0", [128, TCH]), p.sb("lnsq1", [128, TCH])]
        R = p.sb("lnrows", [1, 4 * TCH])
        for c0 in range(0, T, TCH):
            p.barrier()
            for k in range(nk):
                for n0 in range(0, TCH, 512):
                    nsz = min(512, TCH - n0)

                    def thunk(E, k=k, n0=n0, c0=c0, nsz=nsz):
                        return E.matmul(psS[0:1, n0:n0 + nsz], ones[:sizes[k], 0:1],
                                        xT[k][:, c0 + n0:c0 + n0 + nsz],
                                        start=(k == 0), stop=(k == nk - 1))
                    p.do("t", thunk)
            mm_tok = [None, None]
            sq_tok = [None, None]
            for k in range(nk):
                pp = k % 2
                sq_tok[pp] = emit_ew(p, "s", lambda E, k=k, pp=pp, c0=c0: E.activation(
                    sq[pp][:sizes[k], :], xT[k][:, c0:c0 + TCH], AF.Square),
                    dep=mm_tok[pp])
                p.wait("t", sq_tok[pp])
                for n0 in range(0, TCH, 512):
                    nsz = min(512, TCH - n0)

                    def thunk(E, k=k, n0=n0, pp=pp, nsz=nsz):
                        return E.matmul(psQ[0:1, n0:n0 + nsz], ones[:sizes[k], 0:1],
                                        sq[pp][:sizes[k], n0:n0 + nsz],
                                        start=(k == 0), stop=(k == nk - 1))
                    mm_tok[pp] = p.do("t", thunk, tick=(n0 + 512 >= TCH))
            stat_tok = mm_tok[(nk - 1) % 2]
            mu, rstd, nmr, tmp = (R[0:1, 0:TCH], R[0:1, TCH:2 * TCH],
                                  R[0:1, 2 * TCH:3 * TCH], R[0:1, 3 * TCH:4 * TCH])
            emit_ew(p, "v", lambda E: E.tensor_scalar_mul(mu, psS[0:1, 0:TCH], inv),
                    dep=stat_tok, sedge=True)
            emit_ew(p, "v", lambda E: E.tensor_scalar_mul(tmp, psQ[0:1, 0:TCH], inv),
                    sedge=True)
            emit_ew(p, "v", lambda E: E.tensor_tensor(rstd, mu, mu, ALU.mult), sedge=True)
            t4 = emit_ew(p, "v", lambda E: E.tensor_tensor(tmp, tmp, rstd, ALU.subtract),
                         sedge=True)
            t5 = emit_ew(p, "s", lambda E: E.activation(rstd, tmp, AF.Ln, bias=float(eps)),
                         dep=t4)
            t6 = emit_ew(p, "s", lambda E: E.activation(rstd, rstd, AF.Exp, scale=-0.5),
                         sedge=True)
            t7 = emit_ew(p, "v", lambda E: E.tensor_tensor(nmr, mu, rstd, ALU.mult),
                         dep=t6, sedge=True)
            t8 = emit_ew(p, "v", lambda E: E.tensor_scalar_mul(nmr, nmr, -1.0), sedge=True)
            bA = emit_bcast_row(p, psA, rstd, ones_row, TCH, dep=t6)
            bB = emit_bcast_row(p, psB, nmr, ones_row, TCH, dep=t8)
            p.wait("v", [bA, bB])
            for k in range(nk):
                emit_ew(p, "v", lambda E, k=k, c0=c0: E.tensor_tensor(
                    sq[0][:sizes[k], :], xT[k][:, c0:c0 + TCH], psA[:sizes[k], :],
                    ALU.mult), tick=False)
                if gamma is None:
                    emit_ew(p, "v", lambda E, k=k, c0=c0: E.tensor_tensor(
                        out_bf[k][:, c0:c0 + TCH], sq[0][:sizes[k], :],
                        psB[:sizes[k], :], ALU.add), tick=False)
                else:
                    emit_ew(p, "v", lambda E, k=k: E.tensor_tensor(
                        sq[0][:sizes[k], :], sq[0][:sizes[k], :], psB[:sizes[k], :],
                        ALU.add), tick=False)
                    emit_ew(p, "v", lambda E, k=k, c0=c0: E.tensor_scalar(
                        out_bf[k][:, c0:c0 + TCH], sq[0][:sizes[k], :], gamma[k],
                        beta[k], ALU.mult, ALU.add), tick=False)
                if out_extra is not None:
                    emit_ew(p, "v", lambda E, k=k, c0=c0: E.tensor_copy(
                        out_extra[k][:, c0:c0 + TCH], out_bf[k][:, c0:c0 + TCH]),
                        tick=False)
            emit_ew(p, "v", lambda E: E.tensor_copy(R[0:1, 0:1], R[0:1, 0:1]))
    return None


def load_tiles(p, dram, nrow, width, dt=F32, e="y", name="wt", ctx=None,
               dram_row0=0):
    nt = cdiv(nrow, 128)
    tiles = []
    toks = []
    for i in range(nt):
        rsz = min(128, nrow - i * 128)
        t = (ctx or p.stack).enter_context(
            p.nc.sbuf_tensor(f"{name}{i}_{p.uid()}", [rsz, width], dt))
        tiles.append(t)
        toks.append(p.dma(e, t[:, :], dram[dram_row0 + i * 128:dram_row0 + i * 128 + rsz, :]))
    return tiles, toks[-1]


def emit_quantize_absmean(p, ctx, dram, K, M, ones, ones_row, name="wq"):
    nc = p.nc
    nk = cdiv(K, 128)
    wq = []
    for i in range(nk):
        rsz = min(128, K - i * 128)
        wq.append(ctx.enter_context(
            nc.sbuf_tensor(f"{name}{i}_{p.uid()}", [rsz, M], BF16)))
    with p.phase("quant") as qctx:
        stream = [p.sb("qs0", [128, M]), p.sb("qs1", [128, M])]
        acc = p.sb("qacc", [128, nk])
        acc2 = p.sb("qacc2", [128, 1])
        rows = p.sb("qrow", [128, 2])
        ps1 = p.ps_("qps", [1, 1], ctx=qctx)
        psb = p.ps_("qpsb", [128, 2], ctx=qctx)
        emit_ew(p, "v", lambda E: E.memset(acc[:], 0.0), tick=False)
        ld_tok = [None, None]
        red_tok = [None, None]
        for i in range(nk):
            pp = i % 2
            rsz = min(128, K - i * 128)
            p.wait("y", ("pv", p.cnt["pv"]))
            ld_tok[pp] = p.dma("y", stream[pp][:rsz, :], dram[i * 128:i * 128 + rsz, :])
            red_tok[pp] = emit_ew(p, "v", lambda E, i=i, pp=pp, rsz=rsz: E.tensor_reduce(
                acc[:rsz, i:i + 1], stream[pp][:rsz, :], AX.X, ALU.add,
                apply_absolute_value=True), dep=ld_tok[pp])
        rt = emit_ew(p, "v", lambda E: E.tensor_reduce(
            acc2[:, 0:1], acc[:, 0:nk], AX.X, ALU.add),
            dep=[red_tok[0], red_tok[1]])
        p.wait("t", rt)
        p.do("t", lambda E: E.matmul(ps1[0:1, 0:1], acc2[:, 0:1], ones[:, 0:1],
                                     start=True, stop=True), tick=True)
        tmm = ("pt", p.cnt["pt"])
        t1 = emit_ew(p, "v", lambda E: E.tensor_scalar(
            rows[0:1, 0:1], ps1[0:1, 0:1], 1.0 / (K * M), 1e-6, ALU.mult,
            ALU.add), dep=tmm, sedge=True)
        t1 = emit_ew(p, "v", lambda E: E.tensor_scalar_mul(rows[0:1, 0:1],
                                                           rows[0:1, 0:1], 0.5),
                     sedge=True)
        t1 = emit_ew(p, "v", lambda E: E.tensor_scalar_mul(rows[0:1, 1:2],
                                                           rows[0:1, 0:1], -1.0),
                     sedge=True)
        p.wait("t", t1)
        p.do("t", lambda E: E.matmul(psb[:, 0:2], ones_row[0:1, 0:128], rows[0:1, 0:2],
                                     start=True, stop=True), tick=True)
        tbb = ("pt", p.cnt["pt"])
        tcp = emit_ew(p, "v", lambda E: E.tensor_copy(rows[:, 0:2], psb[:, 0:2]),
                      dep=tbb)
        gsc = [p.sb("qg0", [128, M], BF16), p.sb("qg1", [128, M], BF16)]
        ld_tok = [None, None]
        q_tok = [None, None]
        for i in range(nk):
            pp = i % 2
            rsz = min(128, K - i * 128)
            p.wait("y", ("pv", p.cnt["pv"]))
            ld_tok[pp] = p.dma("y", stream[pp][:rsz, :], dram[i * 128:i * 128 + rsz, :])
            emit_ew(p, "v", lambda E, pp=pp, rsz=rsz: E.tensor_scalar(
                gsc[0][:rsz, :], stream[pp][:rsz, :], rows[:rsz, 0:1], None,
                ALU.is_gt), dep=[ld_tok[pp], tcp], tick=False)
            emit_ew(p, "v", lambda E, pp=pp, rsz=rsz: E.tensor_scalar(
                gsc[1][:rsz, :], stream[pp][:rsz, :], rows[:rsz, 1:2], None,
                ALU.is_lt), tick=False)
            q_tok[pp] = emit_ew(p, "v", lambda E, i=i, rsz=rsz: E.tensor_tensor(
                wq[i][:, :], gsc[0][:rsz, :], gsc[1][:rsz, :], ALU.subtract))
    return wq


def block_l0ffn(p, io, consts, dbg=None):
    nc = p.nc
    ones, ones_row = consts["ones"], consts["ones_row"]
    H = 4096
    NH = H // 128
    TQ = 128
    with p.phase("l0ffn") as ctx:
        lnwb = p.sb("lnwb", [128, 2 * NH])
        lnwb_tok = [p.dma("y", lnwb[:, 0:NH], io["l0f_lnw"][:, :]),
                    p.dma("y", lnwb[:, NH:2 * NH], io["l0f_lnb"][:, :])]
        lnw_bf = p.sb("lnwbf", [128, NH], BF16)
        p.wait("v", lnwb_tok)
        emit_ew(p, "v", lambda E: E.tensor_copy(lnw_bf[:, :], lnwb[:, 0:NH]))
        xn = p.sbl("l0f_xn", 8, TC, BF16, ctx=ctx)
        with p.phase("preln") as c0:
            x32, xtok = load_tiles(p, io["res_in"], D, TC, F32, name="l0f_x", ctx=c0)
            p.wait("t", xtok)
            p.wait("s", xtok)
            p.wait("v", xtok)
            emit_ln_feat(p, x32, xn, ones=ones, ones_row=ones_row)
        wfcq = emit_quantize_absmean(p, ctx, io["l0f_wfcT"], D, H, ones, ones_row, "wfcq")
        wprq = emit_quantize_absmean(p, ctx, io["l0f_wprT"], H, D, ones, ones_row, "wprq")
        wlrow = p.sb("wlrow", [1, 1024], ctx=ctx)
        wlrow2 = p.sb("wlrow2", [1, 1024], ctx=ctx)
        with p.phase("wl") as c2:
            psr = p.ps_("wlps", [1, 2048], ctx=c2)
            lnb_bf = p.sb("lnbbf", [128, NH], BF16)
            tb = emit_ew(p, "v", lambda E: E.tensor_copy(lnb_bf[:, :], lnwb[:, NH:2 * NH]))
            tw = emit_ew(p, "v", lambda E: E.tensor_copy(lnw_bf[:, :], lnwb[:, 0:NH]))
            p.wait("t", [tb, tw])
            for k in range(NH):
                for (cofs, wcol) in ((0, lnw_bf), (1024, lnb_bf)):
                    for n0 in range(0, 1024, 512):
                        def thunk(E, k=k, n0=n0, cofs=cofs, wcol=wcol):
                            return E.matmul(psr[0:1, cofs + n0:cofs + n0 + 512],
                                            wcol[:, k:k + 1],
                                            wprq[k][:, n0:n0 + 512],
                                            start=(k == 0), stop=(k == NH - 1))
                        p.do("t", thunk,
                             tick=(k == NH - 1 and cofs == 1024 and n0 == 512))
            tmm = ("pt", p.cnt["pt"])
            emit_ew(p, "v", lambda E: E.tensor_copy(wlrow[0:1, :], psr[0:1, 0:1024]),
                    dep=tmm, sedge=True)
            emit_ew(p, "v", lambda E: E.tensor_copy(wlrow2[0:1, :], psr[0:1, 1024:2048]),
                    sedge=True)
        with p.phase("ffnstream") as c3:
            ps_out = p.ps_("l0f_out", [128, 8 * TQ], ctx=c3)
            ps_h = [p.ps_("l0f_hA", [128, TQ], ctx=c3),
                    p.ps_("l0f_hB", [128, TQ], ctx=c3)]
            ps_sS = p.ps_("l0f_sS", [1, TQ], ctx=c3)
            ps_sQ = p.ps_("l0f_sQ", [1, TQ], ctx=c3)
            ps_bc = p.ps_("l0f_bc", [128, TQ], ctx=c3)
            hg = p.sb("l0f_hg", [128, 2 * TQ])
            sqv = p.sb("l0f_sq", [128, 2 * TQ])
            hgall = p.sbl("l0f_hga", NH, TQ, F32, ctx=c3)
            htil = p.sbl("l0f_ht", NH, TQ, BF16, ctx=c3)
            rows = p.sb("l0f_rows", [1, 5 * TQ])
            rstdb = p.sb("l0f_rstdb", [128, TQ])
            rtile = p.sb("l0f_rt", [128, TQ])
            scr = p.sb("l0f_scr", [128, TQ])
            for tq0 in range(0, TC, TQ):
                p.barrier()
                mm_tok = [None, None]
                g_tok = [None, None]
                for m in range(NH):
                    pp = m % 2
                    p.wait("t", g_tok[pp])
                    for k in range(8):
                        def thunk(E, m=m, k=k, pp=pp, tq0=tq0):
                            return E.matmul(ps_h[pp][:, :],
                                            wfcq[k][:, m * 128:(m + 1) * 128],
                                            xn[k][:, tq0:tq0 + TQ],
                                            start=(k == 0), stop=(k == 7))
                        mm_tok[pp] = p.do("t", thunk, tick=(k == 7))
                    g_tok[pp] = emit_ew(p, "s", lambda E, m=m, pp=pp: E.activation(
                        hgall[m][:, :], ps_h[pp][:, :],
                        AF.Gelu), dep=mm_tok[pp])
                    emit_ew(p, "v", lambda E, m=m: E.tensor_scalar(
                        htil[m][:, :], hgall[m][:, :],
                        lnwb[:, m:m + 1], None, ALU.mult), dep=g_tok[pp],
                        tick=(m == NH - 1))
                ht_last = ("pv", p.cnt["pv"])
                p.wait("t", [g_tok[0], g_tok[1]])
                for m in range(NH):
                    def thunk(E, m=m):
                        return E.matmul(ps_sS[0:1, 0:TQ], ones[:, 0:1],
                                        hgall[m][:, :],
                                        start=(m == 0), stop=(m == NH - 1))
                    p.do("t", thunk, tick=(m == NH - 1))
                s1_last = ("pt", p.cnt["pt"])
                sq_tok = [None, None]
                st_tok = None
                for m in range(NH):
                    pp = m % 2
                    sq_tok[pp] = emit_ew(p, "s", lambda E, m=m, pp=pp: E.activation(
                        sqv[:, pp * TQ:(pp + 1) * TQ], hgall[m][:, :], AF.Square),
                        dep=st_tok if m >= 2 else None)
                    p.wait("t", sq_tok[pp])
                    def thunk(E, m=m, pp=pp):
                        return E.matmul(ps_sQ[0:1, 0:TQ], ones[:, 0:1],
                                        sqv[:, pp * TQ:(pp + 1) * TQ],
                                        start=(m == 0), stop=(m == NH - 1))
                    st_tok = p.do("t", thunk, tick=True)
                mu, rstd, nmu, tmp, irow = (
                    rows[0:1, 0:TQ], rows[0:1, TQ:2 * TQ],
                    rows[0:1, 2 * TQ:3 * TQ], rows[0:1, 3 * TQ:4 * TQ],
                    rows[0:1, 4 * TQ:5 * TQ])
                emit_ew(p, "v", lambda E: E.tensor_scalar_mul(mu, ps_sS[0:1, 0:TQ], 1.0 / H),
                        dep=[st_tok, s1_last], sedge=True)
                emit_ew(p, "v", lambda E: E.tensor_scalar_mul(tmp, ps_sQ[0:1, 0:TQ], 1.0 / H),
                        sedge=True)
                emit_ew(p, "v", lambda E: E.tensor_tensor(rstd, mu, mu, ALU.mult), sedge=True)
                t4 = emit_ew(p, "v", lambda E: E.tensor_tensor(tmp, tmp, rstd, ALU.subtract),
                             sedge=True)
                emit_ew(p, "s", lambda E: E.activation(rstd, tmp, AF.Ln, bias=1e-5),
                        dep=t4)
                t6a = emit_ew(p, "s", lambda E: E.activation(irow, rstd, AF.Exp, scale=0.5),
                              sedge=True)
                t6 = emit_ew(p, "s", lambda E: E.activation(rstd, rstd, AF.Exp, scale=-0.5),
                             sedge=True)
                t8 = emit_ew(p, "v", lambda E: E.tensor_scalar_mul(nmu, mu, -1.0),
                             dep=t6, sedge=True)
                bA = emit_bcast_row(p, ps_bc, rstd, ones_row, TQ, dep=t6)
                cA = emit_ew(p, "v", lambda E: E.tensor_copy(rstdb[:, :], ps_bc[:, :]),
                             dep=bA)
                if dbg is not None and tq0 == 0:
                    print("DBG BLOCK EMITTED")
                    p.barrier()
                    dstg = p.sb("dstg", [128, 5 * TQ])
                    dtok = emit_ew(p, "v", lambda E: E.tensor_copy(
                        dstg[0:1, 0:5 * TQ], rows[0:1, :]), dep=t8, sedge=True)
                    dtok = emit_ew(p, "v", lambda E: E.tensor_copy(
                        dstg[:, TQ:2 * TQ], htil[0][:, :]), dep=ht_last)
                    p.wait("y", dtok)
                    p.dma("y", dbg["d_rows"][:, :], dstg[0:1, 0:4 * TQ])
                    p.dma("y", dbg["d_htil"][:, :], dstg[:, TQ:2 * TQ])
                p.wait("t", [ht_last, t8, t6a])
                for mo in range(8):
                    for k in range(NH):
                        def thunk(E, mo=mo, k=k):
                            return E.matmul(ps_out[:, mo * TQ:(mo + 1) * TQ],
                                            wprq[k][:, mo * 128:(mo + 1) * 128],
                                            htil[k][:, :],
                                            start=(k == 0), stop=False)
                        p.do("t", thunk)
                    def thunk(E, mo=mo):
                        return E.matmul(ps_out[:, mo * TQ:(mo + 1) * TQ],
                                        wlrow[0:1, mo * 128:(mo + 1) * 128],
                                        rows[0:1, 2 * TQ:3 * TQ],
                                        start=False, stop=False)
                    p.do("t", thunk)

                    def thunk(E, mo=mo):
                        return E.matmul(ps_out[:, mo * TQ:(mo + 1) * TQ],
                                        wlrow2[0:1, mo * 128:(mo + 1) * 128],
                                        rows[0:1, 4 * TQ:5 * TQ],
                                        start=False, stop=True)
                    mm2 = p.do("t", thunk, tick=(mo == 7))
                if dbg is not None and tq0 == 0:
                    dstg2 = p.sb("dstg2", [128, TQ])
                    dtok = emit_ew(p, "v", lambda E: E.tensor_copy(
                        dstg2[:, :], ps_out[:, 0:TQ]), dep=mm2)
                    p.wait("y", dtok)
                    p.dma("y", dbg["d_psout"][:, :], dstg2[:, :])
                    dtok = emit_ew(p, "v", lambda E: E.tensor_copy(
                        dstg2[0:1, 0:8], wlrow[0:1, 0:8]))
                    dtok = emit_ew(p, "v", lambda E: E.tensor_copy(
                        dstg2[0:1, 8:16], wlrow2[0:1, 0:8]), dep=dtok, sedge=True)
                    p.wait("y", dtok)
                    p.dma("y", dbg["d_wl"][:, :], dstg2[0:1, 0:16])
                p.wait("v", [mm2, cA])
                sv = None
                for mo in range(8):
                    p.wait("y", sv)
                    ld = p.dma("y", rtile[:, :], io["res_in"][mo * 128:(mo + 1) * 128,
                                                              tq0:tq0 + TQ])
                    emit_ew(p, "v", lambda E, mo=mo: E.tensor_tensor(
                        scr[:, :], ps_out[:, mo * TQ:(mo + 1) * TQ],
                        rstdb[:, :], ALU.mult), tick=False)
                    sv = emit_ew(p, "v", lambda E, mo=mo: E.tensor_tensor(
                        hg[:, 0:TQ], rtile[:, :], scr[:, :], ALU.add), dep=ld)
                    p.wait("y", sv)
                    st = p.dma("y", io["res_out"][mo * 128:(mo + 1) * 128, tq0:tq0 + TQ],
                               hg[:, 0:TQ])
                    p.wait("v", st)
    return None



SCALAR_CONSTS = [0.0, 1.0, -1.0, -0.5, 0.5, 1e-5, 1e-6, 1e-12, 2.0, -2.0, 0.03125,
                 float(np.pi), float(np.pi / 2), float(-np.pi)]


def setup_consts(p):
    nc = p.nc
    consts = {}
    ones = p.sb("ones_col", [128, 1])
    od = p.const("onescol", np.ones((128, 1), np.float32))
    t1 = p.dma("y", ones[:, :], od[:])
    ones_row = p.sb("ones_row", [1, 512])
    ord_ = p.const("onesrow", np.ones((1, 512), np.float32))
    t2 = p.dma("y", ones_row[:, :], ord_[:])
    ones_row_bf = p.sb("ones_row_bf", [1, 512], BF16)
    orb = p.const("onesrowbf", np.ones((1, 512), BF16NP))
    t3 = p.dma("y", ones_row_bf[:, :], orb[:])
    ident_bf = p.sb("ident_bf", [128, 128], BF16)
    idb = p.const("identbf", np.eye(128, dtype=BF16NP))
    t5 = p.dma("y", ident_bf[:, :], idb[:])
    ident_f32 = p.sb("ident_f32", [128, 128])
    idf = p.const("identf", np.eye(128, dtype=np.float32))
    t6 = p.dma("y", ident_f32[:, :], idf[:])
    zero_row_bf = p.sb("zero_row_bf", [1, 512], BF16)
    zrb = p.const("zerorowbf", np.zeros((1, 512), BF16NP))
    t7 = p.dma("y", zero_row_bf[:, :], zrb[:])
    consts_extra = dict(ident_bf=ident_bf, ident_f32=ident_f32,
                        zero_row_bf=zero_row_bf)
    cvals = sorted(set(SCALAR_CONSTS))
    cp = p.sb("cpool", [128, len(cvals)])
    cpd = p.const("cpool", np.tile(np.array(cvals, np.float32), (128, 1)))
    t4 = p.dma("y", cp[:, :], cpd[:])
    for i, cv in enumerate(cvals):
        nc.const_aps.register(F32, cv, cp[:, i:i + 1])
    consts.update(ones=ones, ones_row=ones_row, ones_row_bf=ones_row_bf,
                  tok=[t1, t2, t3, t4, t5, t6, t7], **consts_extra)
    p.barrier()
    return consts


def host_prep_l0ffn(inp):
    f32 = np.float32
    return {
        "l0f_wfcT": np.ascontiguousarray(inp["L0_ffn_W_fc"].T.astype(f32)),
        "l0f_wprT": np.ascontiguousarray(inp["L0_ffn_W_proj"].T.astype(f32)),
        "l0f_lnw": np.ascontiguousarray(inp["L0_ffn_ln_w"].reshape(32, 128).T.astype(f32)),
        "l0f_lnb": np.ascontiguousarray(inp["L0_ffn_ln_b"].reshape(32, 128).T.astype(f32)),
    }


L0FFN_IO = {
    "l0f_wfcT": [D, 4096], "l0f_wprT": [4096, D],
    "l0f_lnw": [128, 32], "l0f_lnb": [128, 32],
}


def declare_io(nc, spec, kind="ExternalInput"):
    return {k: nc.dram_tensor(k, shp, F32, kind=kind) for k, shp in spec.items()}


def build_l0ffn_standalone():
    nc = bass.Bass("TRN2", target_bir_lowering=False, debug=False)
    io = declare_io(nc, L0FFN_IO)
    io["res_in"] = nc.dram_tensor("res_in", [D, TC], F32, kind="ExternalInput")
    io["res_out"] = nc.dram_tensor("res_out", [D, TC], F32, kind="ExternalOutput")
    p = Prog(nc)
    consts = setup_consts(p)
    block_l0ffn(p, io, consts)
    p.build()
    return nc



L1ATTN_IO = {
    "l1a_wq": [D, D], "l1a_wk": [D, D], "l1a_wv": [D, D],
    "l1a_bq": [128, 8], "l1a_bk": [128, 8], "l1a_bv": [1, D],
}


def host_prep_l1attn(inp):
    f32 = np.float32
    return {
        "l1a_wq": np.ascontiguousarray(inp["L1_attn_W_q"].astype(f32)),
        "l1a_wk": np.ascontiguousarray(inp["L1_attn_W_k"].astype(f32)),
        "l1a_wv": np.ascontiguousarray(inp["L1_attn_W_v"].astype(f32)),
        "l1a_bq": np.ascontiguousarray(inp["L1_attn_b_q"].reshape(8, 128).T.astype(f32)),
        "l1a_bk": np.ascontiguousarray(inp["L1_attn_b_k"].reshape(8, 128).T.astype(f32)),
        "l1a_bv": np.ascontiguousarray(inp["L1_attn_b_v"].reshape(1, D).astype(f32)),
    }


def block_l1attn(p, io, consts, ncores, core_id=0):
    nc = p.nc
    ones, ones_row, ones_row_bf = consts["ones"], consts["ones_row"], consts["ones_row_bf"]
    NT = TC // 128
    SCALE = 1.0 / 32.0
    Lg = ncores * TC
    with p.phase("l1attn") as ctx:
        kt_loc = nc.dram_tensor(f"kt_loc_{p.uid()}", [D, TC], BF16)
        v_loc = nc.dram_tensor(f"v_loc_{p.uid()}", [TC, D], BF16)
        aspace = "Shared" if ncores > 4 else "Local"
        kt_all = nc.dram_tensor(f"kt_all_{p.uid()}", [ncores * D, TC], BF16,
                                addr_space=aspace)
        v_all = nc.dram_tensor(f"v_all_{p.uid()}", [Lg, D], BF16,
                               addr_space=aspace)
        oscr = nc.dram_tensor(f"l1a_oscr_{p.uid()}", [TC, D], F32)
        qt_d = nc.dram_tensor(f"l1a_qtd_{p.uid()}", [D, TC], BF16)
        xn_ctx = ExitStack()
        xn = p.sbl("l1a_xn", 8, TC, BF16, ctx=xn_ctx)
        with p.phase("preln") as c0:
            x32, xtok = load_tiles(p, io["res_in"], D, TC, F32, name="l1a_x", ctx=c0)
            for e in ("t", "s", "v"):
                p.wait(e, xtok)
            emit_ln_feat(p, x32, xn, ones=ones, ones_row=ones_row)
        with p.phase("proj") as c1:
            qt = p.sbl("l1a_qt", 8, TC, BF16, ctx=c1)
            wtile = p.sbl("l1a_w", 8, D, BF16, ctx=c1)
            bias = p.sb("l1a_b", [128, 16])
            btok = [p.dma("y", bias[:, 0:8], io["l1a_bq"][:, :]),
                    p.dma("y", bias[:, 8:16], io["l1a_bk"][:, :])]
            bvrow = p.sb("l1a_bvr", [1, D])
            btok.append(p.dma("y", bvrow[:, :], io["l1a_bv"][:, :]))
            bvrow_bf = p.sb("l1a_bvb", [1, D], BF16)
            bvb_tok = emit_ew(p, "v", lambda E: E.tensor_copy(bvrow_bf[:, :], bvrow[:, :]),
                              dep=btok)
            kt_tiles = p.sbl("l1a_kt", 8, TC, BF16, ctx=c1)
            ps = p.ps_("l1a_ps", [128, TC], F32, ctx=c1)
            ev = None
            for (wname, outt, bofs) in (("l1a_wq", qt, 0), ("l1a_wk", kt_tiles, 8)):
                p.wait("g", ev)
                wtok = [p.dma("g", wtile[k][:, :], io[wname][k * 128:(k + 1) * 128, :])
                        for k in range(8)]
                for m in range(8):
                    p.wait("t", ev)
                    tok = emit_matmul(p, [ps], wtile, xn, 128, TC,
                                      lhs_dep=wtok if m == 0 else None,
                                      m_base=m * 128)
                    ev = emit_ew(p, "s", lambda E, m=m, outt=outt, bofs=bofs:
                                 E.activation(outt[m][:, :], ps[:, :], AF.Identity,
                                              bias=bias[:, bofs + m:bofs + m + 1]),
                                 dep=[tok] + btok)
            p.wait("g", ev)
            wtok = [p.dma("g", wtile[k][:, :], io["l1a_wv"][k * 128:(k + 1) * 128, :])
                    for k in range(8)]
            vt = p.sbl("l1a_vt", NT, D, BF16, ctx=c1)
            for m in range(NT):
                p.wait("t", ev)
                if m == 0:
                    p.wait("t", wtok + [bvb_tok])
                for n0 in (0, 512):
                    for k in range(8):
                        def thunk(E, m=m, n0=n0, k=k):
                            return E.matmul(ps[:, n0:n0 + 512],
                                            xn[k][:, m * 128:(m + 1) * 128],
                                            wtile[k][:, n0:n0 + 512],
                                            start=(k == 0), stop=False)
                        p.do("t", thunk)

                    def thunk(E, m=m, n0=n0):
                        return E.matmul(ps[:, n0:n0 + 512], ones_row_bf[0:1, 0:128],
                                        bvrow_bf[0:1, n0:n0 + 512],
                                        start=False, stop=True)
                    tok = p.do("t", thunk, tick=True)
                ev = emit_ew(p, "s", lambda E, m=m: E.activation(
                    vt[m][:, 0:D], ps[:, 0:D], AF.Copy), dep=tok)
            kst_tok = []
            for k in range(8):
                p.wait("y", ev if k == 0 else None)
                kst_tok.append(p.dma("y", kt_loc[k * 128:(k + 1) * 128, :],
                                     kt_tiles[k][:, :]))
                kst_tok.append(p.dma("y", qt_d[k * 128:(k + 1) * 128, :],
                                     qt[k][:, :]))
            for m in range(NT):
                kst_tok.append(p.dma("y", v_loc[m * 128:(m + 1) * 128, :],
                                     vt[m][:, :]))
        cc1 = p.collective("AllGather", ALU.bypass, [kt_loc[:, :]], [kt_all[:, :]],
                           ngroups=ncores)
        cc2 = p.collective("AllGather", ALU.bypass, [v_loc[:, :]], [v_all[:, :]],
                           ngroups=ncores)
        p.barrier()
        xn_ctx.close()
        with p.phase("flash") as c2:
            p.wait("y", [cc1, cc2])
            sfull = p.sb("l1a_s", [128, Lg])
            pbf = p.sb("l1a_p", [128, Lg], BF16)
            kst = [p.sbl("l1a_k0", 8, TC, BF16, ctx=c2),
                   p.sbl("l1a_k1", 8, TC, BF16, ctx=c2)]
            vst = [p.sb("l1a_v0", [128, D], BF16), p.sb("l1a_v1", [128, D], BF16)]
            rowst = p.sb("l1a_rows", [128, 4])
            obuf = sfull[:, 0:D]
            ps_s = [p.ps_("l1a_ss0", [128, 512], F32, ctx=c2),
                    p.ps_("l1a_ss1", [128, 512], F32, ctx=c2)]
            ps_pt = [p.ps_("l1a_pt0", [128, 2048], BF16, ctx=c2),
                     p.ps_("l1a_pt1", [128, 2048], BF16, ctx=c2)]
            ps_o = p.ps_("l1a_pso", [128, D], F32, ctx=c2)
            ptsb = [p.sb("l1a_pb0", [128, 2048], BF16),
                    p.sb("l1a_pb1", [128, 2048], BF16)]
            kuse = [None, None]
            qcur = p.sbl("l1a_qc", 8, 128, BF16, ctx=c2)
            sev = [None, None]
            vuse = [None, None]
            ptuse = [None, None]
            passA_prev = None
            pbend_prev = None
            for q in range(NT):
                p.wait("y", passA_prev)
                qld = [p.dma("y", qcur[k][:, :],
                             qt_d[k * 128:(k + 1) * 128, q * 128:(q + 1) * 128])
                       for k in range(8)]
                p.wait("t", qld)
                for r in range(ncores):
                    ppk = r % 2
                    p.wait("y", kuse[ppk])
                    kld = [p.dma("y", kst[ppk][k][:, :],
                                 kt_all[r * D + k * 128:r * D + (k + 1) * 128, :])
                           for k in range(8)]
                    p.wait("t", kld)
                    for ci, n0 in enumerate(range(0, TC, 512)):
                        pps = ci % 2
                        p.wait("t", sev[pps])
                        for k in range(8):
                            def thunk(E, ppk=ppk, pps=pps, n0=n0, k=k):
                                return E.matmul(ps_s[pps][:, :],
                                                qcur[k][:, :],
                                                kst[ppk][k][:, n0:n0 + 512],
                                                start=(k == 0), stop=(k == 7))
                            p.do("t", thunk, tick=(k == 7))
                        tok = ("pt", p.cnt["pt"])
                        sev[pps] = emit_ew(
                            p, "s", lambda E, r=r, n0=n0, pps=pps: E.activation(
                                sfull[:, r * TC + n0:r * TC + n0 + 512],
                                ps_s[pps][:, :], AF.Identity, scale=SCALE), dep=tok)
                    kuse[ppk] = sev[(len(range(0, TC, 512)) - 1) % 2]
                passA_prev = ("pt", p.cnt["pt"])
                mx = emit_ew(p, "v", lambda E: E.tensor_reduce(
                    rowst[:, 0:1], sfull[:, :], AX.X, ALU.max, negate=True),
                    dep=[sev[0], sev[1]])
                ex = emit_ew(p, "s", lambda E: E.activation(
                    pbf[:, :], sfull[:, :], AF.Exp, bias=rowst[:, 0:1],
                    accum_out=rowst[:, 1:2]), dep=[mx, pbend_prev])
                rz = emit_ew(p, "v", lambda E: E.reciprocal(rowst[:, 2:3],
                                                            rowst[:, 1:2]), dep=ex)
                first_half = {0: True, 512: True}
                nblk = Lg // 2048
                for blk in range(nblk):
                    ppb = blk % 2
                    p.wait("t", [ex, ptuse[ppb]])
                    for j in range(16):
                        def thunk(E, blk=blk, j=j, ppb=ppb):
                            return E.transpose(
                                ps_pt[ppb][:, j * 128:(j + 1) * 128],
                                pbf[:, blk * 2048 + j * 128:blk * 2048 + (j + 1) * 128],
                                consts["ident_bf"][:, :])
                        tok = p.do("t", thunk, tick=(j == 15))
                    ptev = emit_ew(p, "v", lambda E, ppb=ppb: E.tensor_copy(
                        ptsb[ppb][:, :], ps_pt[ppb][:, :]), dep=tok)
                    for j in range(16):
                        pj = j % 2
                        p.wait("y", vuse[pj])
                        vld = p.dma("y", vst[pj][:, :],
                                    v_all[blk * 2048 + j * 128:blk * 2048 + (j + 1) * 128, :])
                        p.wait("t", [vld, ptev])
                        for n0 in (0, 512):
                            def thunk(E, j=j, pj=pj, ppb=ppb, n0=n0,
                                      first=first_half[n0]):
                                return E.matmul(ps_o[:, n0:n0 + 512],
                                                ptsb[ppb][:, j * 128:(j + 1) * 128],
                                                vst[pj][:, n0:n0 + 512],
                                                start=first, stop=False)
                            tok = p.do("t", thunk, tick=(n0 == 512))
                            first_half[n0] = False
                        vuse[pj] = tok
                    ptuse[ppb] = tok
                for n0 in (0, 512):
                    def thunk(E, n0=n0):
                        return E.matmul(ps_o[:, n0:n0 + 512], ones_row_bf[0:1, 0:128],
                                        consts["zero_row_bf"][0:1, 0:512],
                                        start=False, stop=True)
                    tok = p.do("t", thunk, tick=(n0 == 512))
                pbend_prev = ("pt", p.cnt["pt"])
                ev = emit_ew(p, "s", lambda E: E.activation(
                    obuf[:, :], ps_o[:, :], AF.Identity, scale=rowst[:, 2:3]),
                    dep=[tok, rz])
                p.wait("y", ev)
                st = p.dma("y", oscr[q * 128:(q + 1) * 128, :], obuf[:, :])
                p.wait("s", st)
        with p.phase("resup") as c3:
            osb = p.sbl("l1a_ot", NT, D, F32, ctx=c3)
            otok = [p.dma("y", osb[m][:, :], oscr[m * 128:(m + 1) * 128, :])
                    for m in range(NT)]
            ps_t = p.ps_("l1a_pst", [128, TC], F32, ctx=c3)
            rt2 = p.sb("l1a_rt2", [128, TC])
            p.wait("t", otok)
            st = None
            for dk in range(8):
                for m in range(NT):
                    def thunk(E, dk=dk, m=m):
                        return E.transpose(ps_t[:, m * 128:(m + 1) * 128],
                                           osb[m][:, dk * 128:(dk + 1) * 128],
                                           consts["ident_f32"][:, :])
                    tok = p.do("t", thunk, tick=(m == NT - 1))
                p.wait("y", st)
                ld = p.dma("y", rt2[:, :], io["res_in"][dk * 128:(dk + 1) * 128, :])
                sv = emit_ew(p, "v", lambda E, dk=dk: E.tensor_tensor(
                    rt2[:, :], rt2[:, :], ps_t[:, :], ALU.add), dep=[tok, ld])
                p.wait("y", sv)
                st = p.dma("y", io["res_out"][dk * 128:(dk + 1) * 128, :], rt2[:, :])
                p.wait("v", st)
                p.wait("t", sv)
    return None


def build_l1attn_standalone(ncores=1):
    nc = bass.Bass("TRN2", target_bir_lowering=False, debug=False)
    io = declare_io(nc, L1ATTN_IO)
    io["res_in"] = nc.dram_tensor("res_in", [D, TC], F32, kind="ExternalInput")
    io["res_out"] = nc.dram_tensor("res_out", [D, TC], F32, kind="ExternalOutput")
    p = Prog(nc)
    consts = setup_consts(p)
    block_l1attn(p, io, consts, ncores)
    p.build()
    return nc



NHEAD = 16
DHEAD = 64
WIN = 256
INNER = 1024
DI2 = 2730
NJ = cdiv(DI2, 128)

L1FFN_IO = {}
for _i in range(6):
    L1FFN_IO.update({
        f"l1f_wqkv{_i}": [D, 3 * INNER],
        f"l1f_wo{_i}": [INNER, D],
        f"l1f_w1{_i}": [D, 256 * (NJ - 1) + 84],
        f"l1f_w2{_i}": [DI2, D],
        f"l1f_lng{_i}": [128, 8],
        f"l1f_lnb{_i}": [128, 8],
        f"l1f_b1{_i}": [128, 2 * NJ],
        f"l1f_b2{_i}": [128, 8],
    })
for _i in range(6):
    L1FFN_IO.update({
        f"l1f_cosq{_i}": [TC, DHEAD], f"l1f_sinq{_i}": [TC, DHEAD],
        f"l1f_cosk{_i}": [TC, DHEAD], f"l1f_sink{_i}": [TC, DHEAD],
    })
L1FFN_IO.update({
    "l1f_mask": [128, 2048],
    "l1f_c0": [128, 1],
    "l1f_psel": [128, 8],
})


def host_prep_l1ffn(inp, core_id):
    f32 = np.float32
    out = {}
    for i in range(6):
        W1 = inp["L1_ffn_W1"][i].astype(f32)
        chunks = []
        b1 = inp["L1_ffn_b1"][i].astype(f32)
        b1c = []
        for j in range(NJ):
            a0, a1 = j * 128, min((j + 1) * 128, DI2)
            chunks.append(W1[:, a0:a1])
            b1c.append(np.pad(b1[a0:a1], (0, 128 - (a1 - a0))))
            chunks.append(W1[:, DI2 + a0:DI2 + a1])
            b1c.append(np.pad(b1[DI2 + a0:DI2 + a1], (0, 128 - (a1 - a0))))
        out[f"l1f_w1{i}"] = np.ascontiguousarray(np.concatenate(chunks, 1))
        out[f"l1f_b1{i}"] = np.ascontiguousarray(np.stack(b1c, 1))
        out[f"l1f_wqkv{i}"] = np.ascontiguousarray(inp["L1_ffn_Wqkv"][i].astype(f32))
        out[f"l1f_wo{i}"] = np.ascontiguousarray(inp["L1_ffn_Wo"][i].astype(f32))
        out[f"l1f_w2{i}"] = np.ascontiguousarray(inp["L1_ffn_W2"][i].astype(f32))
        out[f"l1f_lng{i}"] = np.ascontiguousarray(
            inp["L1_ffn_ln_g"][i].reshape(8, 128).T.astype(f32))
        out[f"l1f_lnb{i}"] = np.ascontiguousarray(
            inp["L1_ffn_ln_b"][i].reshape(8, 128).T.astype(f32))
        out[f"l1f_b2{i}"] = np.ascontiguousarray(
            inp["L1_ffn_b2"][i].reshape(8, 128).T.astype(f32))
    half = np.arange(0, DHEAD, 2, dtype=np.float64)
    inv_freq = 1.0 / 10000.0 ** (half / DHEAD)
    t = np.arange(core_id * TC, (core_id + 1) * TC, dtype=np.float64)
    freqs = t[:, None] * inv_freq[None, :]
    freqs = np.concatenate([freqs, freqs], -1)
    base_scale = (half + 0.4 * DHEAD) / (1.4 * DHEAD)
    power = (t - L // 2) / (WIN // 2)
    scale = base_scale[None, :] ** power[:, None]
    scale = np.concatenate([scale, scale], -1)
    for i in range(6):
        qs = inp["L1_ffn_q_scale"][i].astype(np.float64)
        ks = inp["L1_ffn_k_scale"][i].astype(np.float64)
        qsr = np.concatenate([qs[DHEAD // 2:], qs[:DHEAD // 2]])
        ksr = np.concatenate([ks[DHEAD // 2:], ks[:DHEAD // 2]])
        out[f"l1f_cosq{i}"] = (np.cos(freqs) * scale * qs).astype(f32)
        out[f"l1f_sinq{i}"] = (np.sin(freqs) * scale * qsr).astype(f32)
        out[f"l1f_cosk{i}"] = (np.cos(freqs) / scale * ks).astype(f32)
        out[f"l1f_sink{i}"] = (np.sin(freqs) / scale * ksr).astype(f32)
    m = np.zeros((2, 128, 512), np.float32)
    for hlf in range(2):
        for i_ in range(128):
            iw = hlf * 128 + i_
            for j_ in range(512):
                jw = j_ - 256
                d_ = iw - jw
                m[hlf, i_, j_] = 1.0 if (0 <= d_ <= 256) else 0.0
    mfirst = m.copy()
    if core_id == 0:
        mfirst[:, :, :256] = 0.0
    out["l1f_mask"] = np.ascontiguousarray(
        np.concatenate([m.transpose(1, 0, 2).reshape(128, 1024),
                        mfirst.transpose(1, 0, 2).reshape(128, 1024)],
                       axis=1))
    out["l1f_c0"] = np.full((128, 1), 0.0 if core_id == 0 else 1.0, np.float32)
    ps = np.zeros((128, 8), np.float32)
    if core_id > 0:
        ps[:, core_id - 1] = 1.0
    out["l1f_psel"] = ps
    return out


def block_l1ffn(p, io, consts, ncores, core_id=0, nlayers=6):
    nc = p.nc
    ones, ones_row, ones_row_bf = consts["ones"], consts["ones_row"], consts["ones_row_bf"]
    NTT = TC // 128
    with p.phase("l1ffn") as ctx:
        aspace = "Shared" if ncores > 4 else "Local"
        xping = [nc.dram_tensor(f"l1f_x{j}_{p.uid()}", [D, TC], F32) for j in range(2)]
        hl_loc = nc.dram_tensor(f"l1f_hl_{p.uid()}", [128, 4], F32)
        hl_all = nc.dram_tensor(f"l1f_hla_{p.uid()}", [ncores * 128, 4], F32,
                                addr_space=aspace)
        kw_loc = nc.dram_tensor(f"l1f_kwl_{p.uid()}", [INNER, WIN], BF16)
        kw_all = nc.dram_tensor(f"l1f_kwa_{p.uid()}", [ncores * INNER, WIN], BF16,
                                addr_space=aspace)
        vw_loc = nc.dram_tensor(f"l1f_vwl_{p.uid()}", [WIN, INNER], BF16)
        vw_all = nc.dram_tensor(f"l1f_vwa_{p.uid()}", [ncores * WIN, INNER], BF16,
                                addr_space=aspace)
        xat_scr = nc.dram_tensor(f"l1f_xat_{p.uid()}", [D, TC], F32)
        prev = (core_id - 1) % ncores
        c0m = p.sb("l1f_c0m", [128, 1])
        psel = p.sb("l1f_ps", [128, 8])
        hscr = [p.sb("l1f_hs0", [128, 4]), p.sb("l1f_hs1", [128, 4])]
        mskb = p.sb("l1f_mskb", [128, 2048], BF16)
        ctok = [p.dma("y", c0m[:, :], io["l1f_c0"][:, :]),
                p.dma("y", psel[:, :], io["l1f_psel"][:, :])]
        for e_ in ("v", "s"):
            p.wait(e_, ctok)
        pcf = [psel[:, cc:cc + 1] for cc in range(8)]
        with p.phase("mskload") as cm0:
            msk = p.sb("l1f_msk", [128, 2048], ctx=cm0)
            mt = p.dma("y", msk[:, :], io["l1f_mask"][:, :])
            emit_ew(p, "v", lambda E: E.tensor_copy(mskb[:, :], msk[:, :]),
                    dep=[mt] + ctok)
        xln_d = nc.dram_tensor(f"l1f_xln_{p.uid()}", [D, TC], F32)
        with p.phase("l1f_pre") as cpre:
            x32p, xtokp = load_tiles(p, io["res_in"], D, TC, F32,
                                     name="l1f_px", ctx=cpre)
            t1p = p.sbl("l1f_pt1", 8, TC, F32, ctx=cpre)
            for e_ in ("t", "s", "v"):
                p.wait(e_, xtokp)
            emit_ln_feat(p, x32p, t1p, ones=consts["ones"],
                         ones_row=consts["ones_row"])
            pst_ = None
            for k in range(8):
                p.wait("y", [("pv", p.cnt["pv"]), pst_])
                pst_ = p.dma("y", xln_d[k * 128:(k + 1) * 128, :], t1p[k][:, :])
            p.wait("v", pst_)
        p.barrier()
        for li in range(nlayers):
            xin = xln_d if li == 0 else xping[li % 2]
            xout = io["res_out"] if li == nlayers - 1 else xping[(li + 1) % 2]
            with p.phase(f"l1f_at{li}") as c1:
                qT = p.sbl("qT", 8, TC, BF16, ctx=c1)
                kT = p.sbl("kT", 8, TC, BF16, ctx=c1)
                vt = p.sbl("vtt", NTT, INNER, BF16, ctx=c1)
                cqkv = ExitStack()
                p._ctx.append(cqkv)
                xs = p.sbl("xsh", 8, TC, BF16)
                x2 = p.sb("x2a", [128, TC])
                contrib = p.sb("ctb", [128, 4])
                halo = p.sb("halo", [128, 4])
                xuse = None
                for k in range(8):
                    p.wait("y", xuse)
                    xt_ = p.dma("y", x2[:, :], xin[k * 128:(k + 1) * 128, :])
                    if k < 4:
                        xuse = emit_ew(p, "v", lambda E, k=k: E.tensor_copy(
                            xs[k][:, :], x2[:, :]), dep=xt_)
                    else:
                        emit_ew(p, "v", lambda E, k=k: E.tensor_copy(
                            xs[k][:, 1:TC], x2[:, 0:TC - 1]), dep=xt_)
                        xuse = emit_ew(p, "v", lambda E, k=k, contrib=contrib:
                                       E.tensor_copy(
                                           contrib[:, k - 4:k - 3],
                                           x2[:, TC - 1:TC]))
                p.wait("y", xuse)
                cst = p.dma("y", hl_loc[:, :], contrib[:, :])
                p.wait("g", cst)
                cc = p.collective("AllGather", ALU.bypass, [hl_loc[:, :]],
                                  [hl_all[:, :]], ngroups=ncores)
                p.wait("y", cc)
                hz = emit_masked_pick(p, halo, hl_all, 0, 128, 4, pcf, hscr,
                                      ncores)
                for k in range(4):
                    emit_ew(p, "v", lambda E, k=k, halo=halo: E.tensor_copy(
                        xs[4 + k][:, 0:1], halo[:, k:k + 1]), dep=hz)
                xs_tok = ("pv", p.cnt["pv"])
                rot = p.sb("l1f_rot", [128, 4 * DHEAD])
                wq = p.sbl("wqk", 8, 2 * INNER, BF16)
                wtok = [p.dma("g", wq[k][:, :],
                              io[f"l1f_wqkv{li}"][k * 128:(k + 1) * 128, 0:2 * INNER])
                        for k in range(8)]
                psq = [p.ps_("psq0", [128, 512], F32),
                       p.ps_("psq1", [128, 512], F32)]
                pst = p.ps_("pstr", [128, 2048], BF16)
                qraw = p.sb("qraw", [128, 2 * INNER])
                sqs = p.sb("sqs", [128, 64])
                rhb = p.sb("rhb", [128, INNER])
                qrot = p.sb("qrot", [128, 2 * INNER], BF16)
                p.wait("t", wtok + [xs_tok])
                ev = [None, None]
                trev = None
                rot_lds = []
                rot_cons = None
                for tt in range(NTT):
                    for ci in range(4):
                        pp = ci % 2
                        p.wait("t", ev[pp])
                        for k in range(8):
                            def thunk(E, tt=tt, ci=ci, k=k, pp=pp):
                                return E.matmul(psq[pp][:, :],
                                                xs[k][:, tt * 128:(tt + 1) * 128],
                                                wq[k][:, ci * 512:(ci + 1) * 512],
                                                start=(k == 0), stop=(k == 7))
                            tok = p.do("t", thunk, tick=(k == 7))
                        ev[pp] = emit_ew(p, "s", lambda E, ci=ci, pp=pp: E.activation(
                            qraw[:, ci * 512:(ci + 1) * 512], psq[pp][:, :],
                            AF.Identity), dep=tok)
                    p.wait("y", rot_lds + [rot_cons])
                    rot_lds = []
                    for ri, rname in enumerate((f"l1f_cosq{li}", f"l1f_sinq{li}",
                                                f"l1f_cosk{li}", f"l1f_sink{li}")):
                        rot_lds.append(
                            p.dma("y", rot[:, ri * DHEAD:(ri + 1) * DHEAD],
                                  io[rname][tt * 128:(tt + 1) * 128, :]))
                    p.wait("v", rot_lds)

                    def rotslc(ri, tt):
                        return rot[:, ri * DHEAD:(ri + 1) * DHEAD]

                    rhb_last = None
                    for (ofs, sofs, cri, sri) in ((0, 0, 0, 1), (INNER, 96, 2, 3)):
                        sq_t = emit_ew(p, "s", lambda E, ofs=ofs: E.activation(
                            rhb[:, :], qraw[:, ofs:ofs + INNER], AF.Square),
                            dep=[ev[0], ev[1], rhb_last])
                        emit_ew(p, "v", lambda E: E.tensor_reduce(
                            sqs[:, 0:16], rhb[:, :].rearrange("p (h d) -> p h d", h=16),
                            AX.X, ALU.add), dep=sq_t)
                        mx_t = emit_ew(p, "v", lambda E: E.tensor_scalar_max(
                            sqs[:, 16:32], sqs[:, 0:16], 1e-24))
                        emit_ew(p, "s", lambda E: E.activation(
                            sqs[:, 32:48], sqs[:, 16:32], AF.Ln), dep=mx_t)
                        rst_t = emit_ew(p, "s", lambda E: E.activation(
                            sqs[:, 32:48], sqs[:, 32:48], AF.Exp, scale=-0.5),
                            sedge=True)
                        for h in range(16):
                            emit_ew(p, "v", lambda E, ofs=ofs, h=h: E.tensor_scalar(
                                qraw[:, ofs + h * 64:ofs + (h + 1) * 64],
                                qraw[:, ofs + h * 64:ofs + (h + 1) * 64],
                                sqs[:, 32 + h:33 + h], None, ALU.mult),
                                sedge=(h == 0), dep=rst_t if h == 0 else None)
                        emit_ew(p, "v", lambda E, ofs=ofs: E.tensor_scalar(
                            rhb[:, :].rearrange("p (h c d) -> p h c d", c=2, d=32)[:, :, 0:1, :],
                            qraw[:, ofs:ofs + INNER].rearrange(
                                "p (h c d) -> p h c d", c=2, d=32)[:, :, 1:2, :],
                            -1.0, None, ALU.mult))
                        emit_ew(p, "v", lambda E, ofs=ofs: E.tensor_copy(
                            rhb[:, :].rearrange("p (h c d) -> p h c d", c=2, d=32)[:, :, 1:2, :],
                            qraw[:, ofs:ofs + INNER].rearrange(
                                "p (h c d) -> p h c d", c=2, d=32)[:, :, 0:1, :]))
                        emit_ew(p, "v", lambda E, ofs=ofs, cri=cri, tt=tt: E.tensor_tensor(
                            qraw[:, ofs:ofs + INNER].rearrange("p (h d) -> p h d", h=16),
                            qraw[:, ofs:ofs + INNER].rearrange("p (h d) -> p h d", h=16),
                            rotslc(cri, tt).rearrange("p (a d) -> p a d", a=1).broadcast_to(
                                (128, 16, DHEAD)), ALU.mult))
                        emit_ew(p, "v", lambda E, sri=sri, tt=tt: E.tensor_tensor(
                            rhb[:, :].rearrange("p (h d) -> p h d", h=16),
                            rhb[:, :].rearrange("p (h d) -> p h d", h=16),
                            rotslc(sri, tt).rearrange("p (a d) -> p a d", a=1).broadcast_to(
                                (128, 16, DHEAD)), ALU.mult))
                        rhb_last = emit_ew(p, "v", lambda E, ofs=ofs: E.tensor_tensor(
                            qrot[:, ofs:ofs + INNER], qraw[:, ofs:ofs + INNER],
                            rhb[:, :], ALU.add))
                    rot_tok = ("pv", p.cnt["pv"])
                    rot_cons = rot_tok
                    p.wait("t", [rot_tok, trev])
                    for j in range(16):
                        def thunk(E, tt=tt, j=j, pst=pst, qrot=qrot):
                            return E.transpose(
                                pst[:, j * 128:(j + 1) * 128],
                                qrot[:, j * 128:(j + 1) * 128],
                                consts["ident_bf"][:, :])
                        tok = p.do("t", thunk, tick=(j == 15))
                    for j in range(8):
                        trev = emit_ew(p, "v", lambda E, tt=tt, j=j, pst=pst, qT=qT:
                                       E.tensor_copy(
                                           qT[j][:, tt * 128:(tt + 1) * 128],
                                           pst[:, j * 128:(j + 1) * 128]),
                                       dep=tok if j == 0 else None)
                    for j in range(8):
                        trev = emit_ew(p, "v", lambda E, tt=tt, j=j, pst=pst, kT=kT:
                                       E.tensor_copy(
                                           kT[j][:, tt * 128:(tt + 1) * 128],
                                           pst[:, (8 + j) * 128:(8 + j + 1) * 128]))
                p.barrier()
                wvtok = [p.dma("g", wq[k][:, 0:INNER],
                               io[f"l1f_wqkv{li}"][k * 128:(k + 1) * 128,
                                                   2 * INNER:3 * INNER])
                         for k in range(8)]
                p.wait("t", wvtok)
                ev = [None, None]
                for tt in range(NTT):
                    for ci in range(2):
                        pp = ci % 2
                        p.wait("t", ev[pp])
                        for k in range(8):
                            def thunk(E, tt=tt, ci=ci, k=k, pp=pp):
                                return E.matmul(psq[pp][:, :],
                                                xs[k][:, tt * 128:(tt + 1) * 128],
                                                wq[k][:, ci * 512:(ci + 1) * 512],
                                                start=(k == 0), stop=(k == 7))
                            tok = p.do("t", thunk, tick=(k == 7))
                        ev[pp] = emit_ew(p, "s", lambda E, tt=tt, ci=ci, pp=pp:
                                         E.activation(
                                             vt[tt][:, ci * 512:(ci + 1) * 512],
                                             psq[pp][:, :], AF.Identity), dep=tok)
                p.barrier()
                p._ctx.pop()
                cqkv.close()
                oT = p.sbl("oT", 8, TC, BF16, ctx=c1)
                catt = ExitStack()
                p._ctx.append(catt)
                pts = p.sb("ptscr", [128, 2048], BF16)
                pst = p.ps_("pstr2", [128, 2048], BF16)
                kvt = []
                for j in range(8):
                    kvt.append(p.dma("y", kw_loc[j * 128:(j + 1) * 128, :],
                                     kT[j][:, TC - WIN:TC]))
                for j in range(2):
                    kvt.append(p.dma("y", vw_loc[j * 128:(j + 1) * 128, :],
                                     vt[NTT - 2 + j][:, :]))
                p.wait("g", kvt)
                cck = p.collective("AllGather", ALU.bypass, [kw_loc[:, :]],
                                   [kw_all[:, :]], ngroups=ncores)
                ccv = p.collective("AllGather", ALU.bypass, [vw_loc[:, :]],
                                   [vw_all[:, :]], ngroups=ncores)
                kprev = p.sbl("kprev", 8, WIN, BF16)
                vprev = p.sbl("vprev", 2, INNER, BF16)
                kvscr = [p.sb("kvs0", [128, INNER], BF16),
                         p.sb("kvs1", [128, INNER], BF16)]
                p.wait("y", [cck, ccv])
                kvtok = []
                pktok = None
                for j in range(8):
                    pktok = emit_masked_pick(p, kprev[j], kw_all, j * 128,
                                             INNER, WIN, pcf, kvscr, ncores,
                                             war=pktok)
                    kvtok.append(pktok)
                for j in range(2):
                    pktok = emit_masked_pick(p, vprev[j], vw_all, j * 128,
                                             WIN, INNER, pcf, kvscr, ncores,
                                             war=pktok)
                    kvtok.append(pktok)
                sfb = p.sb("sfb", [128, 4096])
                pfb = p.sb("pfb", [128, 4096], BF16)
                pfb2 = p.sb("pfb2", [128, 4096], BF16)
                srows = p.sb("srows", [128, 48])
                psS_one = p.ps_("psS0", [128, 2048], F32)
                psS = [psS_one, psS_one]
                psO = p.ps_("psO", [128, 1024], F32)
                p.wait("t", kvtok)
                oev = None
                sev = [None, None]
                tok = None
                for wh in range(16):
                    w, half = wh // 2, wh % 2
                    for hg in range(2):
                        for grp in range(2):
                            pp = grp % 2
                            p.wait("t", [sev[0], sev[1]])
                            for hh in range(4):
                                h = hg * 8 + grp * 4 + hh
                                jt, pb = h // 2, (h % 2) * 64
                                if w == 0:
                                    def thunk(E, jt=jt, pb=pb, hh=hh, wh=wh, pp=pp):
                                        return E.matmul(
                                            psS[pp][:, hh * 512:hh * 512 + 256],
                                            qT[jt][pb:pb + 64, wh * 128:(wh + 1) * 128],
                                            kprev[jt][pb:pb + 64, :],
                                            start=True, stop=True)
                                    p.do("t", thunk)

                                    def thunk(E, jt=jt, pb=pb, hh=hh, wh=wh, pp=pp):
                                        return E.matmul(
                                            psS[pp][:, hh * 512 + 256:(hh + 1) * 512],
                                            qT[jt][pb:pb + 64, wh * 128:(wh + 1) * 128],
                                            kT[jt][pb:pb + 64, 0:256],
                                            start=True, stop=True)
                                    tok = p.do("t", thunk, tick=(hh == 3))
                                else:
                                    def thunk(E, jt=jt, pb=pb, hh=hh, w=w, wh=wh, pp=pp):
                                        return E.matmul(
                                            psS[pp][:, hh * 512:(hh + 1) * 512],
                                            qT[jt][pb:pb + 64, wh * 128:(wh + 1) * 128],
                                            kT[jt][pb:pb + 64, w * 256 - 256:w * 256 + 256],
                                            start=True, stop=True)
                                    tok = p.do("t", thunk, tick=(hh == 3))
                            sev[pp] = emit_ew(p, "s", lambda E, grp=grp, pp=pp:
                                              E.activation(
                                                  sfb[:, grp * 2048:(grp + 1) * 2048],
                                                  psS[pp][:, :], AF.Identity,
                                                  scale=8.0), dep=tok)
                        mx = emit_ew(p, "v", lambda E: E.tensor_reduce(
                            srows[:, 0:8], sfb[:, :].rearrange("p (h j) -> p h j", h=8),
                            AX.X, ALU.max, negate=True), dep=[sev[0], sev[1]])
                        for h8 in range(8):
                            emit_ew(p, "v", lambda E, h8=h8: E.tensor_scalar(
                                sfb[:, h8 * 512:(h8 + 1) * 512],
                                sfb[:, h8 * 512:(h8 + 1) * 512],
                                srows[:, h8:h8 + 1], None, ALU.add), sedge=(h8 == 0))
                        add_tok = ("pv", p.cnt["pv"])
                        exp_t = emit_ew(p, "s", lambda E: E.activation(
                            pfb[:, :], sfb[:, :], AF.Exp), dep=add_tok)
                        mvar = 1024 if w == 0 else 0
                        emit_ew(p, "v", lambda E, half=half, mvar=mvar: E.tensor_tensor(
                            pfb2[:, :].rearrange("p (h j) -> p h j", h=8),
                            pfb[:, :].rearrange("p (h j) -> p h j", h=8),
                            mskb[:, mvar + half * 512:mvar + (half + 1) * 512].rearrange(
                                "p (a j) -> p a j", a=1).broadcast_to((128, 8, 512)), ALU.mult),
                            dep=exp_t)
                        emit_ew(p, "v", lambda E: E.tensor_reduce(
                            srows[:, 16:24], pfb2[:, :].rearrange("p (h j) -> p h j", h=8),
                            AX.X, ALU.add), sedge=True)
                        emit_ew(p, "v", lambda E: E.reciprocal(srows[:, 32:40],
                                                               srows[:, 16:24]),
                                sedge=True)
                        for h8 in range(8):
                            emit_ew(p, "v", lambda E, h8=h8: E.tensor_scalar(
                                pfb2[:, h8 * 512:(h8 + 1) * 512],
                                pfb2[:, h8 * 512:(h8 + 1) * 512],
                                srows[:, 32 + h8:33 + h8], None, ALU.mult),
                                sedge=(h8 == 0))
                        pn_tok = ("pv", p.cnt["pv"])
                        p.wait("t", [pn_tok, oev])
                        for h8 in range(8):
                            h = hg * 8 + h8
                            for kc in range(4):
                                def thunk(E, h8=h8, kc=kc, pst=pst):
                                    return E.transpose(
                                        pst[:, ((h8 % 4) * 4 + kc) * 128:((h8 % 4) * 4 + kc + 1) * 128],
                                        pfb2[:, h8 * 512 + kc * 128:h8 * 512 + (kc + 1) * 128],
                                        consts["ident_bf"][:, :])
                                tok = p.do("t", thunk, tick=(kc == 3))
                            ptev = emit_ew(p, "v", lambda E, h8=h8, pst=pst:
                                           E.tensor_copy(
                                               pts[:, (h8 % 4) * 512:((h8 % 4) + 1) * 512],
                                               pst[:, (h8 % 4) * 512:((h8 % 4) + 1) * 512]),
                                           dep=tok)
                            p.wait("t", ptev)
                            jt, pb = h // 2, (h % 2) * 64
                            for kc in range(4):
                                if w == 0:
                                    vsrc = (vprev[kc] if kc < 2 else vt[kc - 2])
                                else:
                                    vsrc = vt[2 * w - 2 + kc]

                                def thunk(E, h=h, h8=h8, kc=kc, jt=jt, pb=pb, vsrc=vsrc):
                                    return E.matmul(
                                        psO[pb:pb + 64, jt * 128:(jt + 1) * 128],
                                        vsrc[:, h * 64:(h + 1) * 64],
                                        pts[:, ((h8 % 4) * 4 + kc) * 128:((h8 % 4) * 4 + kc + 1) * 128],
                                        start=(kc == 0), stop=(kc == 3))
                                tok = p.do("t", thunk, tick=(kc == 3))
                    oev = None
                    for jt in range(8):
                        oev = emit_ew(p, "v", lambda E, jt=jt, wh=wh: E.tensor_copy(
                            oT[jt][:, wh * 128:(wh + 1) * 128],
                            psO[:, jt * 128:(jt + 1) * 128]),
                            dep=tok if jt == 0 else None)
                p.barrier()
                p._ctx.pop()
                catt.close()
                wo = p.sbl("wo", 8, D, BF16, ctx=c1)
                wotok = [p.dma("g", wo[k][:, :],
                               io[f"l1f_wo{li}"][k * 128:(k + 1) * 128, :])
                         for k in range(8)]
                psA = [p.ps_("psA0", [128, 512], F32, ctx=c1),
                       p.ps_("psA1", [128, 512], F32, ctx=c1)]
                aot = p.sb("aot", [128, 512])
                xres = p.sb("xres", [128, 512])
                p.wait("t", wotok)
                aev = [None, None]
                sst = None
                for m in range(8):
                    for ci in range(4):
                        pp = (m * 4 + ci) % 2
                        p.wait("t", aev[pp])
                        for k in range(8):
                            def thunk(E, m=m, ci=ci, k=k, pp=pp):
                                return E.matmul(psA[pp][:, :],
                                                wo[k][:, m * 128:(m + 1) * 128],
                                                oT[k][:, ci * 512:(ci + 1) * 512],
                                                start=(k == 0), stop=(k == 7))
                            tok = p.do("t", thunk, tick=(k == 7))
                        p.wait("y", aev[pp])
                        xld = p.dma("y", xres[:, :],
                                    xin[m * 128:(m + 1) * 128, ci * 512:(ci + 1) * 512])
                        aev[pp] = emit_ew(p, "v", lambda E, pp=pp: E.tensor_tensor(
                            aot[:, :], psA[pp][:, :], xres[:, :], ALU.add),
                            dep=[tok, xld])
                        p.wait("y", [aev[pp], sst])
                        sst = p.dma("y", xat_scr[m * 128:(m + 1) * 128,
                                                 ci * 512:(ci + 1) * 512], aot[:, :])
                        p.wait("v", sst)
            with p.phase(f"l1f_ff{li}") as c2:
                xf = p.sbl("xfb", 8, TC, BF16, ctx=c2)
                cln = ExitStack()
                p._ctx.append(cln)
                xat = p.sbl("xat", 8, TC, F32)
                xtok = [p.dma("y", xat[k][:, :], xat_scr[k * 128:(k + 1) * 128, :])
                        for k in range(8)]
                contrib = p.sb("ctb2", [128, 4])
                halo = p.sb("halo2", [128, 4])
                p.wait("v", xtok)
                ct = None
                for k in range(4):
                    ct = emit_ew(p, "v", lambda E, k=k, contrib=contrib:
                                 E.tensor_copy(
                                     contrib[:, k:k + 1], xat[4 + k][:, TC - 1:TC]))
                p.wait("y", ct)
                cst = p.dma("y", hl_loc[:, :], contrib[:, :])
                p.wait("g", cst)
                cc = p.collective("AllGather", ALU.bypass, [hl_loc[:, :]],
                                  [hl_all[:, :]], ngroups=ncores)
                p.wait("y", cc)
                hz = emit_masked_pick(p, halo, hl_all, 0, 128, 4, pcf, hscr,
                                      ncores)
                xsh = p.sbl("xshf", 4, TC, F32)
                for k in range(4):
                    emit_ew(p, "v", lambda E, k=k: E.tensor_copy(
                        xsh[k][:, 1:TC], xat[4 + k][:, 0:TC - 1]))
                    emit_ew(p, "v", lambda E, k=k, halo=halo: E.tensor_copy(
                        xsh[k][:, 0:1], halo[:, k:k + 1]), dep=hz)
                lnp = p.sb("lnp", [128, 16])
                lnt = [p.dma("y", lnp[:, 0:8], io[f"l1f_lng{li}"][:, :]),
                       p.dma("y", lnp[:, 8:16], io[f"l1f_lnb{li}"][:, :])]
                p.wait("v", lnt)
                p.wait("t", ("pv", p.cnt["pv"]))
                emit_ln_feat(p, [xat[0], xat[1], xat[2], xat[3],
                                 xsh[0], xsh[1], xsh[2], xsh[3]], xf,
                             gamma=[lnp[:, i:i + 1] for i in range(8)],
                             beta=[lnp[:, 8 + i:9 + i] for i in range(8)],
                             eps=1e-5, ones=ones, ones_row=ones_row)
                p.barrier()
                p._ctx.pop()
                cln.close()
                w1 = p.sbl("w1t", 8, 5460, BF16, ctx=c2)
                w1tok = [p.dma("g", w1[k][:, :],
                               io[f"l1f_w1{li}"][k * 128:(k + 1) * 128, :])
                         for k in range(8)]
                w2 = []
                w2tok = []
                for j in range(NJ):
                    rsz = min(128, DI2 - j * 128)
                    w2.append(c2.enter_context(nc.sbuf_tensor(
                        f"w2t{j}_{p.uid()}", [rsz, D], BF16)))
                    w2tok.append(p.dma("g", w2[j][:, :],
                                       io[f"l1f_w2{li}"][j * 128:j * 128 + rsz, :]))
                b1c = p.sb("b1c", [128, 2 * NJ])
                b2c = p.sb("b2c", [128, 8])
                btok = [p.dma("y", b1c[:, :], io[f"l1f_b1{li}"][:, :]),
                        p.dma("y", b2c[:, :], io[f"l1f_b2{li}"][:, :])]
                psh = [p.ps_("psh0", [128, 512], F32, ctx=c2),
                       p.ps_("psh1", [128, 512], F32, ctx=c2)]
                psf = [p.ps_("psf0", [128, 512], F32, ctx=c2),
                       p.ps_("psf1", [128, 512], F32, ctx=c2)]
                ga = p.sb("ga", [128, 512])
                gg = p.sb("gg", [128, 512])
                agb = p.sbl("agb", NJ, 512, BF16, ctx=c2)
                fout = p.sb("fout", [128, 512])
                xres2 = p.sb("xres2", [128, 512])
                xres3 = p.sb("xres3", [128, 512])
                p.wait("t", w1tok)
                p.wait("s", btok)
                hev = [None, None]
                agt = None
                fst = None
                for tq in range(4):
                    for j in range(NJ):
                        rsz = min(128, DI2 - j * 128)
                        for which in (0, 1):
                            c0_ = 256 * j + which * rsz
                            pp = which
                            p.wait("t", hev[pp])
                            for k in range(8):
                                def thunk(E, tq=tq, k=k, pp=pp, c0_=c0_, rsz=rsz):
                                    return E.matmul(
                                        psh[pp][:rsz, :],
                                        w1[k][:, c0_:c0_ + rsz],
                                        xf[k][:, tq * 512:(tq + 1) * 512],
                                        start=(k == 0), stop=(k == 7))
                                tok = p.do("t", thunk, tick=(k == 7))
                            if which == 0:
                                hev[pp] = emit_ew(p, "s", lambda E, j=j, rsz=rsz, pp=pp:
                                                  E.activation(ga[:rsz, :],
                                                               psh[pp][:rsz, :],
                                                               AF.Identity,
                                                               bias=b1c[:rsz, 2 * j:2 * j + 1]),
                                                  dep=[tok, agt])
                            else:
                                hev[pp] = emit_ew(p, "s", lambda E, j=j, rsz=rsz, pp=pp:
                                                  E.activation(gg[:rsz, :],
                                                               psh[pp][:rsz, :],
                                                               AF.Gelu,
                                                               bias=b1c[:rsz, 2 * j + 1:2 * j + 2]),
                                                  dep=[tok, agt])
                        agt = emit_ew(p, "v", lambda E, j=j, rsz=rsz: E.tensor_tensor(
                            agb[j][:rsz, :], ga[:rsz, :], gg[:rsz, :], ALU.mult),
                            dep=[hev[0], hev[1]])
                    p.wait("t", [agt] + (w2tok if tq == 0 else []))
                    fev = [None, None]
                    for m in range(8):
                        pp = m % 2
                        p.wait("t", fev[pp])
                        for j in range(NJ):
                            rsz = min(128, DI2 - j * 128)

                            def thunk(E, tq=tq, m=m, j=j, pp=pp, rsz=rsz):
                                return E.matmul(psf[pp][:, :],
                                                w2[j][:, m * 128:(m + 1) * 128],
                                                agb[j][:rsz, :],
                                                start=(j == 0), stop=(j == NJ - 1))
                            tok = p.do("t", thunk, tick=(j == NJ - 1))
                        sev2 = emit_ew(p, "s", lambda E, m=m, pp=pp: E.activation(
                            ga[:, :] if False else gg[:, :], psf[pp][:, :], AF.Identity,
                            bias=b2c[:, m:m + 1]), dep=tok)
                        p.wait("y", fev[pp])
                        xld = p.dma("y", xres2[:, :],
                                    xat_scr[m * 128:(m + 1) * 128, tq * 512:(tq + 1) * 512])
                        fev[pp] = emit_ew(p, "v", lambda E: E.tensor_tensor(
                            fout[:, :], gg[:, :], xres2[:, :], ALU.add),
                            dep=[sev2, xld])
                        if li == nlayers - 1:
                            p.wait("y", fev[1 - pp])
                            xld2 = p.dma("y", xres3[:, :],
                                         io["res_in"][m * 128:(m + 1) * 128,
                                                      tq * 512:(tq + 1) * 512])
                            fev[pp] = emit_ew(p, "v", lambda E: E.tensor_tensor(
                                fout[:, :], fout[:, :], xres3[:, :], ALU.add),
                                dep=[xld2])
                        p.wait("y", [fev[pp], fst])
                        fst = p.dma("y", xout[m * 128:(m + 1) * 128,
                                             tq * 512:(tq + 1) * 512], fout[:, :])
                        p.wait("v", fst)
                        p.wait("s", fst)
    return None


def build_l1ffn_standalone(ncores=1, core_id=0, nlayers=6):
    nc = bass.Bass("TRN2", target_bir_lowering=False, debug=False)
    io = declare_io(nc, L1FFN_IO)
    io["res_in"] = nc.dram_tensor("res_in", [D, TC], F32, kind="ExternalInput")
    io["res_out"] = nc.dram_tensor("res_out", [D, TC], F32, kind="ExternalOutput")
    p = Prog(nc)
    consts = setup_consts(p)
    block_l1ffn(p, io, consts, ncores, core_id, nlayers)
    p.build()
    return nc



DI = 2048
DS = 16
NM = DI // 128
XT_EXT = 2176
GRID = 8

L0ATTN_IO = {
    "l0a_xext": [D, XT_EXT],
    "l0a_inwT": [D, 2 * DI],
    "l0a_outwT": [DI, D],
    "l0a_xpjTf": [DI, 96], "l0a_xpjTb": [DI, 96],
    "l0a_dtwTf": [64, DI], "l0a_dtwTb": [64, DI],
    "l0a_convf": [128, NM * 4], "l0a_convb": [128, NM * 4],
    "l0a_cbf": [128, NM], "l0a_cbb": [128, NM],
    "l0a_dtbf": [128, NM], "l0a_dtbb": [128, NM],
    "l0a_Alogf": [128, NM * DS], "l0a_Alogb": [128, NM * DS],
    "l0a_Df": [128, NM], "l0a_Db": [128, NM],
    "l0a_ln1g": [128, 8], "l0a_ln1b": [128, 8],
    "l0a_lng": [128, 8], "l0a_lnb": [128, 8],
    "l0a_kanc": [128, 8 * GRID * D], "l0a_kans": [128, 8 * GRID * D],
    "l0a_mseq": [128, 16],
    "l0a_hm": [128, 2],
}


def host_prep_l0attn(inp, core_id, x_full, ncores=8):
    f32 = np.float32
    out = {}
    hm = np.ones((128, 2), f32)
    if core_id == 0:
        hm[:, 0] = 0.0
    if core_id == ncores - 1:
        hm[:, 1] = 0.0
    out["l0a_hm"] = hm
    xe = np.zeros((XT_EXT, D), f32)
    t0 = core_id * TC
    lo, hi = max(0, t0 - 3), min(L, t0 + TC + 3)
    xe[3 - (t0 - lo):3 + (hi - t0)] = x_full[lo:hi]
    out["l0a_xext"] = np.ascontiguousarray(xe.T)
    out["l0a_inwT"] = np.ascontiguousarray(inp["L0_attn_in_proj_w"].T.astype(f32))
    out["l0a_outwT"] = np.ascontiguousarray(inp["L0_attn_out_proj_w"].T.astype(f32))
    for (suf, S) in (("f", "_f"), ("b", "_b")):
        out[f"l0a_xpjT{suf}"] = np.ascontiguousarray(
            inp[f"L0_attn_x_proj_w{S}"].T.astype(f32))
        out[f"l0a_dtwT{suf}"] = np.ascontiguousarray(
            inp[f"L0_attn_dt_w{S}"].T.astype(f32))
        out[f"l0a_conv{suf}"] = np.ascontiguousarray(
            inp[f"L0_attn_conv_w{S}"].astype(f32).reshape(NM, 128, 4)
            .transpose(1, 0, 2).reshape(128, NM * 4))
        out[f"l0a_cb{suf}"] = np.ascontiguousarray(
            inp[f"L0_attn_conv_b{S}"].astype(f32).reshape(NM, 128).T)
        out[f"l0a_dtb{suf}"] = np.ascontiguousarray(
            inp[f"L0_attn_dt_b{S}"].astype(f32).reshape(NM, 128).T)
        out[f"l0a_Alog{suf}"] = np.ascontiguousarray(
            inp[f"L0_attn_A_log{S}"].astype(f32).reshape(NM, 128, DS)
            .transpose(1, 0, 2).reshape(128, NM * DS))
        out[f"l0a_D{suf}"] = np.ascontiguousarray(
            inp[f"L0_attn_D{S}"].astype(f32).reshape(NM, 128).T)
    out["l0a_ln1g"] = np.ascontiguousarray(
        inp["L0_attn_ln1_g"].astype(f32).reshape(8, 128).T)
    out["l0a_ln1b"] = np.ascontiguousarray(
        inp["L0_attn_ln1_b"].astype(f32).reshape(8, 128).T)
    out["l0a_lng"] = np.ascontiguousarray(
        inp["L0_attn_ln_g"].astype(f32).reshape(8, 128).T)
    out["l0a_lnb"] = np.ascontiguousarray(
        inp["L0_attn_ln_b"].astype(f32).reshape(8, 128).T)
    coef = inp["L0_attn_kan_coef"].astype(f32)
    for c_, nm_ in ((coef[0], "l0a_kanc"), (coef[1], "l0a_kans")):
        t = c_.reshape(8, 128, 8, 128, GRID).transpose(3, 0, 4, 2, 1)
        out[nm_] = np.ascontiguousarray(t.reshape(128, 8 * GRID * 8 * 128))
    ms = np.zeros((128, 16), f32)
    for cc in range(8):
        ms[:, cc] = 1.0 if cc < core_id else 0.0
        ms[:, 8 + cc] = 1.0 if cc > core_id else 0.0
    out["l0a_mseq"] = ms
    return out


def block_l0attn(p, io, consts, ncores, core_id=0):
    nc = p.nc
    ones, ones_row, ones_row_bf = consts["ones"], consts["ones_row"], consts["ones_row_bf"]
    MAG = MAGIC
    inv2pi = 1.0 / (2 * np.pi)
    c1w = float(np.float32(6.28125))
    c2w = float(np.float32(2 * np.pi - 6.28125))
    with p.phase("l0attn") as ctx:
        aspace = "Shared" if ncores > 4 else "Local"
        xcf_d = nc.dram_tensor(f"l0a_xcf_{p.uid()}", [DI, TC], BF16)
        xcb_d = nc.dram_tensor(f"l0a_xcb_{p.uid()}", [DI, TC], BF16)
        sz_d = nc.dram_tensor(f"l0a_sz_{p.uid()}", [DI, TC], BF16)
        dtf_d = nc.dram_tensor(f"l0a_dtf_{p.uid()}", [DI, TC], BF16)
        dtb_d = nc.dram_tensor(f"l0a_dtb_{p.uid()}", [DI, TC], BF16)
        y1f_d = nc.dram_tensor(f"l0a_y1f_{p.uid()}", [DI, TC], F32)
        y1b_d = nc.dram_tensor(f"l0a_y1b_{p.uid()}", [DI, TC], F32)
        yg_d = nc.dram_tensor(f"l0a_yg_{p.uid()}", [2 * DI, TC], BF16)
        st_loc = nc.dram_tensor(f"l0a_stl_{p.uid()}", [128, 1024], F32)
        st_all = nc.dram_tensor(f"l0a_sta_{p.uid()}", [ncores * 128, 1024], F32,
                                addr_space=aspace)
        base_d = nc.dram_tensor(f"l0a_base_{p.uid()}", [D, TC], F32)
        nc._dbg_drams = {"xcf": xcf_d, "xcb": xcb_d, "sz": sz_d, "dtf": dtf_d,
                         "dtb": dtb_d, "y1f": y1f_d, "y1b": y1b_d, "yg": yg_d,
                         "base": base_d, "st": st_loc}
        selarr = np.zeros((96, 32 * 128), BF16NP)
        for s_ in range(32):
            selarr[64 + s_, s_ * 128:(s_ + 1) * 128] = 1
        seld = p.const("bcsel", selarr)
        sel = p.sb("bcsel", [96, 32 * 128], BF16)
        p.dma("y", sel[:, :], seld[:])
        xn_ctx = ExitStack()
        xn = p.sbl("l0a_xn", 8, XT_EXT, BF16, ctx=xn_ctx)
        with p.phase("ph0") as c0:
            x32, xtok = load_tiles(p, io["l0a_xext"], D, XT_EXT, F32,
                                   name="l0a_x", ctx=c0)
            t1 = p.sbl("l0a_t1", 8, XT_EXT, F32, ctx=c0)
            for e in ("t", "s", "v"):
                p.wait(e, xtok)
            emit_ln_feat(p, x32, t1, ones=ones, ones_row=ones_row, eps=1e-5,
                         TCH=544)
            bstage = p.sb("l0a_bst", [128, TC])
            bst = None
            for k in range(8):
                bv = emit_ew(p, "v", lambda E, k=k: E.tensor_tensor(
                    bstage[:, :], x32[k][:, 3:3 + TC], t1[k][:, 3:3 + TC],
                    ALU.add), dep=bst)
                p.wait("y", bv)
                bst = p.dma("y", base_d[k * 128:(k + 1) * 128, :], bstage[:, :])
            p.wait("v", bst)
            lnp = p.sb("l0a_lnp", [128, 16])
            lt = [p.dma("y", lnp[:, 0:8], io["l0a_ln1g"][:, :]),
                  p.dma("y", lnp[:, 8:16], io["l0a_ln1b"][:, :])]
            p.wait("v", lt)
            emit_ln_feat(p, t1, xn, gamma=[lnp[:, i:i + 1] for i in range(8)],
                         beta=[lnp[:, 8 + i:9 + i] for i in range(8)],
                         eps=1e-12, ones=ones, ones_row=ones_row, TCH=544)
            hm = p.sb("l0a_hmv", [128, 2])
            hmb = p.sb("l0a_hmb", [128, 2], BF16)
            hmt = p.dma("y", hm[:, :], io["l0a_hm"][:, :])
            emit_ew(p, "v", lambda E: E.tensor_copy(hmb[:, :], hm[:, :]),
                    dep=hmt)
            for k in range(8):
                emit_ew(p, "v", lambda E, k=k: E.tensor_tensor(
                    xn[k][:, 0:3], xn[k][:, 0:3],
                    hmb[:, 0:1].broadcast_to((128, 3)), ALU.mult))
                emit_ew(p, "v", lambda E, k=k: E.tensor_tensor(
                    xn[k][:, 3 + TC:6 + TC], xn[k][:, 3 + TC:6 + TC],
                    hmb[:, 1:2].broadcast_to((128, 3)), ALU.mult))
        with p.phase("ph1") as c1:
            wt = p.sbl("l0a_wt", 8, 2 * DI, BF16, ctx=c1)
            wtok = [p.dma("g", wt[k][:, :], io["l0a_inwT"][k * 128:(k + 1) * 128, :])
                    for k in range(8)]
            cw = p.sb("l0a_cw", [128, 2 * NM * 4])
            cb = p.sb("l0a_cb", [128, 2 * NM])
            ctok = [p.dma("y", cw[:, 0:NM * 4], io["l0a_convf"][:, :]),
                    p.dma("y", cw[:, NM * 4:], io["l0a_convb"][:, :]),
                    p.dma("y", cb[:, 0:NM], io["l0a_cbf"][:, :]),
                    p.dma("y", cb[:, NM:], io["l0a_cbb"][:, :])]
            psx = [p.ps_("l0a_px0", [128, 512], F32, ctx=c1),
                   p.ps_("l0a_px1", [128, 512], F32, ctx=c1)]
            xcp = p.sb("l0a_xcp", [128, XT_EXT])
            acc = p.sb("l0a_acc", [128, TC])
            stg = p.sb("l0a_stg", [128, TC], BF16)
            p.wait("t", wtok + ctok)
            p.wait("s", ctok)
            p.wait("v", ctok)
            ev = [None, None]
            dst = None
            for m in range(2 * NM):
                width = XT_EXT if m < NM else TC
                cofs = 3 if m >= NM else 0
                nch = cdiv(width, 512)
                for ci in range(nch):
                    n0 = ci * 512
                    nsz = min(512, width - n0)
                    pp = ci % 2
                    p.wait("t", ev[pp])
                    for k in range(8):
                        def thunk(E, m=m, k=k, n0=n0, nsz=nsz, pp=pp, cofs=cofs):
                            return E.matmul(psx[pp][:, 0:nsz],
                                            wt[k][:, m * 128:(m + 1) * 128],
                                            xn[k][:, cofs + n0:cofs + n0 + nsz],
                                            start=(k == 0), stop=(k == 7))
                        tok = p.do("t", thunk, tick=(k == 7))
                    if m < NM:
                        ev[pp] = emit_ew(p, "s", lambda E, n0=n0, nsz=nsz, pp=pp:
                                         E.activation(xcp[:, n0:n0 + nsz],
                                                      psx[pp][:, 0:nsz],
                                                      AF.Identity), dep=tok)
                    else:
                        ev[pp] = emit_ew(p, "s", lambda E, n0=n0, nsz=nsz, pp=pp:
                                         E.activation(stg[:, n0:n0 + nsz],
                                                      psx[pp][:, 0:nsz],
                                                      AF.Silu), dep=tok)
                if m >= NM:
                    p.wait("y", [ev[0], ev[1], dst])
                    dst = p.dma("y", sz_d[(m - NM) * 128:(m - NM + 1) * 128, :],
                                stg[:, :])
                    p.wait("s", dst)
                    continue
                cv = emit_ew(p, "v", lambda E, m=m: E.tensor_scalar(
                    acc[:, :], xcp[:, 0:TC], cw[:, m * 4:m * 4 + 1], None,
                    ALU.mult), dep=[ev[0], ev[1]])
                for k in range(1, 4):
                    cv = emit_ew(p, "v", lambda E, m=m, k=k: E.scalar_tensor_tensor(
                        acc[:, :], xcp[:, k:k + TC], cw[:, m * 4 + k:m * 4 + k + 1],
                        acc[:, :], ALU.mult, ALU.add))
                sv = emit_ew(p, "s", lambda E, m=m: E.activation(
                    stg[:, :], acc[:, :], AF.Silu, bias=cb[:, m:m + 1]), dep=cv)
                p.wait("y", [sv, dst])
                dst = p.dma("y", xcf_d[m * 128:(m + 1) * 128, :], stg[:, :])
                p.wait("v", dst)
                cv = emit_ew(p, "v", lambda E, m=m: E.tensor_scalar(
                    acc[:, :], xcp[:, 2053:5:-1], cw[:, NM * 4 + m * 4:NM * 4 + m * 4 + 1],
                    None, ALU.mult))
                for k in range(1, 4):
                    cv = emit_ew(p, "v", lambda E, m=m, k=k: E.scalar_tensor_tensor(
                        acc[:, :], xcp[:, 2053 - k:5 - k:-1],
                        cw[:, NM * 4 + m * 4 + k:NM * 4 + m * 4 + k + 1],
                        acc[:, :], ALU.mult, ALU.add))
                sv = emit_ew(p, "s", lambda E, m=m: E.activation(
                    stg[:, :], acc[:, :], AF.Silu, bias=cb[:, NM + m:NM + m + 1]),
                    dep=cv)
                p.wait("y", [sv, dst])
                dst = p.dma("y", xcb_d[m * 128:(m + 1) * 128, :], stg[:, :])
                p.wait("s", dst)
                p.wait("v", dst)
        p.barrier()
        xn_ctx.close()
        bcmat = p.sb("l0a_bc", [96, 2 * TC], BF16, ctx=ctx)
        with p.phase("ph2") as c2:
            psd = p.ps_("l0a_psd", [96, TC], F32, ctx=c2)
            psm = [p.ps_("l0a_pm0", [128, 512], F32, ctx=c2),
                   p.ps_("l0a_pm1", [128, 512], F32, ctx=c2)]
            xpj = p.sbl("l0a_xpj", NM, 96, BF16, ctx=c2)
            dtw = p.sb("l0a_dtw", [64, DI], BF16)
            dtbc = p.sb("l0a_dtbc", [128, 2 * NM])
            xck = [p.sb("l0a_xck0", [128, TC], BF16), p.sb("l0a_xck1", [128, TC], BF16)]
            xdb = p.sb("l0a_xdb", [96, TC])
            spa = p.sb("l0a_spa", [128, TC])
            spb = p.sb("l0a_spb", [128, TC])
            stg2 = p.sb("l0a_stg2", [128, TC], BF16)
            dcons = [None, None]
            dprev = []
            for di_, (xc_d, dt_d, xpj_io, dtw_io, dtb_c) in enumerate(
                    ((xcf_d, dtf_d, "l0a_xpjTf", "l0a_dtwTf", 0),
                     (xcb_d, dtb_d, "l0a_xpjTb", "l0a_dtwTb", NM))):
                p.wait("g", [dcons[0], dcons[1]] + dprev)
                xptok = [p.dma("g", xpj[k][:, :],
                               io[xpj_io][k * 128:(k + 1) * 128, :])
                         for k in range(NM)]
                wtok2 = [p.dma("g", dtw[:, :], io[dtw_io][:, :])]
                btok2 = [p.dma("y", dtbc[:, di_ * NM:(di_ + 1) * NM],
                               io["l0a_dtbf" if di_ == 0 else "l0a_dtbb"][:, :])]
                p.wait("t", xptok)
                xld = [None, None]
                cons = [dcons[0], dcons[1]]
                for k in range(NM):
                    pp = k % 2
                    p.wait("y", cons[pp])
                    xld[pp] = p.dma("y", xck[pp][:, :],
                                    xc_d[k * 128:(k + 1) * 128, :])
                    p.wait("t", xld[pp])
                    for n0 in range(0, TC, 512):
                        def thunk(E, k=k, n0=n0, pp=pp):
                            return E.matmul(psd[:, n0:n0 + 512], xpj[k][:, :],
                                            xck[pp][:, n0:n0 + 512],
                                            start=(k == 0), stop=(k == NM - 1))
                        tok = p.do("t", thunk, tick=(n0 + 512 >= TC))
                    cons[pp] = tok
                xdt = emit_ew(p, "v", lambda E: E.tensor_copy(xdb[:, :], psd[:, :]),
                              dep=tok)
                bct = emit_ew(p, "v", lambda E, di_=di_: E.tensor_copy(
                    bcmat[:, di_ * TC:(di_ + 1) * TC], xdb[:, :]))
                p.wait("t", [xdt, bct] + wtok2)
                p.wait("s", btok2)
                dst2 = None
                rt = [None, None]
                for m in range(NM):
                    for ci, n0 in enumerate(range(0, TC, 512)):
                        pp = ci % 2
                        p.wait("t", rt[pp])
                        def thunk(E, m=m, n0=n0, pp=pp, di_=di_):
                            return E.matmul(psm[pp][:, :], dtw[:, m * 128:(m + 1) * 128],
                                            bcmat[0:64, di_ * TC + n0:di_ * TC + n0 + 512],
                                            start=True, stop=True)
                        tok = p.do("t", thunk, tick=True)
                        a1 = emit_ew(p, "s", lambda E, m=m, n0=n0, pp=pp, di_=di_:
                                     E.activation(spa[:, n0:n0 + 512], psm[pp][:, :],
                                                  AF.Abs,
                                                  bias=dtbc[:, di_ * NM + m:di_ * NM + m + 1]),
                                     dep=tok)
                        r1 = emit_ew(p, "s", lambda E, m=m, n0=n0, pp=pp, di_=di_:
                                     E.activation(spb[:, n0:n0 + 512], psm[pp][:, :],
                                                  AF.Relu,
                                                  bias=dtbc[:, di_ * NM + m:di_ * NM + m + 1]))
                        rt[pp] = r1
                    e1 = emit_ew(p, "s", lambda E: E.activation(
                        spa[:, :], spa[:, :], AF.Exp, scale=-1.0), sedge=True)
                    l1 = emit_ew(p, "s", lambda E: E.activation(
                        spa[:, :], spa[:, :], AF.Ln, bias=1.0), sedge=True)
                    dt_t = emit_ew(p, "v", lambda E: E.tensor_tensor(
                        stg2[:, :], spa[:, :], spb[:, :], ALU.add), dep=l1)
                    p.wait("y", dt_t)
                    dst2 = p.dma("y", dt_d[m * 128:(m + 1) * 128, :], stg2[:, :])
                    p.wait("s", dst2)
                    p.wait("v", dst2)
                dcons = [cons[0], cons[1]]
                dprev = [rt[0], rt[1]]
        _l0a_scan_and_gate(p, io, consts, ncores, core_id,
                           (xcf_d, xcb_d, sz_d, dtf_d, dtb_d, y1f_d, y1b_d,
                            yg_d, st_loc, st_all, bcmat, sel), ctx)
        _l0a_out_kan(p, io, consts, ncores, core_id, (yg_d, base_d))
    return None


def _l0a_scan_and_gate(p, io, consts, ncores, core_id, drams, ctx):
    nc = p.nc
    ones, ones_row = consts["ones"], consts["ones_row"]
    (xcf_d, xcb_d, sz_d, dtf_d, dtb_d, y1f_d, y1b_d, yg_d, st_loc, st_all,
     bcmat, sel) = drams
    onesbf_d = p.const("onesbf2k", np.ones((128, TC), BF16NP))
    Acol = p.sb("l0a_A", [128, 2 * NM * DS], ctx=ctx)
    at = [p.dma("y", Acol[:, 0:NM * DS], io["l0a_Alogf"][:, :]),
          p.dma("y", Acol[:, NM * DS:], io["l0a_Alogb"][:, :])]
    mseq = p.sb("l0a_mseq", [128, 32], ctx=ctx)
    mt = p.dma("y", mseq[:, 0:16], io["l0a_mseq"][:, :])
    iv = emit_ew(p, "v", lambda E: E.tensor_scalar(
        mseq[:, 16:32], mseq[:, 0:16], -1.0, 1.0, ALU.mult, ALU.add), dep=mt)
    p.wait("s", at)
    ax = emit_ew(p, "s", lambda E: E.activation(Acol[:, :], Acol[:, :], AF.Exp))
    av = emit_ew(p, "v", lambda E: E.tensor_scalar_mul(Acol[:, :], Acol[:, :],
                                                       -1.0), dep=ax)
    Fst = p.sb("l0a_F", [128, 512], ctx=ctx)
    Pst = p.sb("l0a_P", [128, 512], ctx=ctx)
    Hst = p.sb("l0a_H", [128, 512], ctx=ctx)
    p.barrier()
    dirs = ((0, xcf_d, dtf_d, y1f_d, 0), (1, xcb_d, dtb_d, y1b_d, NM * DS))

    def bc_bcast(psb2, be, ce, s_, di_, dep):
        p.wait("t", dep)
        for (dst, srow) in ((be, s_), (ce, DS + s_)):
            for n0 in range(0, TC, 512):
                def thunk(E, srow=srow, n0=n0, di_=di_):
                    return E.matmul(psb2[:, n0:n0 + 512],
                                    sel[:, srow * 128:(srow + 1) * 128],
                                    bcmat[:, di_ * TC + n0:di_ * TC + n0 + 512],
                                    start=True, stop=True)
                tok = p.do("t", thunk, tick=(n0 + 512 >= TC))
            ev = emit_ew(p, "v", lambda E, dst=dst: E.tensor_copy(dst[:, :],
                                                                  psb2[:, :]),
                         dep=tok)
            p.wait("t", ev)
        return ev

    for di_, xc_d, dt_d, y1_d, aofs in dirs:
        for half in range(2):
            with p.phase(f"swA{di_}{half}") as ch:
                m0 = half * (NM // 2)
                dt = p.sbl("sa_dt", 8, TC, BF16)
                u = p.sbl("sa_u", 8, TC, BF16)
                yt = p.sbl("sa_y", 8, TC, F32)
                be = p.sb("sa_be", [128, TC], BF16)
                ce = p.sb("sa_ce", [128, TC], BF16)
                da = p.sb("sa_da", [128, TC], BF16)
                dbx = p.sb("sa_dbx", [128, TC], BF16)
                hh = p.sb("sa_h", [128, TC], BF16)
                prod = dbx
                psb2 = p.ps_("sa_ps", [128, TC], F32)
                ldt = []
                xcl = None
                for j in range(8):
                    ldt.append(p.dma("y", dt[j][:, :],
                                     dt_d[(m0 + j) * 128:(m0 + j + 1) * 128, :]))
                for j in range(8):
                    p.wait("y", xcl)
                    xt2 = p.dma("y", hh[:, :],
                                xc_d[(m0 + j) * 128:(m0 + j + 1) * 128, :])
                    xcl = emit_ew(p, "v", lambda E, j=j, u=u, dt=dt, hh=hh:
                                  E.tensor_tensor(
                                      u[j][:, :], dt[j][:, :], hh[:, :], ALU.mult),
                                  dep=[xt2, ldt[j]])
                    emit_ew(p, "v", lambda E, j=j, yt=yt: E.memset(yt[j][:, :], 0.0))
                uv = ("pv", p.cnt["pv"])
                bev = [uv]
                sc = None
                for s_ in range(DS):
                    ev = bc_bcast(psb2, be, ce, s_, di_, bev)
                    bev = []
                    for j in range(8):
                        m = m0 + j
                        acl = Acol[:, aofs + m * DS + s_:aofs + m * DS + s_ + 1]
                        ad = emit_ew(p, "s", lambda E, j=j, acl=acl, da=da, dt=dt:
                                     E.activation(
                                         da[:, :], dt[j][:, :], AF.Exp, scale=acl),
                                     dep=[av, ev, sc])
                        dbv = emit_ew(p, "v", lambda E, j=j, dbx=dbx, u=u, be=be:
                                      E.tensor_tensor(
                                          dbx[:, :], u[j][:, :], be[:, :],
                                          ALU.mult), dep=ad)
                        sc = emit_ew(p, "v", lambda E, hh=hh, da=da, dbx=dbx:
                                     E.tensor_tensor_scan(
                                         hh[:, :], da[:, :], dbx[:, :], 0.0,
                                         ALU.mult, ALU.add))
                        fc = emit_ew(p, "v", lambda E, m=m, s_=s_, di_=di_, hh=hh:
                                     E.tensor_copy(
                                         Fst[:, di_ * 256 + m * DS + s_:
                                             di_ * 256 + m * DS + s_ + 1],
                                         hh[:, TC - 1:TC]))
                        pv_ = emit_ew(p, "v", lambda E, prod=prod, hh=hh, ce=ce:
                                      E.tensor_tensor(
                                          prod[:, :], hh[:, :], ce[:, :],
                                          ALU.mult))
                        ya = emit_ew(p, "v", lambda E, j=j, yt=yt, prod=prod:
                                     E.tensor_tensor(
                                         yt[j][:, :], yt[j][:, :], prod[:, :],
                                         ALU.add))
                        bev.append(ya)
                        bev.append(ad)
                sdc = p.sb("sa_sd", [128, 2])
                for j in range(8):
                    m = m0 + j
                    sdv = emit_ew(p, "v", lambda E, j=j, sdc=sdc, dt=dt:
                                  E.tensor_reduce(
                                      sdc[:, 0:1], dt[j][:, :], AX.X, ALU.add))
                    for s_ in range(DS):
                        acl = Acol[:, aofs + m * DS + s_:aofs + m * DS + s_ + 1]
                        tsv = emit_ew(p, "v", lambda E, acl=acl, sdc=sdc:
                                      E.tensor_scalar(
                                          sdc[:, 1:2], sdc[:, 0:1], acl, None,
                                          ALU.mult), sedge=True)
                        emit_ew(p, "s", lambda E, m=m, s_=s_, di_=di_, sdc=sdc:
                                E.activation(
                                    Pst[:, di_ * 256 + m * DS + s_:
                                        di_ * 256 + m * DS + s_ + 1],
                                    sdc[:, 1:2], AF.Exp), dep=tsv)
                        p.wait("v", ("ps", p.cnt["ps"]))
                st = None
                for j in range(8):
                    p.wait("y", ("pv", p.cnt["pv"]))
                    st = p.dma("y", y1_d[(m0 + j) * 128:(m0 + j + 1) * 128, :],
                               yt[j][:, :])
                p.wait("v", st)
                p.wait("s", st)
    with p.phase("l0a_comb") as cc0:
        stg = p.sb("cb_stg", [128, 1024])
        cpv = emit_ew(p, "v", lambda E, stg=stg: E.tensor_copy(stg[:, 0:512],
                                                               Fst[:, :]))
        cpv = emit_ew(p, "v", lambda E, stg=stg: E.tensor_copy(stg[:, 512:1024],
                                                               Pst[:, :]))
        p.wait("y", cpv)
        cst = p.dma("y", st_loc[:, :], stg[:, :])
        p.wait("g", cst)
        cc = p.collective("AllGather", ALU.bypass, [st_loc[:, :]], [st_all[:, :]],
                          ngroups=ncores)
        p.wait("y", cc)
        blk = p.sb("cb_blk", [128, 1024])
        tmp = p.sb("cb_tmp", [128, 256])
        tmp2 = p.sb("cb_tmp2", [128, 256])
        hz = emit_ew(p, "v", lambda E: E.memset(Hst[:, :], 0.0))
        for di_ in range(2):
            ccs = range(ncores) if di_ == 0 else range(ncores - 1, -1, -1)
            for cc_ in ccs:
                bt = p.dma("y", blk[:, :], st_all[cc_ * 128:(cc_ + 1) * 128, :])
                mcol = mseq[:, di_ * 8 + cc_:di_ * 8 + cc_ + 1]
                imcol = mseq[:, 16 + di_ * 8 + cc_:16 + di_ * 8 + cc_ + 1]
                pm = emit_ew(p, "v", lambda E, di_=di_, mcol=mcol, imcol=imcol:
                             E.scalar_tensor_tensor(
                                 tmp[:, :], blk[:, 512 + di_ * 256:768 + di_ * 256],
                                 mcol, imcol.broadcast_to((128, 256)),
                                 ALU.mult, ALU.add), dep=[bt, iv])
                emit_ew(p, "v", lambda E, di_=di_: E.tensor_tensor(
                    tmp2[:, :], Hst[:, 0:256] if di_ == 0 else Hst[:, 256:512],
                    tmp[:, :], ALU.mult))
                emit_ew(p, "v", lambda E, di_=di_, mcol=mcol: E.tensor_scalar(
                    tmp[:, :], blk[:, di_ * 256:(di_ + 1) * 256], mcol, None,
                    ALU.mult))
                hv = emit_ew(p, "v", lambda E, di_=di_: E.tensor_tensor(
                    Hst[:, di_ * 256:(di_ + 1) * 256], tmp2[:, :], tmp[:, :],
                    ALU.add))
                p.wait("y", hv)
    Dcol = p.sb("l0a_Dc", [128, 2 * NM], ctx=ctx)
    dtk = [p.dma("y", Dcol[:, 0:NM], io["l0a_Df"][:, :]),
           p.dma("y", Dcol[:, NM:], io["l0a_Db"][:, :])]
    for di_, xc_d, dt_d, y1_d, aofs in dirs:
        for half in range(2):
            with p.phase(f"swB{di_}{half}") as ch:
                m0 = half * (NM // 2)
                dt = p.sbl("sb_dt", 8, TC, BF16)
                yt = p.sbl("sb_y", 8, TC, F32)
                ct = p.sbl("sb_ct", 8, TC, F32)
                be = p.sb("sb_be", [128, TC], BF16)
                ce = p.sb("sb_ce", [128, TC], BF16)
                dec = p.sb("sb_dec", [128, TC], BF16)
                hc = p.sb("sb_hc", [128, TC], BF16)
                prod = dec
                stg = ce
                psb2 = p.ps_("sb_ps", [128, TC], F32)
                onesbf = be
                ot1 = p.dma("y", onesbf[:, :], onesbf_d[:])
                ldt = []
                for j in range(8):
                    ldt.append(p.dma("y", dt[j][:, :],
                                     dt_d[(m0 + j) * 128:(m0 + j + 1) * 128, :]))
                    ldt.append(p.dma("y", yt[j][:, :],
                                     y1_d[(m0 + j) * 128:(m0 + j + 1) * 128, :]))
                p.wait("v", ldt + [ot1])
                for j in range(8):
                    emit_ew(p, "v", lambda E, j=j, ct=ct, onesbf=onesbf, dt=dt:
                            E.tensor_tensor_scan(
                                ct[j][:, :], onesbf[:, :], dt[j][:, :], 0.0,
                                ALU.mult, ALU.add))
                ctv = ("pv", p.cnt["pv"])
                bev = [ctv]
                ya = None
                for s_ in range(DS):
                    ev = bc_bcast(psb2, be, ce, s_, di_, bev)
                    bev = []
                    for j in range(8):
                        m = m0 + j
                        acl = Acol[:, aofs + m * DS + s_:aofs + m * DS + s_ + 1]
                        hcol = Hst[:, di_ * 256 + m * DS + s_:
                                   di_ * 256 + m * DS + s_ + 1]
                        dv = emit_ew(p, "s", lambda E, j=j, acl=acl, dec=dec, ct=ct:
                                     E.activation(
                                         dec[:, :], ct[j][:, :], AF.Exp, scale=acl),
                                     dep=[av, ev, ctv, ya])
                        hv = emit_ew(p, "v", lambda E, hcol=hcol, hc=hc, dec=dec:
                                     E.tensor_scalar(
                                         hc[:, :], dec[:, :], hcol, None,
                                         ALU.mult), dep=dv)
                        pv_ = emit_ew(p, "v", lambda E, prod=prod, hc=hc, ce=ce:
                                      E.tensor_tensor(
                                          prod[:, :], hc[:, :], ce[:, :],
                                          ALU.mult))
                        ya = emit_ew(p, "v", lambda E, j=j, yt=yt, prod=prod:
                                     E.tensor_tensor(
                                         yt[j][:, :], yt[j][:, :], prod[:, :],
                                         ALU.add))
                        bev.append(ya)
                        bev.append(dv)
                p.wait("y", bev)
                st = None
                g2 = None
                for j in range(8):
                    m = m0 + j
                    dcl = Dcol[:, di_ * NM + m:di_ * NM + m + 1]
                    p.wait("y", g2)
                    xld = p.dma("y", dec[:, :],
                                xc_d[(m0 + j) * 128:(m0 + j + 1) * 128, :])
                    zld = p.dma("y", hc[:, :],
                                sz_d[(m0 + j) * 128:(m0 + j + 1) * 128, :])
                    g1 = emit_ew(p, "v", lambda E, j=j, dcl=dcl, be=be, dec=dec,
                                 yt=yt: E.scalar_tensor_tensor(
                                     be[:, :], dec[:, :], dcl, yt[j][:, :],
                                     ALU.mult, ALU.add),
                                 dep=list(dtk) + [xld, zld])
                    g2 = emit_ew(p, "v", lambda E, j=j, stg=stg, be=be, hc=hc,
                                 di_=di_: E.tensor_tensor(
                                     stg[:, :], be[:, :],
                                     hc[:, ::-1] if di_ == 1 else hc[:, :],
                                     ALU.mult))
                    p.wait("y", [g2, st])
                    st = p.dma("y", yg_d[(di_ * NM + m) * 128:(di_ * NM + m + 1) * 128, :],
                               stg[:, :])
                    p.wait("v", st)
    return None


def _l0a_out_kan(p, io, consts, ncores, core_id, drams):
    nc = p.nc
    ones, ones_row = consts["ones"], consts["ones_row"]
    (yg_d, base_d) = drams
    MAG = MAGIC
    inv2pi = float(1.0 / (2 * np.pi))
    c1w = float(np.float32(6.28125))
    c2w = float(np.float32(2 * np.pi - 6.28125))
    with p.phase("l0a_out") as c4:
        vT = p.sbl("l0a_vT", 8, TC, F32, ctx=c4)
        with p.phase("l0a_op") as c5:
            ot = p.sbl("l0a_ot", 8, TC, F32)
            cop = ExitStack()
            p._ctx.append(cop)
            ow = p.sbl("l0a_ow", NM, D, BF16)
            owt = [p.dma("g", ow[k][:, :], io["l0a_outwT"][k * 128:(k + 1) * 128, :])
                   for k in range(NM)]
            ygs = p.sbl("l0a_ygs", NM, 512, BF16)
            psu = p.ps_("l0a_pu", [128, 8 * 512], F32)
            p.wait("t", owt)
            ev = []
            for tq in range(4):
                mmtok = None
                for half in range(2):
                    ygld = []
                    p.wait("y", ev + ([mmtok] if mmtok else []))
                    for k in range(NM):
                        if half == 0:
                            ygld.append(p.dma("y", ygs[k][:, :],
                                              yg_d[k * 128:(k + 1) * 128,
                                                   tq * 512:(tq + 1) * 512]))
                        else:
                            r0 = TC - (tq + 1) * 512
                            ygld.append(p.dma("y", ygs[k][:, :],
                                              yg_d[(NM + k) * 128:(NM + k + 1) * 128,
                                                   r0:r0 + 512]))
                    p.wait("t", ygld)
                    ev = []
                    for mo in range(8):
                        for k in range(NM):
                            def thunk(E, mo=mo, k=k, half=half):
                                rhs = ygs[k][:, :] if half == 0 else ygs[k][:, ::-1]
                                return E.matmul(psu[:, mo * 512:(mo + 1) * 512],
                                                ow[k][:, mo * 128:(mo + 1) * 128],
                                                rhs, start=(half == 0 and k == 0),
                                                stop=(half == 1 and k == NM - 1))
                            tok = p.do("t", thunk, tick=(k == NM - 1))
                        if half == 1:
                            ev.append(emit_ew(
                                p, "s", lambda E, tq=tq, mo=mo:
                                E.activation(ot[mo][:, tq * 512:(tq + 1) * 512],
                                             psu[:, mo * 512:(mo + 1) * 512],
                                             AF.Identity), dep=tok))
                    mmtok = tok
            p.barrier()
            p._ctx.pop()
            cop.close()
            lnp = p.sb("l0a_lnp2", [128, 16])
            lt = [p.dma("y", lnp[:, 0:8], io["l0a_lng"][:, :]),
                  p.dma("y", lnp[:, 8:16], io["l0a_lnb"][:, :])]
            p.wait("v", lt + ev)
            p.wait("t", ev)
            p.wait("s", ev)
            emit_ln_feat(p, ot, vT, gamma=[lnp[:, i:i + 1] for i in range(8)],
                         beta=[lnp[:, 8 + i:9 + i] for i in range(8)],
                         eps=1e-12, ones=ones, ones_row=ones_row)
        cs = p.sbl("l0a_cs", 8 * GRID, 512, BF16, ctx=c4)
        kslc = [p.sb("l0a_ks0", [128, 64 * 128], BF16, ctx=c4),
                p.sb("l0a_ks1", [128, 64 * 128], BF16, ctx=c4)]
        tbuf = p.sb("l0a_tb", [128, 512], ctx=c4)
        kbuf = p.sb("l0a_kb", [128, 512], ctx=c4)
        abuf = p.sb("l0a_ab", [128, 512], ctx=c4)
        bstage = p.sb("l0a_bst2", [128, 512], ctx=c4)
        psk = p.ps_("l0a_psk", [128, 8 * 512], F32, ctx=c4)
        for tq in range(4):
            p.barrier()
            sin_tok = None
            mm_done = []
            kld = [None, None]
            for trig in range(2):
                p.wait("s", mm_done)
                mm_done = []
                for kk in range(8):
                    for g in range(GRID):
                        gv = float(g + 1)
                        bias = float(np.pi / 2) if trig == 0 else 0.0
                        if trig == 0:
                            emit_ew(p, "v", lambda E, kk=kk, gv=gv, tq=tq:
                                    E.tensor_scalar(
                                        tbuf[:, :],
                                        vT[kk][:, tq * 512:(tq + 1) * 512],
                                        gv * inv2pi, 0.25, ALU.mult, ALU.add),
                                    dep=sin_tok)
                            t1_ = emit_ew(p, "v", lambda E: E.tensor_scalar(
                                tbuf[:, :], tbuf[:, :], MAG, None, ALU.add),
                                sedge=True)
                        else:
                            t1_ = emit_ew(p, "v", lambda E, kk=kk, gv=gv, tq=tq:
                                          E.tensor_scalar(
                                              tbuf[:, :],
                                              vT[kk][:, tq * 512:(tq + 1) * 512],
                                              gv * inv2pi, MAG, ALU.mult,
                                              ALU.add), dep=sin_tok)
                        emit_ew(p, "v", lambda E: E.tensor_scalar(
                            kbuf[:, :], tbuf[:, :], MAG, None, ALU.subtract),
                            sedge=True)
                        emit_ew(p, "v", lambda E, kk=kk, gv=gv, tq=tq: E.tensor_scalar(
                            abuf[:, :], vT[kk][:, tq * 512:(tq + 1) * 512], gv,
                            None, ALU.mult), sedge=True)
                        emit_ew(p, "v", lambda E: E.scalar_tensor_tensor(
                            abuf[:, :], kbuf[:, :], -c1w, abuf[:, :], ALU.mult,
                            ALU.add), sedge=True)
                        emit_ew(p, "v", lambda E: E.scalar_tensor_tensor(
                            abuf[:, :], kbuf[:, :], -c2w, abuf[:, :], ALU.mult,
                            ALU.add), sedge=True)
                        lo, hi = ((-4.7123880, 1.5707957) if trig == 0
                                  else (-3.1415925, 3.1415925))
                        rr = emit_ew(p, "v", lambda E, lo=lo, hi=hi:
                                     E.tensor_scalar(
                                         abuf[:, :], abuf[:, :], lo, hi,
                                         ALU.max, ALU.min), sedge=True)
                        sin_tok = emit_ew(p, "s", lambda E, kk=kk, g=g, bias=bias:
                                          E.activation(
                                              cs[g * 8 + kk][:, :], abuf[:, :],
                                              AF.Sin, bias=bias), dep=rr)
                trig_tok = [("ps", p.cnt["ps"]), ("pv", p.cnt["pv"])]
                kio = io["l0a_kanc"] if trig == 0 else io["l0a_kans"]
                for mo in range(8):
                    pp = mo % 2
                    p.wait("g", kld[pp])
                    kld[pp] = p.dma("g", kslc[pp][:, :],
                                    kio[:, mo * 8192:(mo + 1) * 8192])
                    p.wait("t", [kld[pp]] + trig_tok)
                    for kc in range(64):
                        def thunk(E, tq=tq, trig=trig, mo=mo, pp=pp, kc=kc):
                            return E.matmul(
                                psk[:, mo * 512:(mo + 1) * 512],
                                kslc[pp][:, kc * 128:(kc + 1) * 128],
                                cs[kc][:, :],
                                start=(trig == 0 and kc == 0),
                                stop=(trig == 1 and kc == 63))
                        tok = p.do("t", thunk, tick=(kc == 63))
                    kld[pp] = tok
                    mm_done.append(tok)
            bld = None
            sv = None
            stt = None
            for mo in range(8):
                p.wait("y", sv)
                bld = p.dma("y", bstage[:, :],
                            base_d[mo * 128:(mo + 1) * 128, tq * 512:(tq + 1) * 512])
                sv = emit_ew(p, "v", lambda E, mo=mo: E.tensor_tensor(
                    tbuf[:, :], psk[:, mo * 512:(mo + 1) * 512], bstage[:, :],
                    ALU.add), dep=[bld, ("pt", p.cnt["pt"])])
                p.wait("y", sv)
                stt = p.dma("y", io["res_out"][mo * 128:(mo + 1) * 128,
                                               tq * 512:(tq + 1) * 512], tbuf[:, :])
                p.wait("v", stt)
    return None


def block_l0attn_full(p, io, consts, ncores, core_id=0):
    block_l0attn(p, io, consts, ncores, core_id)


def build_l0attn_standalone(ncores=1, core_id=0):
    nc = bass.Bass("TRN2", target_bir_lowering=False, debug=False)
    io = declare_io(nc, L0ATTN_IO)
    io["res_out"] = nc.dram_tensor("res_out", [D, TC], F32, kind="ExternalOutput")
    p = Prog(nc)
    consts = setup_consts(p)
    block_l0attn(p, io, consts, ncores, core_id)
    p.build()
    return nc


def build_fused(ncores=8, core_id=0):
    nc = bass.Bass("TRN2", target_bir_lowering=False, debug=False)
    io = {}
    for iod in (L0ATTN_IO, L0FFN_IO, L1ATTN_IO, L1FFN_IO):
        io.update(declare_io(nc, iod))
    r1 = nc.dram_tensor("r1_d", [D, TC], F32)
    r2 = nc.dram_tensor("r2_d", [D, TC], F32)
    r3 = nc.dram_tensor("r3_d", [D, TC], F32)
    out = nc.dram_tensor("res_out", [D, TC], F32, kind="ExternalOutput")
    p = Prog(nc)
    consts = setup_consts(p)
    block_l0attn(p, dict(io, res_out=r1), consts, ncores, core_id)
    block_l0ffn(p, dict(io, res_in=r1, res_out=r2), consts)
    block_l1attn(p, dict(io, res_in=r2, res_out=r3), consts, ncores)
    block_l1ffn(p, dict(io, res_in=r3, res_out=out), consts, ncores, core_id, 6)
    p.build()
    return nc


def host_prep_fused(inputs, c, x, ncores=8):
    h = {}
    h.update(host_prep_l0attn(inputs, c, x, ncores=ncores))
    h.update(host_prep_l0ffn(inputs))
    h.update(host_prep_l1attn(inputs))
    h.update(host_prep_l1ffn(inputs, c))
    return h


def kernel(**inputs):
    from concourse.bass_utils import run_bass_kernel_spmd
    NCOR = 8
    cores = list(range(NCOR))
    x = np.ascontiguousarray(
        np.asarray(inputs["L0_attn_x"], np.float32).reshape(L, D))
    nc = build_fused(NCOR)
    hosts = [host_prep_fused(inputs, c, x, ncores=NCOR) for c in cores]
    res = run_bass_kernel_spmd(nc, hosts, core_ids=cores)
    out = np.concatenate(
        [np.asarray(res.results[c]["res_out"], np.float32).T for c in cores], 0)
    return np.ascontiguousarray(out.reshape(1, L, D))


def kernel_unfused(**inputs):
    from concourse.bass_utils import run_bass_kernel_spmd
    NCOR = 8
    cores = list(range(NCOR))
    x = np.ascontiguousarray(
        np.asarray(inputs["L0_attn_x"], np.float32).reshape(L, D))

    nc = build_l0attn_standalone(NCOR)
    hosts = [host_prep_l0attn(inputs, c, x, ncores=NCOR) for c in cores]
    res = run_bass_kernel_spmd(nc, hosts, core_ids=cores)
    rr = [np.ascontiguousarray(np.asarray(res.results[c]["res_out"], np.float32))
          for c in cores]

    nc = build_l0ffn_standalone()
    w = host_prep_l0ffn(inputs)
    hosts = [dict(w, res_in=rr[c]) for c in cores]
    res = run_bass_kernel_spmd(nc, hosts, core_ids=cores)
    rr = [np.ascontiguousarray(np.asarray(res.results[c]["res_out"], np.float32))
          for c in cores]

    nc = build_l1attn_standalone(NCOR)
    w = host_prep_l1attn(inputs)
    hosts = [dict(w, res_in=rr[c]) for c in cores]
    res = run_bass_kernel_spmd(nc, hosts, core_ids=cores)
    rr = [np.ascontiguousarray(np.asarray(res.results[c]["res_out"], np.float32))
          for c in cores]

    nc = build_l1ffn_standalone(NCOR, 0, 6)
    hosts = [dict(host_prep_l1ffn(inputs, c), res_in=rr[c]) for c in cores]
    res = run_bass_kernel_spmd(nc, hosts, core_ids=cores)
    out = np.concatenate(
        [np.asarray(res.results[c]["res_out"], np.float32).T for c in cores], 0)
    return np.ascontiguousarray(out.reshape(1, L, D))
